# version 05
# baseline (speedup 1.0000x reference)
import os
import numpy as np
from contextlib import ExitStack

import concourse.bass as bass
import concourse.mybir as mybir
from concourse.bass_utils import run_bass_kernel_spmd

F32 = mybir.dt.float32
BF16 = mybir.dt.bfloat16
I32 = mybir.dt.int32
AF = mybir.ActivationFunctionType
ALU = mybir.AluOpType
AXX = mybir.AxisListType.X

T, D = 4096, 1024
CH = 256
NCH = T // CH
NTT = CH // 128
S = 16
EO = 1024
HP = 2816
HM = HP // 128
NE = 8
FTOK = 1024
FNT = FTOK // 512
OWN_TOK = 2048
PAIRS = [[0, 1], [2, 3], [4, 5], [6, 7]]
RMS_EPS = float(np.finfo(np.float32).eps)
LN_EPS = 1e-5
PI = float(np.pi)
TWO_PI = float(2 * np.pi)
INV_2PI = float(1.0 / (2 * np.pi))

ENGS = ("sp", "pe", "act", "dve", "po")


class B:
    def __init__(self):
        self.nc = bass.Bass("TRN2", target_bir_lowering=False)
        self.es = ExitStack()
        self.q = {e: [] for e in ENGS}
        self.semh = {}
        self.cnt = {}
        self.lastw = {e: {} for e in ENGS}
        self.bank = {}

    def sem(self, name):
        self.semh[name] = self.es.enter_context(self.nc.semaphore(name))
        self.cnt[name] = 0

    def sb(self, name, shape, dt, stack=None):
        return (stack or self.es).enter_context(self.nc.sbuf_tensor(name, shape, dt))

    def ps(self, name, shape, dt):
        return self.es.enter_context(self.nc.psum_tensor(name, shape, dt))

    def op(self, eng, fn, waits=(), inc=None, ss=False):
        if ss:
            waits = list(waits) + [(eng, self.cnt[eng])]
        ws = []
        lw = self.lastw[eng]
        for w in waits:
            if w is None:
                continue
            s, v = w
            if lw.get(s, 0) >= v:
                continue
            ws.append((s, v))
            lw[s] = v
        if inc is None:
            inc = (eng, 1)
        tok = None
        if inc:
            sname, n = inc
            self.cnt[sname] += n
            tok = (sname, self.cnt[sname])
        semh = self.semh

        def closure(e, ws=tuple(ws), fn=fn, inc=inc):
            for s, v in ws:
                e.wait_ge(semh[s], v)
            if fn is None:
                return
            r = fn(e)
            if inc:
                r.then_inc(semh[inc[0]], inc[1])
        self.q[eng].append(closure)
        return tok

    def barrier(self):
        toks = [(e, self.cnt[e]) for e in ("pe", "act", "dve", "po")]
        for eng in ENGS:
            self.op(eng, None, waits=toks, inc=False)

    def dma(self, eng, out, in_, waits=(), sem="ld", **kw):
        return self.op(eng, lambda e: e.dma_start(out=out, in_=in_, **kw),
                       waits, inc=(sem, 16))

    def pe_mm(self, bankname, waits, fn):
        g = self.bank.get(bankname)
        return self.op("pe", fn, list(waits) + [g])

    def evac(self, bankname, eng, fn, waits=()):
        tok = self.op(eng, fn, waits)
        self.bank[bankname] = tok
        return tok


def build_program(n_chunks=None, do_ffn=None, do_scan=True, ffn_lvl=5):
    if n_chunks is None:
        n_chunks = int(os.environ.get("MK_NCHUNKS", NCH))
    if do_ffn is None:
        do_ffn = bool(int(os.environ.get("MK_FFN", 1)))
    b = B()
    nc = b.nc

    def din(name, shape, dt=F32):
        return nc.dram_tensor(name, shape, dt, kind="ExternalInput").ap()

    xb = din("xb", [T, D])
    x_own = din("x_own", [OWN_TOK, D])
    w_in_pm = din("w_in_pm", [128, 8, 2048])
    w_out_pm = din("w_out_pm", [128, 8, 1024])
    w_x_pm = din("w_x_pm", [128, 8, 34])
    wdt_pm = din("wdt_pm", [128, 16])
    bdt_pm = din("bdt_pm", [128, 16])
    a_pm = din("a_pm", [128, 128])
    d_pm = din("d_pm", [128, 8])
    smalls = din("smalls", [1, 64])
    rfreq = din("rfreq", [1, 8])
    c_pm = din("c_pm", [128, 8])
    wmod_pm = din("wmod_pm", [128, 8, 6144])
    router_pm = din("router_pm", [128, 8, 8])
    wg_pm = din("wg_pm", [NE, HM, 128, 8, 128])
    wu_pm = din("wu_pm", [NE, HM, 128, 8, 128])
    wd_pm = din("wd_pm", [NE, 8, 128, HM, 128])

    out_t = nc.dram_tensor("out", [OWN_TOK, D], F32, kind="ExternalOutput").ap()

    ag_in = nc.dram_tensor("ag_in", [34, CH], F32).ap()
    ag_out = nc.dram_tensor("ag_out", [68, CH], F32).ap()
    rs_in = nc.dram_tensor("rs_in", [256, 8 * CH], BF16).ap()
    rs_out = nc.dram_tensor("rs_out", [128, 8 * CH], BF16).ap()
    xnew_d = nc.dram_tensor("xnew_d", [OWN_TOK, D], F32, kind="ExternalOutput" if os.environ.get("MK_DEBUG") else "Internal").ap()

    dbg_on = bool(os.environ.get("MK_DEBUG"))
    dbg_chunk = int(os.environ.get("MK_DEBUG_CHUNK", 0))

    def dbg(name, ap_, wait):
        if not dbg_on:
            return
        t = nc.dram_tensor("dbg_" + name, list(ap_.shape), ap_.dtype,
                           kind="ExternalOutput").ap()
        b.dma("po", t, ap_, waits=[wait], sem="st")

    for s in ("ld", "st", "cc", "pe", "act", "dve", "po",
              "xA", "xB", "wiS", "agl", "rsl", "lrw",
              "fgA", "fgB", "fuA", "fuB", "fdA", "fdB", "xnA", "xnB"):
        b.sem(s)

    pb = [b.ps(f"pb{i}", [128, 512], F32) for i in range(6)]
    pT1 = b.ps("pT1", [128, 1024], BF16)
    pT2 = b.ps("pT2", [128, 1024], BF16)

    identF = b.sb("identF", [128, 128], F32)
    identB = b.sb("identB", [128, 128], BF16)
    onesCol = b.sb("onesCol", [128, 1], F32)
    ones1 = b.sb("ones1", [1, 128], F32)
    selAll = b.sb("selAll", [16, 16, 128], BF16)
    sfreqRep = b.sb("sfreqRep", [128, 8], F32)
    bRepB = b.sb("bRepB", [128, 16], F32)
    bRepC = b.sb("bRepC", [128, 16], F32)
    wRepB = b.sb("wRepB", [128, 16], F32)
    wRepC = b.sb("wRepC", [128, 16], F32)
    wdtCol = b.sb("wdtCol", [128, 16], F32)
    bdtCol = b.sb("bdtCol", [128, 16], F32)
    A_sb = b.sb("A_sb", [128, 128], F32)
    dColS = b.sb("dColS", [128, 8], F32)
    wx_bf = b.sb("wx_bf", [128, 8, 34], BF16)
    router_bf = b.sb("router_bf", [128, 8, 8], BF16)
    caRep = b.sb("caRep", [128, D], BF16)
    saRep = b.sb("saRep", [128, D], BF16)
    gaRep = b.sb("gaRep", [128, D], BF16)
    cbRep = b.sb("cbRep", [128, D], BF16)
    sbRep = b.sb("sbRep", [128, D], BF16)
    gbRep = b.sb("gbRep", [128, D], BF16)
    smallRow = b.sb("smallRow", [1, 64], F32)
    freqRow = b.sb("freqRow", [1, 8], F32)
    sfRow = b.sb("sfRow", [1, 8], F32)

    CONSTS = [LN_EPS, RMS_EPS, 1.0, -1.0, -0.5, 1.0 / 16, 1.0 / 1024, 1.0 / 2048,
              PI, PI / 2, TWO_PI, -TWO_PI, INV_2PI, 0.25, 2.0, 1e-4, -1e30, 0.0,
              0.5]
    constT = b.sb("constT", [128, 32], F32)
    for i, v in enumerate(CONSTS):
        b.op("po", lambda e, i=i, v=v: e.memset(constT[:, i:i + 1], float(v)),
             inc=False)
        nc.const_aps.register(F32, float(v), constT[:, i:i + 1])

    tb_state = {"pend": [], "bank": 0, "guard": [None, None], "setup": None}

    def flush_tbf():
        pend = tb_state["pend"]
        if not pend:
            return None
        bk = tb_state["bank"]
        bank = pT1 if bk == 0 else pT2
        bankname = "pT1" if bk == 0 else "pT2"
        tb_state["bank"] = 1 - bk
        tb_state["pend"] = []
        waits = [tb_state["setup"]]
        for (_, _, w) in pend:
            waits += w
        waits.append(tb_state["guard"][bk])

        def body(e, pend=tuple(pend), bank=bank):
            for i, (src, _, _) in enumerate(pend):
                r = e.transpose(bank[:, i * 128:(i + 1) * 128], src, identB[:, :])
            return r
        mt = b.pe_mm(bankname, waits, body)
        ev = None
        for i, (_, dst, _) in enumerate(pend):
            ev = b.op("dve", lambda e, i=i, dst=dst, bank=bank: e.tensor_copy(
                dst, bank[:, i * 128:(i + 1) * 128]), waits=[mt])
        tb_state["guard"][bk] = ev
        b.bank[bankname] = ev
        return ev

    def tbf(src_ap, dst_ap, waits):
        tb_state["pend"].append((src_ap, dst_ap, list(waits)))
        if len(tb_state["pend"]) == 4:
            return flush_tbf()
        return None

    def tf32(src_ap, dst_ap, waits, p_in, f_in, eng="act"):
        mt = b.pe_mm("pb5", list(waits) + [tb_state["setup"]], lambda e: e.transpose(
            pb[5][0:f_in, 0:p_in], src_ap, identF[0:p_in, 0:p_in]))
        if eng == "act":
            fn = lambda e: e.copy(dst_ap, pb[5][0:f_in, 0:p_in])
        else:
            fn = lambda e: e.tensor_copy(dst_ap, pb[5][0:f_in, 0:p_in])
        return b.evac("pb5", eng, fn, waits=[mt])

    es_setup = ExitStack()
    selF = b.sb("selF", [16, 16, 128], F32, es_setup)
    modRow = b.sb("modRow", [1, 6144], F32, es_setup)
    wmod_sb = [b.sb(f"wmod_sb{i}", [128, 8, 512], F32, es_setup) for i in range(2)]
    cColT = b.sb("cColT", [128, 8], F32, es_setup)
    scCol = b.sb("scCol", [128, 8], F32, es_setup)
    wxStg = b.sb("wxStg", [128, 8, 34], F32, es_setup)
    rtStg = b.sb("rtStg", [128, 8, 8], F32, es_setup)
    gtmpA = b.sb("gtmpA", [1, 1024], F32, es_setup)
    gtmpS = b.sb("gtmpS", [1, 1024], F32, es_setup)

    b.op("po", lambda e: e.iota(identF[:, :], pattern=[[1, 128]], base=0,
                                channel_multiplier=-1,
                                allow_small_or_imprecise_dtypes=True))
    b.op("po", lambda e: e.tensor_single_scalar(identF[:, :], identF[:, :], 0.0,
                                                ALU.is_equal))
    b.op("po", lambda e: e.tensor_copy(identB[:, :], identF[:, :]))
    b.op("po", lambda e: e.memset(onesCol[:, :], 1.0))
    b.op("po", lambda e: e.memset(ones1[:, :], 1.0))
    b.op("po", lambda e: e.iota(selF[:, :, :], pattern=[[1, 16], [0, 128]], base=0,
                                channel_multiplier=-1,
                                allow_small_or_imprecise_dtypes=True))
    b.op("po", lambda e: e.tensor_single_scalar(selF[:, :, :], selF[:, :, :], 0.0,
                                                ALU.is_equal))
    tok_sel = b.op("po", lambda e: e.tensor_copy(selAll[:, :, :], selF[:, :, :]))
    tb_state["setup"] = tok_sel

    b.dma("sp", smallRow[:, :], smalls)
    b.dma("sp", freqRow[:, :], rfreq)
    b.dma("sp", wdtCol[:, :], wdt_pm)
    b.dma("sp", bdtCol[:, :], bdt_pm)
    b.dma("sp", A_sb[:, :], a_pm)
    b.dma("sp", dColS[:, :], d_pm)
    b.dma("sp", cColT[:, :], c_pm)
    b.dma("sp", wxStg[:, :, :], w_x_pm)
    b.dma("sp", rtStg[:, :, :], router_pm)
    ld0 = ("ld", b.cnt["ld"])

    ta = b.op("act", lambda e: e.activation(A_sb[:, :], A_sb[:, :], AF.Exp),
              waits=[ld0])
    tok_A = b.op("dve", lambda e: e.tensor_scalar(A_sb[:, :], A_sb[:, :], 1e-4, -1.0,
                                                  ALU.max, ALU.mult), waits=[ta])
    b.op("act", lambda e: e.activation(sfRow[:, :], freqRow[:, :], AF.Abs),
         waits=[ld0])
    b.op("act", lambda e: e.activation(sfRow[:, :], sfRow[:, :], AF.Exp, scale=-1.0),
         ss=True)
    ta2 = b.op("act", lambda e: e.activation(sfRow[:, :], sfRow[:, :], AF.Ln, bias=1.0),
               ss=True)
    td2 = b.op("dve", lambda e: e.scalar_tensor_tensor(
        sfRow[:, :], freqRow[:, :], 0.0, sfRow[:, :], ALU.max, ALU.add), waits=[ta2])
    b.op("act", lambda e: e.copy(wx_bf[:, :, :], wxStg[:, :, :]), waits=[ld0],
         inc=False)
    b.op("act", lambda e: e.copy(router_bf[:, :, :], rtStg[:, :, :]))
    tsc = b.op("act", lambda e: e.activation(scCol[:, :], cColT[:, :], AF.Silu))

    wm_guard = [None, None]
    for nchunk in range(12):
        sl = nchunk % 2
        tlw = b.dma("sp", wmod_sb[sl][:, :, :],
                    wmod_pm[:, :, nchunk * 512:(nchunk + 1) * 512],
                    waits=[wm_guard[sl]], sem="wiS")

        def mk(nchunk=nchunk, sl=sl):
            def f(e):
                for kc in range(8):
                    r = e.matmul(pb[5][0:1, 0:512], scCol[:, kc:kc + 1],
                                 wmod_sb[sl][:, kc, :],
                                 start=(kc == 0), stop=(kc == 7))
                return r
            return f
        mt = b.pe_mm("pb5", [tsc, ld0, tlw], mk())
        wm_guard[sl] = mt
        b.evac("pb5", "act", lambda e, nchunk=nchunk: e.copy(
            modRow[0:1, nchunk * 512:(nchunk + 1) * 512], pb[5][0:1, 0:512]),
            waits=[mt])
    tmod = ("act", b.cnt["act"])

    tmodd = tmod
    prev_clamp = None
    for base in (2048, 5120):
        b.op("dve", lambda e, base=base: e.tensor_scalar(
            gtmpS[:, :], modRow[0:1, base:base + 1024], 0.0, None, ALU.is_ge),
            waits=[tmod])
        b.op("dve", lambda e: e.tensor_scalar(gtmpS[:, :], gtmpS[:, :], 2.0, -1.0,
                                              ALU.mult, ALU.add))
        tabs = b.op("act", lambda e, base=base: e.activation(
            gtmpA[:, :], modRow[0:1, base:base + 1024], AF.Abs),
            waits=[tmod, prev_clamp])
        b.op("dve", lambda e: e.tensor_scalar_max(gtmpA[:, :], gtmpA[:, :], 0.25),
             waits=[tabs])
        tmodd = b.op("dve", lambda e, base=base: e.tensor_tensor(
            modRow[0:1, base:base + 1024], gtmpS[:, :], gtmpA[:, :], ALU.mult))
        prev_clamp = tmodd

    for (dst, base, bias) in ((caRep, 1024, 1.0), (saRep, 0, 0.0), (gaRep, 2048, 0.0),
                              (cbRep, 4096, 1.0), (sbRep, 3072, 0.0),
                              (gbRep, 5120, 0.0)):
        for nchunk in range(2):
            mt = b.pe_mm("pb4", [tmodd], lambda e, base=base, nchunk=nchunk: e.matmul(
                pb[4][:, :], ones1[0:1, :],
                modRow[0:1, base + nchunk * 512: base + (nchunk + 1) * 512],
                start=True, stop=True))
            b.evac("pb4", "act", lambda e, dst=dst, nchunk=nchunk, bias=bias:
                   e.activation(dst[:, nchunk * 512:(nchunk + 1) * 512],
                                pb[4][:, :], AF.Copy, bias=bias), waits=[mt])

    for (dst, off) in ((bRepB, 0), (bRepC, 16), (wRepB, 32), (wRepC, 48)):
        mt = b.pe_mm("pb4", [ld0], lambda e, off=off: e.matmul(
            pb[4][:, 0:16], ones1[0:1, :], smallRow[0:1, off:off + 16],
            start=True, stop=True))
        b.evac("pb4", "act", lambda e, dst=dst: e.copy(dst[:, :], pb[4][:, 0:16]),
               waits=[mt])
    mt = b.pe_mm("pb4", [td2], lambda e: e.matmul(
        pb[4][:, 0:8], ones1[0:1, :], sfRow[0:1, :], start=True, stop=True))
    b.evac("pb4", "act", lambda e: e.copy(sfreqRep[:, :], pb[4][:, 0:8]), waits=[mt])

    setup_act = ("act", b.cnt["act"])
    es_setup.close()
    b.barrier()

    es_m = ExitStack()
    w_in_bf = b.sb("w_in_bf", [128, 8, 2048], BF16, es_m)
    w_out_bf = b.sb("w_out_bf", [128, 8, 1024], BF16, es_m)
    wstgM = b.sb("wstgM", [128, 8, 128], F32, es_m)
    x_t = [b.sb("x_t0", [128, D], F32, es_m)] * 2
    h_tm = b.sb("h_tm", [128, D], BF16, es_m)
    h_fm = b.sb("h_fm", [128, 8, CH], BF16, es_m)
    xp_bf = [b.sb(f"xp{p}", [128, 8, CH], BF16, es_m) for p in range(2)]
    sz_bf = [b.sb(f"sz{p}", [128, 8, CH], BF16, es_m) for p in range(2)]
    wc_bf = [b.sb(f"wc{p}", [128, 8, CH], BF16, es_m) for p in range(2)]
    w0_bf = [b.sb(f"w0{p}", [128, 8, CH], BF16, es_m) for p in range(2)]
    dtO = [b.sb(f"dtO{p}", [128, 8, CH], F32, es_m) for p in range(2)]
    dtX = b.sb("dtX", [128, CH], F32, es_m)
    yacc = b.sb("yacc", [128, 8, CH], F32, es_m)
    carryA = b.sb("carryA", [128, 128], F32, es_m)
    ldtLast = [b.sb(f"ldtL{p}", [128, 8], F32, es_m) for p in range(2)]
    ssm_a = b.sb("ssm_a", [34, CH], F32, es_m)
    ssm_b = b.sb("ssm_b", [34, CH], F32, es_m)
    ssm_f = b.sb("ssm_f", [34, CH], F32, es_m)
    ssmT = b.sb("ssmT", [128, NTT, 34], F32, es_m)
    bnB = b.sb("bnB", [128, 16], F32, es_m)
    bnC = b.sb("bnC", [128, 16], F32, es_m)
    sqS = b.sb("sqS", [128, 16], F32, es_m)
    colS = b.sb("colS", [128, 4], F32, es_m)
    BrTm = b.sb("BrTm", [128, 16], F32, es_m)
    CrTm = b.sb("CrTm", [128, 16], F32, es_m)
    ropeT = [b.sb(f"ropeT{i}", [128, 8], F32, es_m) for i in range(6)]
    ropeI = b.sb("ropeI", [128, 8], I32, es_m)
    BrTf = [b.sb(f"BrTf{p}", [16, CH], BF16, es_m) for p in range(2)]
    CrTf = [b.sb(f"CrTf{p}", [16, CH], BF16, es_m) for p in range(2)]
    qcol = b.sb("qcol", [128, 1], F32, es_m)
    qrow = b.sb("qrow", [1, CH], F32, es_m)
    qRep = [b.sb(f"qRep{p}", [128, CH], F32, es_m) for p in range(2)]
    dt_inRep = b.sb("dt_inRep", [128, CH], F32, es_m)
    lamRep = b.sb("lamRep", [128, CH], F32, es_m)
    olamRep = b.sb("olamRep", [128, CH], F32, es_m)
    lamrow = b.sb("lamrow", [1, CH], F32, es_m)
    dtirow = b.sb("dtirow", [1, CH], F32, es_m)
    mdtRow = b.sb("mdtRow", [1, CH], F32, es_m)
    cumRow = [b.sb(f"cumRow{p}", [1, CH], F32, es_m) for p in range(2)]
    onesRow = b.sb("onesRow", [1, CH], F32, es_m)
    ldt_t = b.sb("ldt_t", [128, CH], F32, es_m)
    p2sh_t = b.sb("p2sh_t", [128, CH], F32, es_m)
    s1_t = b.sb("s1_t", [128, CH], F32, es_m)
    ppB = b.sb("ppB", [128, 8], F32, es_m)
    BrB = b.sb("BrB", [128, CH], BF16, es_m)
    CrB = b.sb("CrB", [128, CH], BF16, es_m)
    dA_t = [b.sb(f"dA{i}", [128, CH], F32, es_m) for i in range(2)]
    u_t = [b.sb(f"u{i}", [128, CH], BF16, es_m) for i in range(2)]
    h_t = [b.sb(f"h{i}", [128, CH], BF16, es_m) for i in range(2)]
    tmp_t = [b.sb(f"tmp{i}", [128, CH], BF16, es_m) for i in range(2)]
    yp1 = b.sb("yp1", [128, CH], F32, es_m)
    yp2 = b.sb("yp2", [128, CH], F32, es_m)
    Y3 = b.sb("Y3", [128, 8, CH], BF16, es_m)
    attnP = b.sb("attnP", [128, 8, CH], BF16, es_m)
    attnS = b.sb("attnS", [128, 8, CH], BF16, es_m)
    attn_tm = b.sb("attn_tm", [128, D], BF16, es_m)
    xo_t = [b.sb("xo0", [128, D], F32, es_m)] * 2
    xn_t = b.sb("xn_t", [128, D], F32, es_m)
    snap = b.sb("snap", [128, 64], F32, es_m)
    snapT = b.sb("snapT", [128, D], BF16, es_m)
    snapX = b.sb("snapX", [128, D], F32, es_m)
    lnc = b.sb("lnc", [128, D], F32, es_m)
    lncol = b.sb("lncol", [128, 4], F32, es_m)
    lnsq = b.sb("lnsq", [128, D], F32, es_m)

    tcast = ("po", b.cnt["po"])
    for mo in range(16):
        tl = b.dma("sp", wstgM[:, :, :], w_in_pm[:, :, mo * 128:(mo + 1) * 128],
                   waits=[tcast], sem="wiS")
        tcast = b.op("po", lambda e, mo=mo: e.tensor_copy(
            w_in_bf[:, :, mo * 128:(mo + 1) * 128], wstgM[:, :, :]), waits=[tl])
    for mo in range(8):
        tl = b.dma("sp", wstgM[:, :, :], w_out_pm[:, :, mo * 128:(mo + 1) * 128],
                   waits=[tcast], sem="wiS")
        tcast = b.op("po", lambda e, mo=mo: e.tensor_copy(
            w_out_bf[:, :, mo * 128:(mo + 1) * 128], wstgM[:, :, :]), waits=[tl])
    w_cast_done = tcast

    b.op("dve", lambda e: e.memset(carryA[:, :], 0.0))
    b.op("po", lambda e: e.memset(onesRow[:, :], 1.0))

    gd = {
        "xp_free": [None, None], "sz_free": [None, None],
        "wc_free": [None, None], "w0_free": [None, None],
        "dtO_free": [None, None], "BrTf_free": [None, None],
        "qRep_free": [None, None], "hfm_pe": None, "Y3_free": None,
        "attnP_free": None, "attnS_free": None,
        "x_slot": [None, None], "xo_slot": [None, None],
        "scan_dve": [None, None], "ypost_dve": [None, None],
        "wc_patch": [None, None], "w0_patch": [None, None],
        "dt_tok": [None, None], "rope_tok": [None, None],
        "dA_guard": [None, None], "h_guard": [None, None],
        "tmp_guard": [None, None], "prev_u": None, "prev_y": None,
        "prev_pm": None, "tmm_prev": None,
        "ag_next": None, "rs_next": None, "lamrow_free": None,
        "ropePE": None, "rope_dve": None, "tsum_prev": None,
        "cum_pe": [None, None],
        "ln_sq_prev": None, "bn_prev": [None, None],
        "sz_tok": {}, "dA_read": [None, None],
    }

    def ln_adaln(xtile, crep, srep, out_bf, waits_x, extra=()):
        b.op("dve", lambda e: e.reduce_sum(lncol[:, 0:1], xtile[:, :], axis=AXX),
             waits=list(waits_x) + list(extra) + [gd["ln_sq_prev"]])
        b.op("dve", lambda e: e.tensor_scalar_mul(lncol[:, 1:2], lncol[:, 0:1],
                                                  1.0 / D), ss=True)
        tc = b.op("dve", lambda e: e.tensor_scalar(lnc[:, :], xtile[:, :],
                                                   lncol[:, 1:2], None, ALU.subtract),
                  ss=True)
        tq = b.op("act", lambda e: e.activation(lnsq[:, :], lnc[:, :], AF.Square,
                                                accum_out=lncol[:, 2:3]), waits=[tc])
        gd["ln_sq_prev"] = tq
        b.op("act", lambda e: e.activation(lncol[:, 3:4], lncol[:, 2:3], AF.Ln,
                                           scale=1.0 / D, bias=LN_EPS), ss=True)
        tr = b.op("act", lambda e: e.activation(lncol[:, 3:4], lncol[:, 3:4], AF.Exp,
                                                scale=-0.5), ss=True)
        b.op("dve", lambda e: e.tensor_scalar(lnc[:, :], lnc[:, :], lncol[:, 3:4],
                                              None, ALU.mult), waits=[tr])
        b.op("dve", lambda e: e.tensor_tensor(lnsq[:, :], lnc[:, :], crep[:, :],
                                              ALU.mult))
        return b.op("dve", lambda e: e.tensor_tensor(out_bf[:, :], lnsq[:, :],
                                                     srep[:, :], ALU.add))

    def front(c):
        par = c % 2
        for tt in range(NTT):
            sl = 0
            tl = b.dma("sp", x_t[sl][:, :],
                       xb[c * CH + tt * 128: c * CH + (tt + 1) * 128, :],
                       waits=[gd["x_slot"][sl], ld0], sem="xA")
            tln = ln_adaln(x_t[sl], caRep, saRep, h_tm, [tl],
                           extra=[("pe", b.cnt["pe"]), setup_act])
            gd["x_slot"][sl] = tln
            for dc in range(8):
                tbf(h_tm[:, dc * 128:(dc + 1) * 128],
                    h_fm[:, dc, tt * 128:(tt + 1) * 128], [tln])
            flush_tbf()
        th = None

        ev_xp = None
        ev = None
        for mo in range(16):
            bankname = "pb0" if mo % 2 == 0 else "pb1"
            bank = pb[0] if mo % 2 == 0 else pb[1]

            def mmf(e, mo=mo, bank=bank):
                for kc in range(8):
                    r = e.matmul(bank[:, 0:CH], w_in_bf[:, kc, mo * 128:(mo + 1) * 128],
                                 h_fm[:, kc, :], start=(kc == 0), stop=(kc == 7))
                return r
            mt = b.pe_mm(bankname, [tb_state["guard"][0], tb_state["guard"][1],
                                    w_cast_done], mmf)
            if mo < 8:
                dst, w8 = xp_bf[par], gd["xp_free"][par]
            else:
                dst, w8 = sz_bf[par], gd["sz_free"][par]
            ev = b.evac(bankname, "act", lambda e, dst=dst, mo=mo, bank=bank:
                        e.activation(dst[:, mo % 8, :], bank[:, 0:CH], AF.Silu),
                        waits=[mt, w8])
            if mo == 7:
                ev_xp = ev
        gd["sz_tok"][c] = ev
        if c == dbg_chunk:
            dbg("hfm", h_fm[:, :, :], ev)
        gd["hfm_pe"] = ("pe", b.cnt["pe"])

        def ssm_mm(e):
            for kc in range(8):
                r = e.matmul(pb[5][0:34, 0:CH], wx_bf[:, kc, :], xp_bf[par][:, kc, :],
                             start=(kc == 0), stop=(kc == 7))
            return r
        mt = b.pe_mm("pb5", [ev_xp], ssm_mm)
        evs = b.evac("pb5", "act", lambda e: e.copy(ssm_a[:, :], pb[5][0:34, 0:CH]),
                     waits=[mt, gd["tsum_prev"]])
        tst = b.dma("po", ag_in, ssm_a[:, :], waits=[evs, gd["ag_next"]], sem="st")
        tcc = b.op("po", lambda e: e.collective_compute(
            "AllGather", ALU.bypass, replica_groups=PAIRS, ins=[ag_in], outs=[ag_out]),
            waits=[("st", b.cnt["st"]), ("agl", b.cnt["agl"])], inc=("cc", 1))
        gd["ag_next"] = tcc
        tla = b.dma("sp", ssm_a[:, :], ag_out[0:34, :], waits=[tcc], sem="agl")
        tlb = b.dma("sp", ssm_b[:, :], ag_out[34:68, :], sem="agl")
        tsum = b.op("dve", lambda e: e.tensor_tensor(ssm_f[:, :], ssm_a[:, :],
                                                     ssm_b[:, :], ALU.add),
                    waits=[tla, tlb])
        gd["tsum_prev"] = tsum

        if c == dbg_chunk:
            dbg("ssm_f", ssm_f[:, :], tsum)
        evT = []
        for tt in range(NTT):
            evT.append(tf32(ssm_f[0:34, tt * 128:(tt + 1) * 128],
                            ssmT[:, tt, 0:34], [tsum, gd["rope_dve"]], 34, 128))

        tdr = b.dma("sp", dtirow[:, :], ssm_f[32:33, :],
                    waits=[tsum, gd["lamrow_free"]], sem="lrw")
        mt = b.pe_mm("pb4", [tdr], lambda e: e.matmul(
            pb[4][:, 0:CH], ones1[0:1, :], dtirow[0:1, :], start=True, stop=True))
        tdir = b.evac("pb4", "act", lambda e: e.copy(dt_inRep[:, :], pb[4][:, 0:CH]),
                      waits=[mt, gd["scan_dve"][par]])
        trow = b.dma("sp", lamrow[:, :], ssm_f[33:34, :],
                     waits=[tsum, gd["lamrow_free"]], sem="lrw")
        tsg = b.op("act", lambda e: e.activation(lamrow[:, :], lamrow[:, :],
                                                 AF.Sigmoid), waits=[trow])
        mt = b.pe_mm("pb4", [tsg], lambda e: e.matmul(
            pb[4][:, 0:CH], ones1[0:1, :], lamrow[0:1, :], start=True, stop=True))
        gd["lamrow_free"] = ("pe", b.cnt["pe"])
        tlr = b.evac("pb4", "act", lambda e: e.copy(lamRep[:, :], pb[4][:, 0:CH]),
                     waits=[mt, gd["scan_dve"][par]])
        tol = b.op("dve", lambda e: e.tensor_scalar(olamRep[:, :], lamRep[:, :],
                                                    -1.0, 1.0, ALU.mult, ALU.add),
                   waits=[tlr])

        last = None
        for etf in range(16):
            own = etf < 8
            ta1 = b.op("dve", lambda e, etf=etf: e.tensor_scalar(
                dtX[:, :], dt_inRep[:, :], wdtCol[:, etf:etf + 1],
                bdtCol[:, etf:etf + 1], ALU.mult, ALU.add),
                waits=[tdir, ld0, gd["tmm_prev"],
                       gd["dtO_free"][par] if own else None])
            tab = b.op("act", lambda e: e.activation(lnsq[:, 0:CH], dtX[:, :], AF.Abs),
                       waits=[ta1])
            b.op("act", lambda e: e.activation(lnsq[:, 0:CH], lnsq[:, 0:CH], AF.Exp,
                                               scale=-1.0))
            tal = b.op("act", lambda e: e.activation(lnsq[:, 0:CH], lnsq[:, 0:CH],
                                                     AF.Ln, bias=1.0))
            if own:
                tdt = b.op("dve", lambda e, etf=etf: e.scalar_tensor_tensor(
                    dtO[par][:, etf, :], dtX[:, :], 0.0, lnsq[:, 0:CH],
                    ALU.max, ALU.add), waits=[tal, gd["dA_read"][par]])
                src = dtO[par][:, etf, :]
            else:
                tdt = b.op("dve", lambda e: e.scalar_tensor_tensor(
                    dtX[:, :], dtX[:, :], 0.0, lnsq[:, 0:CH], ALU.max, ALU.add),
                    waits=[tal])
                src = dtX[:, :]
            gd["tmm_prev"] = b.pe_mm("pb3", [tdt], lambda e, src=src, etf=etf: e.matmul(
                pb[3][0:1, 0:CH], onesCol[:, 0:1], src,
                start=(etf == 0), stop=(etf == 15)))
            if own:
                tld = b.op("dve", lambda e, etf=etf: e.tensor_tensor(
                    ldt_t[:, :], dtO[par][:, etf, :], lamRep[:, :], ALU.mult),
                    waits=[tol])
                if c == 0:
                    b.op("dve", lambda e, etf=etf: e.tensor_copy(
                        ldt_t[:, 0:1], dtO[par][:, etf, 0:1]))
                b.op("dve", lambda e, etf=etf: e.tensor_tensor(
                    p2sh_t[:, 0:CH - 1], dtO[par][:, etf, 1:CH], olamRep[:, 1:CH],
                    ALU.mult))
                b.op("dve", lambda e: e.tensor_tensor(
                    s1_t[:, 0:CH - 1], ldt_t[:, 0:CH - 1], p2sh_t[:, 0:CH - 1],
                    ALU.add))
                b.op("dve", lambda e, etf=etf: e.tensor_tensor(
                    wc_bf[par][:, etf, 0:CH - 1], xp_bf[par][:, etf, 0:CH - 1],
                    s1_t[:, 0:CH - 1], ALU.mult),
                    waits=[ev_xp, gd["wc_free"][par]])
                b.op("dve", lambda e, etf=etf: e.tensor_tensor(
                    w0_bf[par][:, etf, 0:CH - 1], xp_bf[par][:, etf, 0:CH - 1],
                    p2sh_t[:, 0:CH - 1], ALU.mult),
                    waits=[gd["w0_free"][par]])
                last = b.op("dve", lambda e, etf=etf: e.tensor_copy(
                    ldtLast[par][:, etf:etf + 1], ldt_t[:, CH - 1:CH]), ss=True)
        gd["dt_tok"][par] = last
        if c == dbg_chunk:
            dbg("dt_inRep", dt_inRep[:, :], tdir)
            dbg("lamRep", lamRep[:, :], tlr)
            dbg("dtO", dtO[par][:, :, :], last)
            dbg("xp", xp_bf[par][:, :, :], ev_xp)
        tmean = ("pe", b.cnt["pe"])
        tmd = b.evac("pb3", "act", lambda e: e.activation(
            mdtRow[0:1, :], pb[3][0:1, 0:CH], AF.Copy, scale=1.0 / 2048.0),
            waits=[tmean, gd["scan_dve"][par]])

        init = 0.0 if c == 0 else cumRow[1 - par][0:1, CH - 1:CH]
        tcum = b.op("dve", lambda e, init=init: e.tensor_tensor_scan(
            cumRow[par][0:1, :], onesRow[0:1, :], mdtRow[0:1, :], init,
            ALU.mult, ALU.add), waits=[tmd, gd["cum_pe"][par]])

        rope_evs = []
        for tt in range(NTT):
            cix = 0 if tt == 0 else 3
            tcc2 = tf32(cumRow[par][0:1, tt * 128:(tt + 1) * 128],
                        colS[:, cix:cix + 1], [tcum], 1, 128)
            tb1 = b.op("dve", lambda e, tt=tt: e.tensor_tensor(
                bnB[:, :], ssmT[:, tt, 0:16], bRepB[:, :], ALU.add),
                waits=[evT[tt], gd["bn_prev"][0]], ss=True)
            tb2 = b.op("act", lambda e: e.activation(sqS[:, :], bnB[:, :], AF.Square,
                                                     accum_out=colS[:, 1:2]),
                       waits=[tb1])
            b.op("act", lambda e: e.activation(colS[:, 1:2], colS[:, 1:2], AF.Ln,
                                               scale=1.0 / 16, bias=RMS_EPS),
                 ss=True)
            tb3 = b.op("act", lambda e: e.activation(colS[:, 1:2], colS[:, 1:2],
                                                     AF.Exp, scale=-0.5), ss=True)
            tb4 = b.op("dve", lambda e: e.scalar_tensor_tensor(
                bnB[:, :], bnB[:, :], colS[:, 1:2], wRepB[:, :], ALU.mult, ALU.mult),
                waits=[tb3], ss=True)
            gd["bn_prev"][0] = tb2
            tc1 = b.op("dve", lambda e, tt=tt: e.tensor_tensor(
                bnC[:, :], ssmT[:, tt, 16:32], bRepC[:, :], ALU.add),
                waits=[gd["bn_prev"][1]], ss=True)
            tc2 = b.op("act", lambda e: e.activation(sqS[:, :], bnC[:, :], AF.Square,
                                                     accum_out=colS[:, 2:3]),
                       waits=[tc1])
            b.op("act", lambda e: e.activation(colS[:, 2:3], colS[:, 2:3], AF.Ln,
                                               scale=1.0 / 16, bias=RMS_EPS),
                 ss=True)
            tc3 = b.op("act", lambda e: e.activation(colS[:, 2:3], colS[:, 2:3],
                                                     AF.Exp, scale=-0.5), ss=True)
            tc4 = b.op("dve", lambda e: e.scalar_tensor_tensor(
                bnC[:, :], bnC[:, :], colS[:, 2:3], wRepC[:, :], ALU.mult, ALU.mult),
                waits=[tc3], ss=True)
            gd["bn_prev"][1] = tc2

            tan = b.op("dve", lambda e, cix=cix: e.tensor_scalar(
                ropeT[0][:, :], sfreqRep[:, :], colS[:, cix:cix + 1], None, ALU.mult),
                waits=[tcc2], ss=True)
            b.op("dve", lambda e: e.tensor_scalar_mul(ropeT[1][:, :], ropeT[0][:, :],
                                                      INV_2PI), ss=True)
            b.op("dve", lambda e: e.tensor_scalar_add(ropeT[1][:, :], ropeT[1][:, :],
                                                      8388608.0), ss=True)
            b.op("dve", lambda e: e.tensor_scalar_sub(ropeT[1][:, :], ropeT[1][:, :],
                                                      8388608.0), ss=True)
            ty = b.op("dve", lambda e: e.scalar_tensor_tensor(
                ropeT[2][:, :], ropeT[1][:, :], -TWO_PI, ropeT[0][:, :],
                ALU.mult, ALU.add), ss=True)
            if dbg_on and c == dbg_chunk and tt == 0:
                b.op("act", lambda e, cix=cix: e.copy(snap[:, 0:1], colS[:, cix:cix + 1]),
                     waits=[tan])
                b.op("act", lambda e: e.copy(snap[:, 8:16], ropeT[0][:, :]),
                     waits=[tan])
                b.op("act", lambda e: e.copy(snap[:, 16:24], ropeT[1][:, :]),
                     waits=[ty])
                b.op("act", lambda e: e.copy(snap[:, 24:32], ropeT[2][:, :]),
                     waits=[ty])
            tsin = b.op("act", lambda e: e.activation(ropeT[3][:, :], ropeT[2][:, :],
                                                      AF.Sin), waits=[ty])
            b.op("dve", lambda e: e.tensor_scalar_add(ropeT[4][:, :], ropeT[2][:, :],
                                                      PI / 2), ss=True)
            b.op("dve", lambda e: e.tensor_single_scalar(
                ropeT[5][:, :], ropeT[4][:, :], PI, ALU.is_gt), ss=True)
            tyc = b.op("dve", lambda e: e.scalar_tensor_tensor(
                ropeT[4][:, :], ropeT[5][:, :], -TWO_PI, ropeT[4][:, :],
                ALU.mult, ALU.add), ss=True)
            tcos = b.op("act", lambda e: e.activation(ropeT[5][:, :], ropeT[4][:, :],
                                                      AF.Sin), waits=[tyc])
            if dbg_on and c == dbg_chunk and tt == 0:
                tsnap = b.op("act", lambda e: e.copy(snap[:, 32:40], ropeT[3][:, :]))
                dbg("snap", snap[:, :], tsnap)

            for (srcv, dstv, tw) in ((bnB, BrTm, tb4), (bnC, CrTm, tc4)):
                b.op("dve", lambda e, srcv=srcv: e.tensor_tensor(
                    ropeT[0][:, :], srcv[:, 0:8], ropeT[5][:, :], ALU.mult),
                    waits=[tcos, tsin, tw, gd["ropePE"]], ss=True)
                b.op("dve", lambda e, srcv=srcv: e.tensor_tensor(
                    ropeT[1][:, :], srcv[:, 8:16], ropeT[3][:, :], ALU.mult),
                    ss=True)
                b.op("dve", lambda e, dstv=dstv: e.tensor_tensor(
                    dstv[:, 0:8], ropeT[0][:, :], ropeT[1][:, :], ALU.subtract),
                    ss=True)
                b.op("dve", lambda e, srcv=srcv: e.tensor_tensor(
                    ropeT[0][:, :], srcv[:, 0:8], ropeT[3][:, :], ALU.mult),
                    ss=True)
                b.op("dve", lambda e, srcv=srcv: e.tensor_tensor(
                    ropeT[1][:, :], srcv[:, 8:16], ropeT[5][:, :], ALU.mult),
                    ss=True)
                b.op("dve", lambda e, dstv=dstv: e.tensor_tensor(
                    dstv[:, 8:16], ropeT[0][:, :], ropeT[1][:, :], ALU.add),
                    ss=True)
            b.op("dve", lambda e: e.tensor_tensor(sqS[:, :], BrTm[:, :], CrTm[:, :],
                                                  ALU.mult), ss=True)
            tq2 = b.op("dve", lambda e: e.reduce_sum(qcol[:, 0:1], sqS[:, :],
                                                     axis=AXX), ss=True)
            if dbg_on and c == dbg_chunk and tt == 0:
                tsb = b.op("dve", lambda e: e.tensor_copy(snap[:, 40:56], BrTm[:, :]))
                b.op("act", lambda e: e.copy(snap[:, 56:64], ropeT[5][:, :]),
                     waits=[tsb])
                tsb2 = b.op("act", lambda e: e.copy(snap[:, 4:5], qcol[:, 0:1]), waits=[tsb])
                dbg("snapBr", snap[:, :], tsb2)
            if c == dbg_chunk and tt == 0:
                dbg("sfreqRep", sfreqRep[:, :], tq2)
                dbg("angT", ropeT[2][:, :], ty)
                dbg("BrTm", BrTm[:, :], tq2)
                dbg("CrTm", CrTm[:, :], tq2)
                dbg("bnB", bnB[:, :], tq2)
                dbg("sinT", ropeT[3][:, :], tq2)
                dbg("cosT", ropeT[5][:, :], tq2)
                dbg("qcolD", qcol[:, :], tq2)
            rope_evs.append(tf32(BrTm[:, :], BrTf[par][0:16, tt * 128:(tt + 1) * 128],
                                 [tq2, gd["BrTf_free"][par]], 128, 16))
            rope_evs.append(tf32(CrTm[:, :], CrTf[par][0:16, tt * 128:(tt + 1) * 128],
                                 [tq2], 128, 16))
            rope_evs.append(tf32(qcol[:, 0:1], qrow[0:1, tt * 128:(tt + 1) * 128],
                                 [tq2], 128, 1))
            gd["ropePE"] = ("pe", b.cnt["pe"])
        gd["rope_dve"] = ("dve", b.cnt["dve"])
        gd["cum_pe"][par] = ("pe", b.cnt["pe"])
        mt = b.pe_mm("pb4", rope_evs, lambda e: e.matmul(
            pb[4][:, 0:CH], ones1[0:1, :], qrow[0:1, :], start=True, stop=True))
        tqr = b.evac("pb4", "act", lambda e: e.copy(qRep[par][:, :], pb[4][:, 0:CH]),
                     waits=[mt, gd["qRep_free"][par]])
        gd["rope_tok"][par] = tqr
        if c == dbg_chunk:
            dbg("cumRow", cumRow[par][:, :], tcum)
            dbg("BrTf", BrTf[par][:, :], tqr)
            dbg("CrTf", CrTf[par][:, :], tqr)
            dbg("qRep", qRep[par][:, :], tqr)

    def patch(cprev, clast=False):
        par = cprev % 2
        npar = 1 - par
        for et in range(8):
            if not clast:
                b.op("dve", lambda e, et=et: e.tensor_tensor(
                    ppB[:, et:et + 1], dtO[npar][:, et, 0:1], olamRep[:, 0:1],
                    ALU.mult), waits=[gd["dt_tok"][npar]], ss=True)
                b.op("dve", lambda e, et=et: e.tensor_tensor(
                    w0_bf[par][:, et, CH - 1:CH], xp_bf[par][:, et, CH - 1:CH],
                    ppB[:, et:et + 1], ALU.mult), ss=True)
                b.op("dve", lambda e, et=et: e.tensor_tensor(
                    ppB[:, et:et + 1], ldtLast[par][:, et:et + 1], ppB[:, et:et + 1],
                    ALU.add), ss=True)
                b.op("dve", lambda e, et=et: e.tensor_tensor(
                    wc_bf[par][:, et, CH - 1:CH], xp_bf[par][:, et, CH - 1:CH],
                    ppB[:, et:et + 1], ALU.mult), ss=True)
            else:
                b.op("dve", lambda e, et=et: e.tensor_tensor(
                    wc_bf[par][:, et, CH - 1:CH], xp_bf[par][:, et, CH - 1:CH],
                    ldtLast[par][:, et:et + 1], ALU.mult), ss=True)
                b.op("dve", lambda e, et=et: e.memset(
                    w0_bf[par][:, et, CH - 1:CH], 0.0), ss=True)
        tok = b.op("dve", lambda e: e.engine_nop())
        gd["wc_patch"][par] = tok
        gd["w0_patch"][par] = tok

    def scan(c):
        par = c % 2
        dA_guard = gd["dA_guard"]
        h_guard = gd["h_guard"]
        tmp_guard = gd["tmp_guard"]
        prev_u = gd["prev_u"]
        prev_y = gd["prev_y"]
        prev_pm = gd["prev_pm"]
        for s in range(S):
            mtb = b.pe_mm("pb4", [gd["rope_tok"][par], prev_u], lambda e, s=s:
                          e.matmul(pb[4][:, 0:CH], selAll[:, s, :], BrTf[par][:, :],
                                   start=True, stop=True))
            tBrB = b.evac("pb4", "act", lambda e: e.copy(BrB[:, :], pb[4][:, 0:CH]),
                          waits=[mtb, prev_u])
            mtc = b.pe_mm("pb2", [gd["rope_tok"][par], prev_y, prev_pm],
                          lambda e, s=s: e.matmul(pb[2][:, 0:CH], selAll[:, s, :],
                                                  CrTf[par][:, :],
                                                  start=True, stop=True))
            tCrB = b.evac("pb2", "act", lambda e: e.copy(CrB[:, :], pb[2][:, 0:CH]),
                          waits=[mtc, prev_y, prev_pm])
            for et in range(8):
                it = s * 8 + et
                sl = it % 2
                tda = b.op("act", lambda e, s=s, et=et, sl=sl: e.activation(
                    dA_t[sl][:, :], dtO[par][:, et, :], AF.Exp,
                    scale=A_sb[:, et * 16 + s: et * 16 + s + 1]),
                    waits=[dA_guard[sl], gd["dt_tok"][par], tok_A])
                b.op("dve", lambda e, et=et, sl=sl: e.tensor_tensor(
                    u_t[sl][:, :], wc_bf[par][:, et, :], BrB[:, :], ALU.mult),
                    waits=[tBrB, gd["wc_patch"][par]])
                tsc2 = b.op("dve", lambda e, sl=sl, col=it: e.tensor_tensor_scan(
                    h_t[sl][:, :], dA_t[sl][:, :], u_t[sl][:, :],
                    carryA[:, col:col + 1], ALU.mult, ALU.add),
                    waits=[tda, h_guard[sl]])
                dA_guard[sl] = tsc2
                b.op("dve", lambda e, sl=sl, col=it: e.tensor_copy(
                    carryA[:, col:col + 1], h_t[sl][:, CH - 1:CH]), ss=True)
                if s == 0:
                    ty = b.op("dve", lambda e, et=et, sl=sl: e.tensor_tensor(
                        yacc[:, et, :], h_t[sl][:, :], CrB[:, :], ALU.mult),
                        waits=[tCrB, gd["ypost_dve"][par]])
                    h_guard[sl] = ty
                else:
                    tm = b.op("po", lambda e, sl=sl: e.tensor_tensor(
                        tmp_t[sl][:, :], h_t[sl][:, :], CrB[:, :], ALU.mult),
                        waits=[tsc2, tCrB, tmp_guard[sl]])
                    h_guard[sl] = tm
                    ty = b.op("dve", lambda e, et=et, sl=sl: e.tensor_tensor(
                        yacc[:, et, :], yacc[:, et, :], tmp_t[sl][:, :], ALU.add),
                        waits=[tm])
                    tmp_guard[sl] = ty
                if et == 7:
                    prev_u = ("dve", b.cnt["dve"])
                    prev_y = ty
                    if s > 0:
                        prev_pm = ("po", b.cnt["po"])
        gd["prev_u"] = prev_u
        gd["prev_y"] = prev_y
        gd["prev_pm"] = prev_pm
        gd["dA_read"][par] = ("act", b.cnt["act"])
        gd["scan_dve"][par] = ("dve", b.cnt["dve"])
        if c == dbg_chunk:
            dbg("yacc", yacc[:, :, :], ("dve", b.cnt["dve"]))
            dbg("wc", wc_bf[par][:, :, :], gd["wc_patch"][par])
            dbg("w0", w0_bf[par][:, :, :], gd["w0_patch"][par])
            dbg("carry", carryA[:, :], ("dve", b.cnt["dve"]))

    def ypost_wout(c):
        par = c % 2
        for et in range(8):
            b.op("dve", lambda e, et=et: e.tensor_tensor(
                yp1[:, :], w0_bf[par][:, et, :], qRep[par][:, :], ALU.mult),
                waits=[gd["w0_patch"][par], gd["rope_tok"][par], ld0])
            b.op("dve", lambda e, et=et: e.tensor_tensor(
                yp2[:, :], yacc[:, et, :], yp1[:, :], ALU.subtract))
            b.op("dve", lambda e, et=et: e.scalar_tensor_tensor(
                yp1[:, :], xp_bf[par][:, et, :], dColS[:, et:et + 1], yp2[:, :],
                ALU.mult, ALU.add))
            b.op("dve", lambda e, et=et: e.tensor_tensor(
                Y3[:, et, :], yp1[:, :], sz_bf[par][:, et, :], ALU.mult),
                waits=[gd["sz_tok"].get(c), gd["Y3_free"]])
        typ = b.op("dve", lambda e: e.engine_nop())
        gd["ypost_dve"][par] = typ
        gd["xp_free"][par] = typ
        gd["sz_free"][par] = typ
        gd["wc_free"][par] = typ
        gd["w0_free"][par] = typ
        gd["dtO_free"][par] = typ
        gd["qRep_free"][par] = typ
        gd["sz_tok"].pop(c, None)

        evs = []
        for mo in range(8):
            bankname = "pb2" if mo % 2 == 0 else "pb3"
            bank = pb[2] if mo % 2 == 0 else pb[3]

            def mmf(e, mo=mo, bank=bank):
                for kc in range(8):
                    r = e.matmul(bank[:, 0:CH], w_out_bf[:, kc, mo * 128:(mo + 1) * 128],
                                 Y3[:, kc, :], start=(kc == 0), stop=(kc == 7))
                return r
            mt = b.pe_mm(bankname, [typ, w_cast_done], mmf)
            ev = b.evac(bankname, "act", lambda e, mo=mo, bank=bank: e.copy(
                attnP[:, mo, :], bank[:, 0:CH]), waits=[mt, gd["attnP_free"]])
            evs.append(ev)
        gd["Y3_free"] = ("pe", b.cnt["pe"])
        gd["BrTf_free"][par] = ("pe", b.cnt["pe"])
        if c == dbg_chunk:
            dbg("Y3", Y3[:, :, :], gd["ypost_dve"][par])
            dbg("attnP", attnP[:, :, :], evs[-1])
            dbg("szD", sz_bf[par][:, :, :], gd["ypost_dve"][par])
        tst = b.dma("po", rs_in[(c % 2) * 128:(c % 2) * 128 + 128, :],
                    attnP.ap().rearrange("p a b -> p (a b)"),
                    waits=evs + [gd["rs_next"]], sem="st")
        gd["attnP_free"] = tst

    def rs_xnew(c):
        oc = c // 2
        tcc = b.op("po", lambda e: e.collective_compute(
            "ReduceScatter", ALU.add, replica_groups=PAIRS,
            ins=[rs_in], outs=[rs_out]),
            waits=[("st", b.cnt["st"]), ("rsl", b.cnt["rsl"])], inc=("cc", 1))
        gd["rs_next"] = tcc
        tlb = b.dma("sp", attnS.ap().rearrange("p a b -> p (a b)"), rs_out,
                    waits=[tcc, gd["attnS_free"]], sem="rsl")
        if oc == 0:
            dbg("attnS", attnS[:, :, :], tlb)
        prev_store = None
        for tt in range(NTT):
            evl = []
            for dc in range(8):
                ev = tbf(attnS[:, dc, tt * 128:(tt + 1) * 128],
                         attn_tm[:, dc * 128:(dc + 1) * 128], [tlb])
                if ev is not None:
                    evl.append(ev)
            ev = flush_tbf()
            if ev is not None:
                evl.append(ev)
            sl = 0
            tl = b.dma("sp", xo_t[sl][:, :],
                       x_own[oc * CH + tt * 128: oc * CH + (tt + 1) * 128, :],
                       waits=[gd["xo_slot"][sl]], sem="xB")
            b.op("dve", lambda e: e.tensor_tensor(xn_t[:, :], attn_tm[:, :],
                                                  gaRep[:, :], ALU.mult),
                 waits=evl + [tl, prev_store, setup_act])
            txn = b.op("dve", lambda e, sl=sl: e.tensor_tensor(
                xn_t[:, :], xo_t[sl][:, :], xn_t[:, :], ALU.add))
            if dbg_on and oc == 0 and tt == 0:
                dbg("gaRep", gaRep[:, :], txn)
                dbg("saRep", saRep[:, :], txn)
                dbg("caRep", caRep[:, :], txn)
                t1_ = b.op("dve", lambda e: e.tensor_copy(snapT[:, :], attn_tm[:, :]))
                t2_ = b.op("dve", lambda e, sl=sl: e.tensor_copy(snapX[:, :], xo_t[sl][:, :]))
                dbg("attn_tm0", snapT[:, :], t1_)
                dbg("xo0", snapX[:, :], t2_)
            gd["xo_slot"][sl] = txn
            prev_store = b.dma("po", xnew_d[oc * CH + tt * 128: oc * CH + (tt + 1) * 128, :],
                               xn_t[:, :], waits=[txn], sem="st")
        gd["attnS_free"] = ("pe", b.cnt["pe"])

    if n_chunks > 0:
        front(0)
    for c in range(n_chunks):
        if c + 1 < n_chunks:
            front(c + 1)
            patch(c)
        else:
            patch(c, clast=True)
        if not do_scan:
            continue
        scan(c)
        ypost_wout(c)
        if c % 2 == 1:
            rs_xnew(c)

    mamba_done = {e: (e, b.cnt[e]) for e in ("pe", "act", "dve", "po")}
    mamba_st = ("st", b.cnt["st"])
    es_m.close()
    b.barrier()

    es_f = ExitStack()
    fx_t = [b.sb(f"fx{i}", [128, D], F32, es_f) for i in range(2)]
    fh_tm = b.sb("fh_tm", [128, D], BF16, es_f)
    h2_fm = b.sb("h2_fm", [128, 8, FTOK], BF16, es_f)
    flnc = b.sb("flnc", [128, D], F32, es_f)
    flncol = b.sb("flncol", [128, 4], F32, es_f)
    flnsq = b.sb("flnsq", [128, D], F32, es_f)
    logitsF = b.sb("logitsF", [8, FTOK], F32, es_f)
    lg_t = b.sb("lg_t", [128, 8], F32, es_f)
    top_t = [b.sb(f"top{i}", [128, 8], F32, es_f) for i in range(2)]
    topc = b.sb("topc", [128, 4], F32, es_f)
    combA = b.sb("combA", [128, 8, 8], F32, es_f)
    combT = b.sb("combT", [8, FTOK], BF16, es_f)
    cRep = b.sb("cRep", [128, FTOK], F32, es_f)
    wgStg = [b.sb(f"wgS{i}", [128, 8, 128], F32, es_f) for i in range(2)]
    wuStg = [b.sb(f"wuS{i}", [128, 8, 128], F32, es_f) for i in range(2)]
    wgB = [b.sb(f"wgB{i}", [128, 8, 128], BF16, es_f) for i in range(2)]
    wuB = [b.sb(f"wuB{i}", [128, 8, 128], BF16, es_f) for i in range(2)]
    wdStg = [b.sb(f"wdS{i}", [128, HM, 128], F32, es_f) for i in range(2)]
    wdB = [b.sb("wdB0", [128, HM, 128], BF16, es_f)] * 2
    sg_t = [b.sb(f"sg{i}", [128, 512], BF16, es_f) for i in range(2)]
    hgAll = b.sb("hgAll", [128, HM, FTOK], BF16, es_f)
    acc = b.sb("acc", [128, 8, FTOK], F32, es_f)
    dtmp = [b.sb(f"dtmp{i}", [128, 512], F32, es_f) for i in range(2)]
    acc_tm = b.sb("acc_tm", [128, D], F32, es_f)
    fo_t = b.sb("fo_t", [128, D], F32, es_f)

    fgd = {"fx": [None, None], "h2fm_pe": None, "stg": {}, "acc_free": None,
           "hg_pe": None, "sg_guard": [None, None], "dtmp_guard": [None, None],
           "fsq_prev": None, "fo_store": None, "crep_dve": None, "acc_dve": None}

    def ffn_ln(xtile, out_bf, waits_x):
        b.op("dve", lambda e: e.reduce_sum(flncol[:, 0:1], xtile[:, :], axis=AXX),
             waits=list(waits_x) + [fgd["fsq_prev"]])
        b.op("dve", lambda e: e.tensor_scalar_mul(flncol[:, 1:2], flncol[:, 0:1],
                                                  1.0 / D), ss=True)
        tc = b.op("dve", lambda e: e.tensor_scalar(flnc[:, :], xtile[:, :],
                                                   flncol[:, 1:2], None, ALU.subtract),
                  ss=True)
        tq = b.op("act", lambda e: e.activation(flnsq[:, :], flnc[:, :], AF.Square,
                                                accum_out=flncol[:, 2:3]), waits=[tc])
        fgd["fsq_prev"] = tq
        b.op("act", lambda e: e.activation(flncol[:, 3:4], flncol[:, 2:3], AF.Ln,
                                           scale=1.0 / D, bias=LN_EPS), ss=True)
        tr = b.op("act", lambda e: e.activation(flncol[:, 3:4], flncol[:, 3:4],
                                                AF.Exp, scale=-0.5), ss=True)
        b.op("dve", lambda e: e.tensor_scalar(flnc[:, :], flnc[:, :], flncol[:, 3:4],
                                              None, ALU.mult), waits=[tr])
        b.op("dve", lambda e: e.tensor_tensor(flnsq[:, :], flnc[:, :], cbRep[:, :],
                                              ALU.mult))
        return b.op("dve", lambda e: e.tensor_tensor(out_bf[:, :], flnsq[:, :],
                                                     sbRep[:, :], ALU.add))

    for fb in range(min(2 if do_ffn else 0, int(os.environ.get('MK_NFB', 2)))):
        base_tok = fb * FTOK
        th2 = None
        for tt in range(8):
            sl = tt % 2
            sname = "xnA" if sl == 0 else "xnB"
            extra = [mamba_done["dve"], mamba_done["po"], mamba_done["act"],
                     mamba_done["pe"], mamba_st] if (fb == 0 and tt < 2) else []
            tl = b.dma("sp", fx_t[sl][:, :],
                       xnew_d[base_tok + tt * 128: base_tok + (tt + 1) * 128, :],
                       waits=[fgd["fx"][sl]] + extra, sem=sname)
            tln = ffn_ln(fx_t[sl], fh_tm, [tl, ("pe", b.cnt["pe"])])
            fgd["fx"][sl] = tln
            for dc in range(8):
                tbf(fh_tm[:, dc * 128:(dc + 1) * 128],
                    h2_fm[:, dc, tt * 128:(tt + 1) * 128],
                    [tln, fgd["h2fm_pe"]])
            th2 = flush_tbf()
        th2 = None

        lg_evs = []
        for nt in range(FNT):
            def rmm(e, nt=nt):
                for kc in range(8):
                    r = e.matmul(pb[5][0:8, 0:512], router_bf[:, kc, :],
                                 h2_fm[:, kc, nt * 512:(nt + 1) * 512],
                                 start=(kc == 0), stop=(kc == 7))
                return r
            mt = b.pe_mm("pb5", [tb_state["guard"][0], tb_state["guard"][1]], rmm)
            lg_evs.append(b.evac("pb5", "act", lambda e, nt=nt: e.copy(
                logitsF[0:8, nt * 512:(nt + 1) * 512], pb[5][0:8, 0:512]),
                waits=[mt]))
        comb_evs = []
        for tt in range(8):
            tT = tf32(logitsF[0:8, tt * 128:(tt + 1) * 128], lg_t[:, 0:8],
                      lg_evs + comb_evs, 8, 128, eng="dve")
            b.op("dve", lambda e: e.reduce_max(topc[:, 0:1], lg_t[:, :], axis=AXX),
                 waits=[tT], ss=True)
            b.op("dve", lambda e: e.tensor_scalar(top_t[0][:, :], lg_t[:, :],
                                                  topc[:, 0:1], None, ALU.is_equal),
                 ss=True)
            b.op("dve", lambda e: e.scalar_tensor_tensor(
                top_t[1][:, :], top_t[0][:, :], -1e30, lg_t[:, :], ALU.mult, ALU.add),
                ss=True)
            b.op("dve", lambda e: e.reduce_max(topc[:, 1:2], top_t[1][:, :], axis=AXX),
                 ss=True)
            tdd = b.op("dve", lambda e: e.tensor_tensor(topc[:, 2:3], topc[:, 1:2],
                                                        topc[:, 0:1], ALU.subtract),
                       ss=True)
            tex = b.op("act", lambda e: e.activation(topc[:, 2:3], topc[:, 2:3],
                                                     AF.Exp), waits=[tdd])
            b.op("dve", lambda e: e.tensor_scalar_add(topc[:, 3:4], topc[:, 2:3], 1.0),
                 waits=[tex], ss=True)
            b.op("dve", lambda e: e.reciprocal(topc[:, 3:4], topc[:, 3:4]), ss=True)
            b.op("dve", lambda e: e.tensor_scalar(topc[:, 2:3], topc[:, 3:4], -1.0,
                                                  1.0, ALU.mult, ALU.add), ss=True)
            b.op("dve", lambda e: e.tensor_scalar(top_t[1][:, :], top_t[1][:, :],
                                                  topc[:, 1:2], None, ALU.is_equal),
                 ss=True)
            b.op("dve", lambda e: e.tensor_scalar(top_t[0][:, :], top_t[0][:, :],
                                                  topc[:, 3:4], None, ALU.mult),
                 ss=True)
            tcb = b.op("dve", lambda e, tt=tt: e.scalar_tensor_tensor(
                combA[:, tt, :], top_t[1][:, :], topc[:, 2:3], top_t[0][:, :],
                ALU.mult, ALU.add), ss=True)
            comb_evs.append(tf32(combA[:, tt, :],
                                 combT[0:8, tt * 128:(tt + 1) * 128],
                                 [tcb], 128, 8, eng="dve"))

        for e_i in range(int(os.environ.get('MK_NE', NE)) if ffn_lvl >= 2 else 0):
            crep_evs = []
            for nt in range(FNT):
                mt = b.pe_mm("pb4", comb_evs, lambda e, e_i=e_i, nt=nt: e.matmul(
                    pb[4][:, 0:512], selAll[0:8, e_i, :],
                    combT[:, nt * 512:(nt + 1) * 512], start=True, stop=True))
                crep_evs.append(b.evac("pb4", "act", lambda e, nt=nt: e.copy(
                    cRep[:, nt * 512:(nt + 1) * 512], pb[4][:, 0:512]),
                    waits=[mt, fgd["crep_dve"]]))

            hg_last = None
            for m in range(int(os.environ.get('MK_HM', HM)) if ffn_lvl >= 3 else 0):
                sl = m % 2
                sgn = "fgA" if sl == 0 else "fgB"
                sun = "fuA" if sl == 0 else "fuB"
                tlg = b.dma("sp", wgStg[sl][:, :, :], wg_pm[e_i, m],
                            waits=[fgd["stg"].get(("wg", sl))], sem=sgn)
                tlu = b.dma("sp", wuStg[sl][:, :, :], wu_pm[e_i, m],
                            waits=[fgd["stg"].get(("wu", sl))], sem=sun)
                tcg = b.op("po", lambda e, sl=sl: e.tensor_copy(
                    wgB[sl][:, :, :], wgStg[sl][:, :, :]),
                    waits=[tlg, fgd["stg"].get(("wgB", sl))])
                tcu = b.op("po", lambda e, sl=sl: e.tensor_copy(
                    wuB[sl][:, :, :], wuStg[sl][:, :, :]),
                    waits=[tlu, fgd["stg"].get(("wuB", sl))])
                fgd["stg"][("wg", sl)] = tcg
                fgd["stg"][("wu", sl)] = tcu
                if dbg_on and fb == 0 and e_i == 0 and m == 0:
                    dbg("wgB0", wgB[0][:, :, :], tcg)
                    dbg("wgS0", wgStg[0][:, :, :], tcg)
                MK_SUB = int(os.environ.get('MK_SUB', 5))
                for nt in range(FNT if MK_SUB >= 2 else 0):
                    def gmm(e, sl=sl, nt=nt):
                        for kc in range(8):
                            r = e.matmul(pb[0][:, :], wgB[sl][:, kc, :],
                                         h2_fm[:, kc, nt * 512:(nt + 1) * 512],
                                         start=(kc == 0), stop=(kc == 7))
                        return r
                    mtg = b.pe_mm("pb0", [tcg, tb_state["guard"][0], tb_state["guard"][1]], gmm)

                    def umm(e, sl=sl, nt=nt):
                        for kc in range(8):
                            r = e.matmul(pb[1][:, :], wuB[sl][:, kc, :],
                                         h2_fm[:, kc, nt * 512:(nt + 1) * 512],
                                         start=(kc == 0), stop=(kc == 7))
                        return r
                    if MK_SUB < 3:
                        continue
                    mtu = b.pe_mm("pb1", [tcu, tb_state["guard"][0], tb_state["guard"][1]], umm)
                    ssl = (m * FNT + nt) % 2
                    if MK_SUB < 4:
                        continue
                    tsg2 = b.evac("pb0", "act", lambda e, ssl=ssl: e.activation(
                        sg_t[ssl][:, :], pb[0][:, :], AF.Silu),
                        waits=[mtg, fgd["sg_guard"][ssl]])
                    if MK_SUB < 5:
                        continue
                    if dbg_on and fb == 0 and e_i == 0 and m == 0 and nt == 0:
                        ts1_ = b.op("act", lambda e, ssl=ssl: e.copy(
                            dtmp[1][:, :], sg_t[ssl][:, :]), waits=[tsg2])
                        ts2_ = b.op("dve", lambda e: e.tensor_copy(
                            dtmp[0][:, :], pb[1][:, :]), waits=[mtu])
                        dbg("sg0", dtmp[1][:, :], ts1_)
                        dbg("up0", dtmp[0][:, :], ts2_)
                        b.bank["pb1"] = ts2_
                        fgd["dtmp_guard"][0] = ("st", b.cnt["st"])
                        fgd["dtmp_guard"][1] = ("st", b.cnt["st"])
                    thg = b.evac("pb1", "dve",
                                 lambda e, ssl=ssl, m=m, nt=nt: e.tensor_tensor(
                                     hgAll[:, m, nt * 512:(nt + 1) * 512],
                                     sg_t[ssl][:, :], pb[1][:, :], ALU.mult),
                                 waits=[mtu, tsg2, fgd["hg_pe"]])
                    fgd["sg_guard"][ssl] = thg
                    hg_last = thg
                fgd["stg"][("wgB", sl)] = ("pe", b.cnt["pe"])
                fgd["stg"][("wuB", sl)] = ("pe", b.cnt["pe"])
                if m == HM - 1:
                    fgd["stg"][("wgB", 1 - sl)] = ("pe", b.cnt["pe"])
                    fgd["stg"][("wuB", 1 - sl)] = ("pe", b.cnt["pe"])

            for dm in range(8 if ffn_lvl >= 4 else 0):
                sl = dm % 2
                sdn = "fdA" if sl == 0 else "fdB"
                tld = b.dma("sp", wdStg[sl][:, :, :], wd_pm[e_i, dm],
                            waits=[fgd["stg"].get(("wd", sl))], sem=sdn)
                tcd = b.op("po", lambda e, sl=sl: e.tensor_copy(
                    wdB[sl][:, :, :], wdStg[sl][:, :, :]),
                    waits=[tld, fgd["stg"].get("wdB")])
                fgd["stg"][("wd", sl)] = tcd
                for nt in range(FNT):
                    bankname = "pb2" if (dm * FNT + nt) % 2 == 0 else "pb3"
                    bank = pb[2] if (dm * FNT + nt) % 2 == 0 else pb[3]

                    def dmm(e, sl=sl, nt=nt, bank=bank):
                        for kc in range(HM):
                            r = e.matmul(bank[:, :], wdB[sl][:, kc, :],
                                         hgAll[:, kc, nt * 512:(nt + 1) * 512],
                                         start=(kc == 0), stop=(kc == HM - 1))
                        return r
                    mtd = b.pe_mm(bankname, [tcd, hg_last], dmm)
                    if e_i == 0:
                        b.evac(bankname, "dve",
                               lambda e, dm=dm, nt=nt, bank=bank: e.tensor_tensor(
                                   acc[:, dm, nt * 512:(nt + 1) * 512], bank[:, :],
                                   cRep[:, nt * 512:(nt + 1) * 512], ALU.mult),
                               waits=[mtd, fgd["acc_free"]] + crep_evs)
                    else:
                        tsl = (dm * FNT + nt) % 2
                        ev = b.evac(bankname, "dve",
                                    lambda e, tsl=tsl, nt=nt, bank=bank:
                                    e.tensor_tensor(dtmp[tsl][:, :], bank[:, :],
                                                    cRep[:, nt * 512:(nt + 1) * 512],
                                                    ALU.mult),
                                    waits=[mtd, fgd["dtmp_guard"][tsl]] + crep_evs)
                        fgd["dtmp_guard"][tsl] = b.op(
                            "po", lambda e, dm=dm, nt=nt, tsl=tsl: e.tensor_tensor(
                                acc[:, dm, nt * 512:(nt + 1) * 512],
                                acc[:, dm, nt * 512:(nt + 1) * 512],
                                dtmp[tsl][:, :], ALU.add), waits=[ev])
                fgd["stg"]["wdB"] = ("pe", b.cnt["pe"])
            fgd["hg_pe"] = ("pe", b.cnt["pe"])
            fgd["crep_dve"] = ("dve", b.cnt["dve"])
        tacc = b.op("po", lambda e: e.engine_nop())

        for tt in range(8 if ffn_lvl >= 5 else 0):
            evl = []
            for dc in range(8):
                evl.append(tf32(acc[:, dc, tt * 128:(tt + 1) * 128],
                                acc_tm[:, dc * 128:(dc + 1) * 128],
                                [tacc, fgd["acc_dve"]], 128, 128))
            sl = tt % 2
            sname = "xnA" if sl == 0 else "xnB"
            tl = b.dma("sp", fx_t[sl][:, :],
                       xnew_d[base_tok + tt * 128: base_tok + (tt + 1) * 128, :],
                       waits=[fgd["fx"][sl]], sem=sname)
            b.op("dve", lambda e: e.tensor_tensor(fo_t[:, :], acc_tm[:, :],
                                                  gbRep[:, :], ALU.mult),
                 waits=evl + [tl, fgd["fo_store"]])
            tfo = b.op("dve", lambda e, sl=sl: e.tensor_tensor(
                fo_t[:, :], fx_t[sl][:, :], fo_t[:, :], ALU.add))
            fgd["acc_dve"] = tfo
            fgd["fx"][sl] = tfo
            fgd["fo_store"] = b.dma(
                "po", out_t[base_tok + tt * 128: base_tok + (tt + 1) * 128, :],
                fo_t[:, :], waits=[tfo], sem="st")
        if dbg_on and fb == 0:
            dbg("cRep", cRep[:, :], ("act", b.cnt["act"]))
            dbg("logitsF", logitsF[:, :], ("act", b.cnt["act"]))
            dbg("combT", combT[:, :], ("dve", b.cnt["dve"]))
            dbg("acc", acc[:, :, :], ("po", b.cnt["po"]))
            dbg("hgAll", hgAll[:, :, :], ("dve", b.cnt["dve"]))
            dbg("h2fm", h2_fm[:, :, :], ("dve", b.cnt["dve"]))
        fgd["acc_free"] = ("pe", b.cnt["pe"])
        fgd["h2fm_pe"] = ("pe", b.cnt["pe"])

    b.op("po", lambda e: e.engine_nop(), waits=[("st", b.cnt["st"])])
    es_f.close()

    blk = b.es.enter_context(nc.Block())

    def runner(qn):
        def body(eng):
            for cl in b.q[qn]:
                cl(eng)
        return body

    blk.sync(runner("sp"))
    blk.tensor(runner("pe"))
    blk.scalar(runner("act"))
    blk.vector(runner("dve"))
    blk.gpsimd(runner("po"))
    b.es.close()
    return b


_prog = {}


def get_prog():
    if "b" not in _prog:
        _prog["b"] = build_program()
    return _prog["b"]


def kernel(L0_attn_x, L0_attn_W_in, L0_attn_W_x, L0_attn_W_dt, L0_attn_b_dt,
           L0_attn_A_log, L0_attn_B_bias, L0_attn_C_bias, L0_attn_B_norm_w,
           L0_attn_C_norm_w, L0_attn_rope_freqs, L0_attn_D, L0_attn_W_out,
           L0_ffn_router_w, L0_ffn_wg, L0_ffn_wu, L0_ffn_wd, L0_c, L0_wmod):
    f32 = np.float32
    x = np.ascontiguousarray(np.asarray(L0_attn_x, f32))
    W_in = np.asarray(L0_attn_W_in, f32)
    W_x = np.asarray(L0_attn_W_x, f32)
    W_dt = np.asarray(L0_attn_W_dt, f32).reshape(-1)
    b_dt = np.asarray(L0_attn_b_dt, f32).reshape(-1)
    A_log = np.asarray(L0_attn_A_log, f32)
    Dv = np.asarray(L0_attn_D, f32).reshape(-1)
    W_out = np.asarray(L0_attn_W_out, f32)
    router = np.asarray(L0_ffn_router_w, f32)
    wg = np.asarray(L0_ffn_wg, f32)
    wu = np.asarray(L0_ffn_wu, f32)
    wd = np.asarray(L0_ffn_wd, f32)
    cmat = np.asarray(L0_c, f32)
    wmod = np.asarray(L0_wmod, f32)

    def pm_kc(w):
        K, M = w.shape
        return np.ascontiguousarray(w.reshape(K // 128, 128, M).transpose(1, 0, 2))

    sperm = np.concatenate([np.arange(0, 16, 2), np.arange(1, 16, 2)])
    W_x = np.concatenate([W_x[sperm], W_x[16 + sperm], W_x[32:34]], 0)
    A_log = A_log[:, sperm]
    smalls_perm = sperm

    H = 2752
    wg_p = np.zeros((NE, 1024, HP), f32)
    wg_p[:, :, :H] = wg
    wu_p = np.zeros((NE, 1024, HP), f32)
    wu_p[:, :, :H] = wu
    wd_p = np.zeros((NE, HP, 1024), f32)
    wd_p[:, :H, :] = wd
    wg_pm = np.ascontiguousarray(
        wg_p.reshape(NE, 8, 128, HM, 128).transpose(0, 3, 2, 1, 4))
    wu_pm = np.ascontiguousarray(
        wu_p.reshape(NE, 8, 128, HM, 128).transpose(0, 3, 2, 1, 4))
    wd_pm = np.ascontiguousarray(
        wd_p.reshape(NE, HM, 128, 8, 128).transpose(0, 3, 2, 1, 4))
    router_pm = pm_kc(router)
    wmod_pm = pm_kc(wmod)
    smalls = np.concatenate([np.asarray(L0_attn_B_bias, f32)[sperm],
                             np.asarray(L0_attn_C_bias, f32)[sperm],
                             np.asarray(L0_attn_B_norm_w, f32)[sperm],
                             np.asarray(L0_attn_C_norm_w, f32)[sperm]]).reshape(1, 64)
    rfreq = np.asarray(L0_attn_rope_freqs, f32).reshape(1, 8)

    in_maps = []
    for core in range(8):
        bb, j = core // 2, core % 2
        own = slice(j * 1024, (j + 1) * 1024)
        W_sel = np.concatenate(
            [W_in[own], W_in[2048 + j * 1024: 2048 + (j + 1) * 1024]], 0)
        ordv = np.concatenate([np.arange(j * 1024, (j + 1) * 1024),
                               np.arange((1 - j) * 1024, (2 - j) * 1024)])
        in_maps.append({
            "xb": np.ascontiguousarray(x[bb]),
            "x_own": np.ascontiguousarray(
                x[bb].reshape(NCH, CH, D)[j::2].reshape(OWN_TOK, D)),
            "w_in_pm": pm_kc(np.ascontiguousarray(W_sel.T)),
            "w_out_pm": pm_kc(np.ascontiguousarray(W_out[:, own].T)),
            "w_x_pm": pm_kc(np.ascontiguousarray(W_x[:, own].T)),
            "wdt_pm": np.ascontiguousarray(W_dt[ordv].reshape(16, 128).T),
            "bdt_pm": np.ascontiguousarray(b_dt[ordv].reshape(16, 128).T),
            "a_pm": np.ascontiguousarray(
                A_log[own].reshape(8, 128, 16).transpose(1, 0, 2).reshape(128, 128)),
            "d_pm": np.ascontiguousarray(Dv[own].reshape(8, 128).T),
            "smalls": smalls, "rfreq": rfreq,
            "c_pm": np.ascontiguousarray(cmat[bb].reshape(8, 128).T),
            "wmod_pm": wmod_pm, "router_pm": router_pm,
            "wg_pm": wg_pm, "wu_pm": wu_pm, "wd_pm": wd_pm,
        })

    bobj = get_prog()
    res = run_bass_kernel_spmd(bobj.nc, in_maps, core_ids=list(range(8)))

    out = np.empty((4, T, D), np.float32)
    for core in range(8):
        bb, j = core // 2, core % 2
        o = np.asarray(res.results[core]["out"]).reshape(OWN_TOK, D)
        out[bb].reshape(NCH, CH, D)[j::2] = o.reshape(NCH // 2, CH, D)
    return out


# version 06
# speedup vs baseline: 1.1697x; 1.1697x over previous
import os
import numpy as np
from contextlib import ExitStack

import concourse.bass as bass
import concourse.mybir as mybir
from concourse.bass_utils import run_bass_kernel_spmd

F32 = mybir.dt.float32
BF16 = mybir.dt.bfloat16
I32 = mybir.dt.int32
AF = mybir.ActivationFunctionType
ALU = mybir.AluOpType
AXX = mybir.AxisListType.X

T, D = 4096, 1024
CH = 256
NCH = T // CH
NTT = CH // 128
S = 16
EO = 1024
HP = 2816
HM = HP // 128
NE = 8
FTOK = 1024
FNT = FTOK // 512
OWN_TOK = 2048
PAIRS = [[0, 1], [2, 3], [4, 5], [6, 7]]
RMS_EPS = float(np.finfo(np.float32).eps)
LN_EPS = 1e-5
PI = float(np.pi)
TWO_PI = float(2 * np.pi)
INV_2PI = float(1.0 / (2 * np.pi))

ENGS = ("sp", "pe", "act", "dve", "po")


class B:
    def __init__(self):
        self.nc = bass.Bass("TRN2", target_bir_lowering=False)
        self.es = ExitStack()
        self.q = {e: [] for e in ENGS}
        self.semh = {}
        self.cnt = {}
        self.lastw = {e: {} for e in ENGS}
        self.bank = {}

    def sem(self, name):
        self.semh[name] = self.es.enter_context(self.nc.semaphore(name))
        self.cnt[name] = 0

    def sb(self, name, shape, dt, stack=None):
        return (stack or self.es).enter_context(self.nc.sbuf_tensor(name, shape, dt))

    def ps(self, name, shape, dt):
        return self.es.enter_context(self.nc.psum_tensor(name, shape, dt))

    def op(self, eng, fn, waits=(), inc=None, ss=False):
        if ss:
            waits = list(waits) + [(eng, self.cnt[eng])]
        ws = []
        lw = self.lastw[eng]
        for w in waits:
            if w is None:
                continue
            s, v = w
            if lw.get(s, 0) >= v:
                continue
            ws.append((s, v))
            lw[s] = v
        if inc is None:
            inc = (eng, 1)
        tok = None
        if inc:
            sname, n = inc
            self.cnt[sname] += n
            tok = (sname, self.cnt[sname])
        semh = self.semh

        def closure(e, ws=tuple(ws), fn=fn, inc=inc):
            for s, v in ws:
                e.wait_ge(semh[s], v)
            if fn is None:
                return
            r = fn(e)
            if inc:
                r.then_inc(semh[inc[0]], inc[1])
        self.q[eng].append(closure)
        return tok

    def barrier(self):
        toks = [(e, self.cnt[e]) for e in ("pe", "act", "dve", "po")]
        for eng in ENGS:
            self.op(eng, None, waits=toks, inc=False)

    def dma(self, eng, out, in_, waits=(), sem="ld", **kw):
        return self.op(eng, lambda e: e.dma_start(out=out, in_=in_, **kw),
                       waits, inc=(sem, 16))

    def pe_mm(self, bankname, waits, fn):
        g = self.bank.get(bankname)
        return self.op("pe", fn, list(waits) + [g])

    def evac(self, bankname, eng, fn, waits=()):
        tok = self.op(eng, fn, waits)
        self.bank[bankname] = tok
        return tok


def build_program(n_chunks=None, do_ffn=None, do_scan=True, ffn_lvl=5):
    if n_chunks is None:
        n_chunks = int(os.environ.get("MK_NCHUNKS", NCH))
    if do_ffn is None:
        do_ffn = bool(int(os.environ.get("MK_FFN", 1)))
    b = B()
    nc = b.nc

    def din(name, shape, dt=F32):
        return nc.dram_tensor(name, shape, dt, kind="ExternalInput").ap()

    xb = din("xb", [T, D])
    x_own = din("x_own", [OWN_TOK, D])
    w_in_pm = din("w_in_pm", [128, 8, 2048])
    w_out_pm = din("w_out_pm", [128, 8, 1024])
    w_x_pm = din("w_x_pm", [128, 8, 34])
    wdt_pm = din("wdt_pm", [128, 16])
    bdt_pm = din("bdt_pm", [128, 16])
    a_pm = din("a_pm", [128, 128])
    d_pm = din("d_pm", [128, 8])
    smalls = din("smalls", [1, 64])
    rfreq = din("rfreq", [1, 8])
    c_pm = din("c_pm", [128, 8])
    wmod_pm = din("wmod_pm", [128, 8, 6144])
    router_pm = din("router_pm", [128, 8, 8])
    wg_pm = din("wg_pm", [NE, HM, 128, 8, 128])
    wu_pm = din("wu_pm", [NE, HM, 128, 8, 128])
    wd_pm = din("wd_pm", [NE, 8, 128, HM, 128])

    out_t = nc.dram_tensor("out", [OWN_TOK, D], F32, kind="ExternalOutput").ap()

    ag_in = nc.dram_tensor("ag_in", [34, CH], F32).ap()
    ag_out = nc.dram_tensor("ag_out", [68, CH], F32).ap()
    rs_in = nc.dram_tensor("rs_in", [256, 8 * CH], BF16).ap()
    rs_out = nc.dram_tensor("rs_out", [128, 8 * CH], BF16).ap()
    xnew_d = nc.dram_tensor("xnew_d", [OWN_TOK, D], F32, kind="ExternalOutput" if os.environ.get("MK_DEBUG") else "Internal").ap()

    dbg_on = bool(os.environ.get("MK_DEBUG"))
    dbg_chunk = int(os.environ.get("MK_DEBUG_CHUNK", 0))

    def dbg(name, ap_, wait):
        if not dbg_on:
            return
        t = nc.dram_tensor("dbg_" + name, list(ap_.shape), ap_.dtype,
                           kind="ExternalOutput").ap()
        b.dma("po", t, ap_, waits=[wait], sem="st")

    for s in ("ld", "st", "cc", "pe", "act", "dve", "po",
              "xA", "xB", "wiS", "agl", "rsl", "lrw",
              "fgA", "fgB", "fuA", "fuB", "fdA", "fdB", "xnA", "xnB"):
        b.sem(s)

    pb = [b.ps(f"pb{i}", [128, 512], F32) for i in range(6)]
    pT1 = b.ps("pT1", [128, 1024], BF16)
    pT2 = b.ps("pT2", [128, 1024], BF16)

    identF = b.sb("identF", [128, 128], F32)
    identB = b.sb("identB", [128, 128], BF16)
    onesCol = b.sb("onesCol", [128, 1], F32)
    ones1 = b.sb("ones1", [1, 128], F32)
    selAll = b.sb("selAll", [16, 16, 128], BF16)
    sfreqRep = b.sb("sfreqRep", [128, 8], F32)
    bRepB = b.sb("bRepB", [128, 16], F32)
    bRepC = b.sb("bRepC", [128, 16], F32)
    wRepB = b.sb("wRepB", [128, 16], F32)
    wRepC = b.sb("wRepC", [128, 16], F32)
    wdtCol = b.sb("wdtCol", [128, 16], F32)
    bdtCol = b.sb("bdtCol", [128, 16], F32)
    A_sb = b.sb("A_sb", [128, 128], F32)
    dColS = b.sb("dColS", [128, 8], F32)
    wx_bf = b.sb("wx_bf", [128, 8, 34], BF16)
    router_bf = b.sb("router_bf", [128, 8, 8], BF16)
    caRep = b.sb("caRep", [128, D], BF16)
    saRep = b.sb("saRep", [128, D], BF16)
    gaRep = b.sb("gaRep", [128, D], BF16)
    cbRep = b.sb("cbRep", [128, D], BF16)
    sbRep = b.sb("sbRep", [128, D], BF16)
    gbRep = b.sb("gbRep", [128, D], BF16)
    smallRow = b.sb("smallRow", [1, 64], F32)
    freqRow = b.sb("freqRow", [1, 8], F32)
    sfRow = b.sb("sfRow", [1, 8], F32)

    CONSTS = [LN_EPS, RMS_EPS, 1.0, -1.0, -0.5, 1.0 / 16, 1.0 / 1024, 1.0 / 2048,
              PI, PI / 2, TWO_PI, -TWO_PI, INV_2PI, 0.25, 2.0, 1e-4, -1e30, 0.0,
              0.5]
    constT = b.sb("constT", [128, 32], F32)
    for i, v in enumerate(CONSTS):
        b.op("po", lambda e, i=i, v=v: e.memset(constT[:, i:i + 1], float(v)),
             inc=False)
        nc.const_aps.register(F32, float(v), constT[:, i:i + 1])

    tb_state = {"pend": [], "bank": 0, "guard": [None, None], "setup": None}

    def flush_tbf():
        pend = tb_state["pend"]
        if not pend:
            return None
        bk = tb_state["bank"]
        bank = pT1 if bk == 0 else pT2
        bankname = "pT1" if bk == 0 else "pT2"
        tb_state["bank"] = 1 - bk
        tb_state["pend"] = []
        waits = [tb_state["setup"]]
        for (_, _, w) in pend:
            waits += w
        waits.append(tb_state["guard"][bk])

        def body(e, pend=tuple(pend), bank=bank):
            for i, (src, _, _) in enumerate(pend):
                r = e.transpose(bank[:, i * 128:(i + 1) * 128], src, identB[:, :])
            return r
        mt = b.pe_mm(bankname, waits, body)
        ev = None
        for i, (_, dst, _) in enumerate(pend):
            ev = b.op("dve", lambda e, i=i, dst=dst, bank=bank: e.tensor_copy(
                dst, bank[:, i * 128:(i + 1) * 128]), waits=[mt])
        tb_state["guard"][bk] = ev
        b.bank[bankname] = ev
        return ev

    def tbf(src_ap, dst_ap, waits):
        tb_state["pend"].append((src_ap, dst_ap, list(waits)))
        if len(tb_state["pend"]) == 4:
            return flush_tbf()
        return None

    def tf32(src_ap, dst_ap, waits, p_in, f_in, eng="act"):
        mt = b.pe_mm("pb5", list(waits) + [tb_state["setup"]], lambda e: e.transpose(
            pb[5][0:f_in, 0:p_in], src_ap, identF[0:p_in, 0:p_in]))
        if eng == "act":
            fn = lambda e: e.copy(dst_ap, pb[5][0:f_in, 0:p_in])
        else:
            fn = lambda e: e.tensor_copy(dst_ap, pb[5][0:f_in, 0:p_in])
        return b.evac("pb5", eng, fn, waits=[mt])

    es_setup = ExitStack()
    selF = b.sb("selF", [16, 16, 128], F32, es_setup)
    modRow = b.sb("modRow", [1, 6144], F32, es_setup)
    wmod_sb = [b.sb(f"wmod_sb{i}", [128, 8, 512], F32, es_setup) for i in range(2)]
    cColT = b.sb("cColT", [128, 8], F32, es_setup)
    scCol = b.sb("scCol", [128, 8], F32, es_setup)
    wxStg = b.sb("wxStg", [128, 8, 34], F32, es_setup)
    rtStg = b.sb("rtStg", [128, 8, 8], F32, es_setup)
    gtmpA = b.sb("gtmpA", [1, 1024], F32, es_setup)
    gtmpS = b.sb("gtmpS", [1, 1024], F32, es_setup)

    b.op("po", lambda e: e.iota(identF[:, :], pattern=[[1, 128]], base=0,
                                channel_multiplier=-1,
                                allow_small_or_imprecise_dtypes=True))
    b.op("po", lambda e: e.tensor_single_scalar(identF[:, :], identF[:, :], 0.0,
                                                ALU.is_equal))
    b.op("po", lambda e: e.tensor_copy(identB[:, :], identF[:, :]))
    b.op("po", lambda e: e.memset(onesCol[:, :], 1.0))
    b.op("po", lambda e: e.memset(ones1[:, :], 1.0))
    b.op("po", lambda e: e.iota(selF[:, :, :], pattern=[[1, 16], [0, 128]], base=0,
                                channel_multiplier=-1,
                                allow_small_or_imprecise_dtypes=True))
    b.op("po", lambda e: e.tensor_single_scalar(selF[:, :, :], selF[:, :, :], 0.0,
                                                ALU.is_equal))
    tok_sel = b.op("po", lambda e: e.tensor_copy(selAll[:, :, :], selF[:, :, :]))
    tb_state["setup"] = tok_sel

    b.dma("sp", smallRow[:, :], smalls)
    b.dma("sp", freqRow[:, :], rfreq)
    b.dma("sp", wdtCol[:, :], wdt_pm)
    b.dma("sp", bdtCol[:, :], bdt_pm)
    b.dma("sp", A_sb[:, :], a_pm)
    b.dma("sp", dColS[:, :], d_pm)
    b.dma("sp", cColT[:, :], c_pm)
    b.dma("sp", wxStg[:, :, :], w_x_pm)
    b.dma("sp", rtStg[:, :, :], router_pm)
    ld0 = ("ld", b.cnt["ld"])

    ta = b.op("act", lambda e: e.activation(A_sb[:, :], A_sb[:, :], AF.Exp),
              waits=[ld0])
    tok_A = b.op("dve", lambda e: e.tensor_scalar(A_sb[:, :], A_sb[:, :], 1e-4, -1.0,
                                                  ALU.max, ALU.mult), waits=[ta])
    b.op("act", lambda e: e.activation(sfRow[:, :], freqRow[:, :], AF.Abs),
         waits=[ld0])
    b.op("act", lambda e: e.activation(sfRow[:, :], sfRow[:, :], AF.Exp, scale=-1.0),
         ss=True)
    ta2 = b.op("act", lambda e: e.activation(sfRow[:, :], sfRow[:, :], AF.Ln, bias=1.0),
               ss=True)
    td2 = b.op("dve", lambda e: e.scalar_tensor_tensor(
        sfRow[:, :], freqRow[:, :], 0.0, sfRow[:, :], ALU.max, ALU.add), waits=[ta2])
    b.op("act", lambda e: e.copy(wx_bf[:, :, :], wxStg[:, :, :]), waits=[ld0],
         inc=False)
    b.op("act", lambda e: e.copy(router_bf[:, :, :], rtStg[:, :, :]))
    tsc = b.op("act", lambda e: e.activation(scCol[:, :], cColT[:, :], AF.Silu))

    wm_guard = [None, None]
    for nchunk in range(12):
        sl = nchunk % 2
        tlw = b.dma("sp", wmod_sb[sl][:, :, :],
                    wmod_pm[:, :, nchunk * 512:(nchunk + 1) * 512],
                    waits=[wm_guard[sl]], sem="wiS")

        def mk(nchunk=nchunk, sl=sl):
            def f(e):
                for kc in range(8):
                    r = e.matmul(pb[5][0:1, 0:512], scCol[:, kc:kc + 1],
                                 wmod_sb[sl][:, kc, :],
                                 start=(kc == 0), stop=(kc == 7))
                return r
            return f
        mt = b.pe_mm("pb5", [tsc, ld0, tlw], mk())
        wm_guard[sl] = mt
        b.evac("pb5", "act", lambda e, nchunk=nchunk: e.copy(
            modRow[0:1, nchunk * 512:(nchunk + 1) * 512], pb[5][0:1, 0:512]),
            waits=[mt])
    tmod = ("act", b.cnt["act"])

    tmodd = tmod
    prev_clamp = None
    for base in (2048, 5120):
        b.op("dve", lambda e, base=base: e.tensor_scalar(
            gtmpS[:, :], modRow[0:1, base:base + 1024], 0.0, None, ALU.is_ge),
            waits=[tmod])
        b.op("dve", lambda e: e.tensor_scalar(gtmpS[:, :], gtmpS[:, :], 2.0, -1.0,
                                              ALU.mult, ALU.add))
        tabs = b.op("act", lambda e, base=base: e.activation(
            gtmpA[:, :], modRow[0:1, base:base + 1024], AF.Abs),
            waits=[tmod, prev_clamp])
        b.op("dve", lambda e: e.tensor_scalar_max(gtmpA[:, :], gtmpA[:, :], 0.25),
             waits=[tabs])
        tmodd = b.op("dve", lambda e, base=base: e.tensor_tensor(
            modRow[0:1, base:base + 1024], gtmpS[:, :], gtmpA[:, :], ALU.mult))
        prev_clamp = tmodd

    for (dst, base, bias) in ((caRep, 1024, 1.0), (saRep, 0, 0.0), (gaRep, 2048, 0.0),
                              (cbRep, 4096, 1.0), (sbRep, 3072, 0.0),
                              (gbRep, 5120, 0.0)):
        for nchunk in range(2):
            mt = b.pe_mm("pb4", [tmodd], lambda e, base=base, nchunk=nchunk: e.matmul(
                pb[4][:, :], ones1[0:1, :],
                modRow[0:1, base + nchunk * 512: base + (nchunk + 1) * 512],
                start=True, stop=True))
            b.evac("pb4", "act", lambda e, dst=dst, nchunk=nchunk, bias=bias:
                   e.activation(dst[:, nchunk * 512:(nchunk + 1) * 512],
                                pb[4][:, :], AF.Copy, bias=bias), waits=[mt])

    for (dst, off) in ((bRepB, 0), (bRepC, 16), (wRepB, 32), (wRepC, 48)):
        mt = b.pe_mm("pb4", [ld0], lambda e, off=off: e.matmul(
            pb[4][:, 0:16], ones1[0:1, :], smallRow[0:1, off:off + 16],
            start=True, stop=True))
        b.evac("pb4", "act", lambda e, dst=dst: e.copy(dst[:, :], pb[4][:, 0:16]),
               waits=[mt])
    mt = b.pe_mm("pb4", [td2], lambda e: e.matmul(
        pb[4][:, 0:8], ones1[0:1, :], sfRow[0:1, :], start=True, stop=True))
    b.evac("pb4", "act", lambda e: e.copy(sfreqRep[:, :], pb[4][:, 0:8]), waits=[mt])

    setup_act = ("act", b.cnt["act"])
    es_setup.close()
    b.barrier()

    es_m = ExitStack()
    w_in_bf = b.sb("w_in_bf", [128, 8, 2048], BF16, es_m)
    w_out_bf = b.sb("w_out_bf", [128, 8, 1024], BF16, es_m)
    wstgM = b.sb("wstgM", [128, 8, 128], F32, es_m)
    x_t = [b.sb("x_t0", [128, D], F32, es_m)] * 2
    h_tm = b.sb("h_tm", [128, D], BF16, es_m)
    h_fm = b.sb("h_fm", [128, 8, CH], BF16, es_m)
    xp_bf = [b.sb(f"xp{p}", [128, 8, CH], BF16, es_m) for p in range(2)]
    sz_bf = [b.sb(f"sz{p}", [128, 8, CH], BF16, es_m) for p in range(2)]
    wc_bf = [b.sb(f"wc{p}", [128, 8, CH], BF16, es_m) for p in range(2)]
    w0_bf = [b.sb(f"w0{p}", [128, 8, CH], BF16, es_m) for p in range(2)]
    dtO = [b.sb(f"dtO{p}", [128, 8, CH], F32, es_m) for p in range(2)]
    dtX = b.sb("dtX", [128, CH], F32, es_m)
    yacc = b.sb("yacc", [128, 8, CH], F32, es_m)
    carryA = b.sb("carryA", [128, 128], F32, es_m)
    ldtLast = [b.sb(f"ldtL{p}", [128, 8], F32, es_m) for p in range(2)]
    ssm_a = b.sb("ssm_a", [34, CH], F32, es_m)
    ssm_b = b.sb("ssm_b", [34, CH], F32, es_m)
    ssm_f = b.sb("ssm_f", [34, CH], F32, es_m)
    ssmT = b.sb("ssmT", [128, NTT, 34], F32, es_m)
    bnB = b.sb("bnB", [128, 16], F32, es_m)
    bnC = b.sb("bnC", [128, 16], F32, es_m)
    sqS = b.sb("sqS", [128, 16], F32, es_m)
    colS = b.sb("colS", [128, 4], F32, es_m)
    BrTm = b.sb("BrTm", [128, 16], F32, es_m)
    CrTm = b.sb("CrTm", [128, 16], F32, es_m)
    ropeT = [b.sb(f"ropeT{i}", [128, 8], F32, es_m) for i in range(6)]
    ropeI = b.sb("ropeI", [128, 8], I32, es_m)
    BrTf = [b.sb(f"BrTf{p}", [16, CH], BF16, es_m) for p in range(2)]
    CrTf = [b.sb(f"CrTf{p}", [16, CH], BF16, es_m) for p in range(2)]
    qcol = b.sb("qcol", [128, 1], F32, es_m)
    qrow = b.sb("qrow", [1, CH], F32, es_m)
    qRep = [b.sb(f"qRep{p}", [128, CH], F32, es_m) for p in range(2)]
    dt_inRep = b.sb("dt_inRep", [128, CH], F32, es_m)
    lamRep = b.sb("lamRep", [128, CH], F32, es_m)
    olamRep = b.sb("olamRep", [128, CH], F32, es_m)
    lamrow = b.sb("lamrow", [1, CH], F32, es_m)
    dtirow = b.sb("dtirow", [1, CH], F32, es_m)
    mdtRow = b.sb("mdtRow", [1, CH], F32, es_m)
    cumRow = [b.sb(f"cumRow{p}", [1, CH], F32, es_m) for p in range(2)]
    onesRow = b.sb("onesRow", [1, CH], F32, es_m)
    ldt_t = b.sb("ldt_t", [128, CH], F32, es_m)
    p2sh_t = b.sb("p2sh_t", [128, CH], F32, es_m)
    s1_t = b.sb("s1_t", [128, CH], F32, es_m)
    ppB = b.sb("ppB", [128, 8], F32, es_m)
    BrB = b.sb("BrB", [128, CH], BF16, es_m)
    CrB2 = [b.sb(f"CrB{i}", [128, CH], BF16, es_m) for i in range(2)]
    dA_t = [b.sb(f"dA{i}", [128, CH], F32, es_m) for i in range(2)]
    u_t = [b.sb(f"u{i}", [128, CH], BF16, es_m) for i in range(2)]
    h_t = [b.sb(f"h{i}", [128, CH], BF16, es_m) for i in range(2)]
    tmp_t = [b.sb(f"tmp{i}", [128, CH], BF16, es_m) for i in range(2)]
    yp1 = b.sb("yp1", [128, CH], F32, es_m)
    yp2 = b.sb("yp2", [128, CH], F32, es_m)
    Y3 = b.sb("Y3", [128, 8, CH], BF16, es_m)
    attnP = b.sb("attnP", [128, 8, CH], BF16, es_m)
    attnS = b.sb("attnS", [128, 8, CH], BF16, es_m)
    attn_tm = b.sb("attn_tm", [128, D], BF16, es_m)
    xo_t = [b.sb("xo0", [128, D], F32, es_m)] * 2
    xn_t = b.sb("xn_t", [128, D], F32, es_m)
    snap = b.sb("snap", [128, 64], F32, es_m)
    snapT = b.sb("snapT", [128, D], BF16, es_m)
    snapX = b.sb("snapX", [128, D], F32, es_m)
    lnc = b.sb("lnc", [128, D], F32, es_m)
    lncol = b.sb("lncol", [128, 4], F32, es_m)
    lnsq = b.sb("lnsq", [128, D], F32, es_m)

    tcast = ("po", b.cnt["po"])
    for mo in range(16):
        tl = b.dma("sp", wstgM[:, :, :], w_in_pm[:, :, mo * 128:(mo + 1) * 128],
                   waits=[tcast], sem="wiS")
        tcast = b.op("po", lambda e, mo=mo: e.tensor_copy(
            w_in_bf[:, :, mo * 128:(mo + 1) * 128], wstgM[:, :, :]), waits=[tl])
    for mo in range(8):
        tl = b.dma("sp", wstgM[:, :, :], w_out_pm[:, :, mo * 128:(mo + 1) * 128],
                   waits=[tcast], sem="wiS")
        tcast = b.op("po", lambda e, mo=mo: e.tensor_copy(
            w_out_bf[:, :, mo * 128:(mo + 1) * 128], wstgM[:, :, :]), waits=[tl])
    w_cast_done = tcast

    b.op("dve", lambda e: e.memset(carryA[:, :], 0.0))
    b.op("po", lambda e: e.memset(onesRow[:, :], 1.0))

    gd = {
        "xp_free": [None, None], "sz_free": [None, None],
        "wc_free": [None, None], "w0_free": [None, None],
        "dtO_free": [None, None], "BrTf_free": [None, None],
        "qRep_free": [None, None], "hfm_pe": None, "Y3_free": None,
        "attnP_free": None, "attnS_free": None,
        "x_slot": [None, None], "xo_slot": [None, None],
        "scan_dve": [None, None], "ypost_dve": [None, None],
        "wc_patch": [None, None], "w0_patch": [None, None],
        "dt_tok": [None, None], "rope_tok": [None, None],
        "dA_guard": [None, None], "h_guard": [None, None],
        "tmp_guard": [None, None], "prev_u": None, "prev_y": None,
        "prev_pm": None, "tmm_prev": None,
        "scan_po": [None, None], "ys_last2": None,
        "ag_next": None, "rs_next": None, "lamrow_free": None,
        "ropePE": None, "rope_dve": None, "tsum_prev": None,
        "cum_pe": [None, None],
        "ln_sq_prev": None, "bn_prev": [None, None],
        "sz_tok": {}, "dA_read": [None, None],
    }

    def ln_adaln(xtile, crep, srep, out_bf, waits_x, extra=()):
        b.op("dve", lambda e: e.reduce_sum(lncol[:, 0:1], xtile[:, :], axis=AXX),
             waits=list(waits_x) + list(extra) + [gd["ln_sq_prev"]])
        b.op("dve", lambda e: e.tensor_scalar_mul(lncol[:, 1:2], lncol[:, 0:1],
                                                  1.0 / D), ss=True)
        tc = b.op("dve", lambda e: e.tensor_scalar(lnc[:, :], xtile[:, :],
                                                   lncol[:, 1:2], None, ALU.subtract),
                  ss=True)
        tq = b.op("act", lambda e: e.activation(lnsq[:, :], lnc[:, :], AF.Square,
                                                accum_out=lncol[:, 2:3]), waits=[tc])
        gd["ln_sq_prev"] = tq
        b.op("act", lambda e: e.activation(lncol[:, 3:4], lncol[:, 2:3], AF.Ln,
                                           scale=1.0 / D, bias=LN_EPS), ss=True)
        tr = b.op("act", lambda e: e.activation(lncol[:, 3:4], lncol[:, 3:4], AF.Exp,
                                                scale=-0.5), ss=True)
        b.op("dve", lambda e: e.tensor_scalar(lnc[:, :], lnc[:, :], lncol[:, 3:4],
                                              None, ALU.mult), waits=[tr])
        b.op("dve", lambda e: e.tensor_tensor(lnsq[:, :], lnc[:, :], crep[:, :],
                                              ALU.mult))
        return b.op("dve", lambda e: e.tensor_tensor(out_bf[:, :], lnsq[:, :],
                                                     srep[:, :], ALU.add))

    def front(c):
        par = c % 2
        for tt in range(NTT):
            sl = 0
            tl = b.dma("sp", x_t[sl][:, :],
                       xb[c * CH + tt * 128: c * CH + (tt + 1) * 128, :],
                       waits=[gd["x_slot"][sl], ld0], sem="xA")
            tln = ln_adaln(x_t[sl], caRep, saRep, h_tm, [tl],
                           extra=[("pe", b.cnt["pe"]), setup_act])
            gd["x_slot"][sl] = tln
            for dc in range(8):
                tbf(h_tm[:, dc * 128:(dc + 1) * 128],
                    h_fm[:, dc, tt * 128:(tt + 1) * 128], [tln])
            flush_tbf()
        th = None

        ev_xp = None
        ev = None
        for mo in range(16):
            bankname = "pb0" if mo % 2 == 0 else "pb1"
            bank = pb[0] if mo % 2 == 0 else pb[1]

            def mmf(e, mo=mo, bank=bank):
                for kc in range(8):
                    r = e.matmul(bank[:, 0:CH], w_in_bf[:, kc, mo * 128:(mo + 1) * 128],
                                 h_fm[:, kc, :], start=(kc == 0), stop=(kc == 7))
                return r
            mt = b.pe_mm(bankname, [tb_state["guard"][0], tb_state["guard"][1],
                                    w_cast_done], mmf)
            if mo < 8:
                dst, w8 = xp_bf[par], gd["xp_free"][par]
            else:
                dst, w8 = sz_bf[par], gd["sz_free"][par]
            ev = b.evac(bankname, "act", lambda e, dst=dst, mo=mo, bank=bank:
                        e.activation(dst[:, mo % 8, :], bank[:, 0:CH], AF.Silu),
                        waits=[mt, w8])
            if mo == 7:
                ev_xp = ev
        gd["sz_tok"][c] = ev
        if c == dbg_chunk:
            dbg("hfm", h_fm[:, :, :], ev)
        gd["hfm_pe"] = ("pe", b.cnt["pe"])

        def ssm_mm(e):
            for kc in range(8):
                r = e.matmul(pb[5][0:34, 0:CH], wx_bf[:, kc, :], xp_bf[par][:, kc, :],
                             start=(kc == 0), stop=(kc == 7))
            return r
        mt = b.pe_mm("pb5", [ev_xp], ssm_mm)
        evs = b.evac("pb5", "act", lambda e: e.copy(ssm_a[:, :], pb[5][0:34, 0:CH]),
                     waits=[mt, gd["tsum_prev"]])
        tst = b.dma("po", ag_in, ssm_a[:, :], waits=[evs, gd["ag_next"]], sem="st")
        tcc = b.op("po", lambda e: e.collective_compute(
            "AllGather", ALU.bypass, replica_groups=PAIRS, ins=[ag_in], outs=[ag_out]),
            waits=[("st", b.cnt["st"]), ("agl", b.cnt["agl"])], inc=("cc", 1))
        gd["ag_next"] = tcc
        tla = b.dma("sp", ssm_a[:, :], ag_out[0:34, :], waits=[tcc], sem="agl")
        tlb = b.dma("sp", ssm_b[:, :], ag_out[34:68, :], sem="agl")
        tsum = b.op("dve", lambda e: e.tensor_tensor(ssm_f[:, :], ssm_a[:, :],
                                                     ssm_b[:, :], ALU.add),
                    waits=[tla, tlb])
        gd["tsum_prev"] = tsum

        if c == dbg_chunk:
            dbg("ssm_f", ssm_f[:, :], tsum)
        evT = []
        for tt in range(NTT):
            evT.append(tf32(ssm_f[0:34, tt * 128:(tt + 1) * 128],
                            ssmT[:, tt, 0:34], [tsum, gd["rope_dve"]], 34, 128))

        tdr = b.dma("sp", dtirow[:, :], ssm_f[32:33, :],
                    waits=[tsum, gd["lamrow_free"]], sem="lrw")
        mt = b.pe_mm("pb4", [tdr], lambda e: e.matmul(
            pb[4][:, 0:CH], ones1[0:1, :], dtirow[0:1, :], start=True, stop=True))
        tdir = b.evac("pb4", "act", lambda e: e.copy(dt_inRep[:, :], pb[4][:, 0:CH]),
                      waits=[mt, gd["scan_dve"][par]])
        trow = b.dma("sp", lamrow[:, :], ssm_f[33:34, :],
                     waits=[tsum, gd["lamrow_free"]], sem="lrw")
        tsg = b.op("act", lambda e: e.activation(lamrow[:, :], lamrow[:, :],
                                                 AF.Sigmoid), waits=[trow])
        mt = b.pe_mm("pb4", [tsg], lambda e: e.matmul(
            pb[4][:, 0:CH], ones1[0:1, :], lamrow[0:1, :], start=True, stop=True))
        gd["lamrow_free"] = ("pe", b.cnt["pe"])
        tlr = b.evac("pb4", "act", lambda e: e.copy(lamRep[:, :], pb[4][:, 0:CH]),
                     waits=[mt, gd["scan_dve"][par]])
        tol = b.op("dve", lambda e: e.tensor_scalar(olamRep[:, :], lamRep[:, :],
                                                    -1.0, 1.0, ALU.mult, ALU.add),
                   waits=[tlr])

        last = None
        for etf in range(16):
            own = etf < 8
            ta1 = b.op("dve", lambda e, etf=etf: e.tensor_scalar(
                dtX[:, :], dt_inRep[:, :], wdtCol[:, etf:etf + 1],
                bdtCol[:, etf:etf + 1], ALU.mult, ALU.add),
                waits=[tdir, ld0, gd["tmm_prev"],
                       gd["dtO_free"][par] if own else None])
            tab = b.op("act", lambda e: e.activation(lnsq[:, 0:CH], dtX[:, :], AF.Abs),
                       waits=[ta1])
            b.op("act", lambda e: e.activation(lnsq[:, 0:CH], lnsq[:, 0:CH], AF.Exp,
                                               scale=-1.0))
            tal = b.op("act", lambda e: e.activation(lnsq[:, 0:CH], lnsq[:, 0:CH],
                                                     AF.Ln, bias=1.0))
            if own:
                tdt = b.op("dve", lambda e, etf=etf: e.scalar_tensor_tensor(
                    dtO[par][:, etf, :], dtX[:, :], 0.0, lnsq[:, 0:CH],
                    ALU.max, ALU.add), waits=[tal, gd["dA_read"][par]])
                src = dtO[par][:, etf, :]
            else:
                tdt = b.op("dve", lambda e: e.scalar_tensor_tensor(
                    dtX[:, :], dtX[:, :], 0.0, lnsq[:, 0:CH], ALU.max, ALU.add),
                    waits=[tal])
                src = dtX[:, :]
            gd["tmm_prev"] = b.pe_mm("pb3", [tdt], lambda e, src=src, etf=etf: e.matmul(
                pb[3][0:1, 0:CH], onesCol[:, 0:1], src,
                start=(etf == 0), stop=(etf == 15)))
            if own:
                tld = b.op("dve", lambda e, etf=etf: e.tensor_tensor(
                    ldt_t[:, :], dtO[par][:, etf, :], lamRep[:, :], ALU.mult),
                    waits=[tol])
                if c == 0:
                    b.op("dve", lambda e, etf=etf: e.tensor_copy(
                        ldt_t[:, 0:1], dtO[par][:, etf, 0:1]))
                b.op("dve", lambda e, etf=etf: e.tensor_tensor(
                    p2sh_t[:, 0:CH - 1], dtO[par][:, etf, 1:CH], olamRep[:, 1:CH],
                    ALU.mult))
                b.op("dve", lambda e: e.tensor_tensor(
                    s1_t[:, 0:CH - 1], ldt_t[:, 0:CH - 1], p2sh_t[:, 0:CH - 1],
                    ALU.add))
                b.op("dve", lambda e, etf=etf: e.tensor_tensor(
                    wc_bf[par][:, etf, 0:CH - 1], xp_bf[par][:, etf, 0:CH - 1],
                    s1_t[:, 0:CH - 1], ALU.mult),
                    waits=[ev_xp, gd["wc_free"][par], gd["scan_po"][par]])
                b.op("dve", lambda e, etf=etf: e.tensor_tensor(
                    w0_bf[par][:, etf, 0:CH - 1], xp_bf[par][:, etf, 0:CH - 1],
                    p2sh_t[:, 0:CH - 1], ALU.mult),
                    waits=[gd["w0_free"][par]])
                last = b.op("dve", lambda e, etf=etf: e.tensor_copy(
                    ldtLast[par][:, etf:etf + 1], ldt_t[:, CH - 1:CH]), ss=True)
        gd["dt_tok"][par] = last
        if c == dbg_chunk:
            dbg("dt_inRep", dt_inRep[:, :], tdir)
            dbg("lamRep", lamRep[:, :], tlr)
            dbg("dtO", dtO[par][:, :, :], last)
            dbg("xp", xp_bf[par][:, :, :], ev_xp)
        tmean = ("pe", b.cnt["pe"])
        tmd = b.evac("pb3", "act", lambda e: e.activation(
            mdtRow[0:1, :], pb[3][0:1, 0:CH], AF.Copy, scale=1.0 / 2048.0),
            waits=[tmean, gd["scan_dve"][par]])

        init = 0.0 if c == 0 else cumRow[1 - par][0:1, CH - 1:CH]
        tcum = b.op("dve", lambda e, init=init: e.tensor_tensor_scan(
            cumRow[par][0:1, :], onesRow[0:1, :], mdtRow[0:1, :], init,
            ALU.mult, ALU.add), waits=[tmd, gd["cum_pe"][par]])

        rope_evs = []
        for tt in range(NTT):
            cix = 0 if tt == 0 else 3
            tcc2 = tf32(cumRow[par][0:1, tt * 128:(tt + 1) * 128],
                        colS[:, cix:cix + 1], [tcum], 1, 128)
            tb1 = b.op("dve", lambda e, tt=tt: e.tensor_tensor(
                bnB[:, :], ssmT[:, tt, 0:16], bRepB[:, :], ALU.add),
                waits=[evT[tt], gd["bn_prev"][0]], ss=True)
            tb2 = b.op("act", lambda e: e.activation(sqS[:, :], bnB[:, :], AF.Square,
                                                     accum_out=colS[:, 1:2]),
                       waits=[tb1])
            b.op("act", lambda e: e.activation(colS[:, 1:2], colS[:, 1:2], AF.Ln,
                                               scale=1.0 / 16, bias=RMS_EPS),
                 ss=True)
            tb3 = b.op("act", lambda e: e.activation(colS[:, 1:2], colS[:, 1:2],
                                                     AF.Exp, scale=-0.5), ss=True)
            tb4 = b.op("dve", lambda e: e.scalar_tensor_tensor(
                bnB[:, :], bnB[:, :], colS[:, 1:2], wRepB[:, :], ALU.mult, ALU.mult),
                waits=[tb3], ss=True)
            gd["bn_prev"][0] = tb2
            tc1 = b.op("dve", lambda e, tt=tt: e.tensor_tensor(
                bnC[:, :], ssmT[:, tt, 16:32], bRepC[:, :], ALU.add),
                waits=[gd["bn_prev"][1]], ss=True)
            tc2 = b.op("act", lambda e: e.activation(sqS[:, :], bnC[:, :], AF.Square,
                                                     accum_out=colS[:, 2:3]),
                       waits=[tc1])
            b.op("act", lambda e: e.activation(colS[:, 2:3], colS[:, 2:3], AF.Ln,
                                               scale=1.0 / 16, bias=RMS_EPS),
                 ss=True)
            tc3 = b.op("act", lambda e: e.activation(colS[:, 2:3], colS[:, 2:3],
                                                     AF.Exp, scale=-0.5), ss=True)
            tc4 = b.op("dve", lambda e: e.scalar_tensor_tensor(
                bnC[:, :], bnC[:, :], colS[:, 2:3], wRepC[:, :], ALU.mult, ALU.mult),
                waits=[tc3], ss=True)
            gd["bn_prev"][1] = tc2

            tan = b.op("dve", lambda e, cix=cix: e.tensor_scalar(
                ropeT[0][:, :], sfreqRep[:, :], colS[:, cix:cix + 1], None, ALU.mult),
                waits=[tcc2], ss=True)
            b.op("dve", lambda e: e.tensor_scalar_mul(ropeT[1][:, :], ropeT[0][:, :],
                                                      INV_2PI), ss=True)
            b.op("dve", lambda e: e.tensor_scalar_add(ropeT[1][:, :], ropeT[1][:, :],
                                                      8388608.0), ss=True)
            b.op("dve", lambda e: e.tensor_scalar_sub(ropeT[1][:, :], ropeT[1][:, :],
                                                      8388608.0), ss=True)
            ty = b.op("dve", lambda e: e.scalar_tensor_tensor(
                ropeT[2][:, :], ropeT[1][:, :], -TWO_PI, ropeT[0][:, :],
                ALU.mult, ALU.add), ss=True)
            if dbg_on and c == dbg_chunk and tt == 0:
                b.op("act", lambda e, cix=cix: e.copy(snap[:, 0:1], colS[:, cix:cix + 1]),
                     waits=[tan])
                b.op("act", lambda e: e.copy(snap[:, 8:16], ropeT[0][:, :]),
                     waits=[tan])
                b.op("act", lambda e: e.copy(snap[:, 16:24], ropeT[1][:, :]),
                     waits=[ty])
                b.op("act", lambda e: e.copy(snap[:, 24:32], ropeT[2][:, :]),
                     waits=[ty])
            tsin = b.op("act", lambda e: e.activation(ropeT[3][:, :], ropeT[2][:, :],
                                                      AF.Sin), waits=[ty])
            b.op("dve", lambda e: e.tensor_scalar_add(ropeT[4][:, :], ropeT[2][:, :],
                                                      PI / 2), ss=True)
            b.op("dve", lambda e: e.tensor_single_scalar(
                ropeT[5][:, :], ropeT[4][:, :], PI, ALU.is_gt), ss=True)
            tyc = b.op("dve", lambda e: e.scalar_tensor_tensor(
                ropeT[4][:, :], ropeT[5][:, :], -TWO_PI, ropeT[4][:, :],
                ALU.mult, ALU.add), ss=True)
            tcos = b.op("act", lambda e: e.activation(ropeT[5][:, :], ropeT[4][:, :],
                                                      AF.Sin), waits=[tyc])
            if dbg_on and c == dbg_chunk and tt == 0:
                tsnap = b.op("act", lambda e: e.copy(snap[:, 32:40], ropeT[3][:, :]))
                dbg("snap", snap[:, :], tsnap)

            for (srcv, dstv, tw) in ((bnB, BrTm, tb4), (bnC, CrTm, tc4)):
                b.op("dve", lambda e, srcv=srcv: e.tensor_tensor(
                    ropeT[0][:, :], srcv[:, 0:8], ropeT[5][:, :], ALU.mult),
                    waits=[tcos, tsin, tw, gd["ropePE"]], ss=True)
                b.op("dve", lambda e, srcv=srcv: e.tensor_tensor(
                    ropeT[1][:, :], srcv[:, 8:16], ropeT[3][:, :], ALU.mult),
                    ss=True)
                b.op("dve", lambda e, dstv=dstv: e.tensor_tensor(
                    dstv[:, 0:8], ropeT[0][:, :], ropeT[1][:, :], ALU.subtract),
                    ss=True)
                b.op("dve", lambda e, srcv=srcv: e.tensor_tensor(
                    ropeT[0][:, :], srcv[:, 0:8], ropeT[3][:, :], ALU.mult),
                    ss=True)
                b.op("dve", lambda e, srcv=srcv: e.tensor_tensor(
                    ropeT[1][:, :], srcv[:, 8:16], ropeT[5][:, :], ALU.mult),
                    ss=True)
                b.op("dve", lambda e, dstv=dstv: e.tensor_tensor(
                    dstv[:, 8:16], ropeT[0][:, :], ropeT[1][:, :], ALU.add),
                    ss=True)
            b.op("dve", lambda e: e.tensor_tensor(sqS[:, :], BrTm[:, :], CrTm[:, :],
                                                  ALU.mult), ss=True)
            tq2 = b.op("dve", lambda e: e.reduce_sum(qcol[:, 0:1], sqS[:, :],
                                                     axis=AXX), ss=True)
            if dbg_on and c == dbg_chunk and tt == 0:
                tsb = b.op("dve", lambda e: e.tensor_copy(snap[:, 40:56], BrTm[:, :]))
                b.op("act", lambda e: e.copy(snap[:, 56:64], ropeT[5][:, :]),
                     waits=[tsb])
                tsb2 = b.op("act", lambda e: e.copy(snap[:, 4:5], qcol[:, 0:1]), waits=[tsb])
                dbg("snapBr", snap[:, :], tsb2)
            if c == dbg_chunk and tt == 0:
                dbg("sfreqRep", sfreqRep[:, :], tq2)
                dbg("angT", ropeT[2][:, :], ty)
                dbg("BrTm", BrTm[:, :], tq2)
                dbg("CrTm", CrTm[:, :], tq2)
                dbg("bnB", bnB[:, :], tq2)
                dbg("sinT", ropeT[3][:, :], tq2)
                dbg("cosT", ropeT[5][:, :], tq2)
                dbg("qcolD", qcol[:, :], tq2)
            rope_evs.append(tf32(BrTm[:, :], BrTf[par][0:16, tt * 128:(tt + 1) * 128],
                                 [tq2, gd["BrTf_free"][par]], 128, 16))
            rope_evs.append(tf32(CrTm[:, :], CrTf[par][0:16, tt * 128:(tt + 1) * 128],
                                 [tq2], 128, 16))
            rope_evs.append(tf32(qcol[:, 0:1], qrow[0:1, tt * 128:(tt + 1) * 128],
                                 [tq2], 128, 1))
            gd["ropePE"] = ("pe", b.cnt["pe"])
        gd["rope_dve"] = ("dve", b.cnt["dve"])
        gd["cum_pe"][par] = ("pe", b.cnt["pe"])
        mt = b.pe_mm("pb4", rope_evs, lambda e: e.matmul(
            pb[4][:, 0:CH], ones1[0:1, :], qrow[0:1, :], start=True, stop=True))
        tqr = b.evac("pb4", "act", lambda e: e.copy(qRep[par][:, :], pb[4][:, 0:CH]),
                     waits=[mt, gd["qRep_free"][par]])
        gd["rope_tok"][par] = tqr
        if c == dbg_chunk:
            dbg("cumRow", cumRow[par][:, :], tcum)
            dbg("BrTf", BrTf[par][:, :], tqr)
            dbg("CrTf", CrTf[par][:, :], tqr)
            dbg("qRep", qRep[par][:, :], tqr)

    def patch(cprev, clast=False):
        par = cprev % 2
        npar = 1 - par
        for et in range(8):
            if not clast:
                b.op("dve", lambda e, et=et: e.tensor_tensor(
                    ppB[:, et:et + 1], dtO[npar][:, et, 0:1], olamRep[:, 0:1],
                    ALU.mult), waits=[gd["dt_tok"][npar]], ss=True)
                b.op("dve", lambda e, et=et: e.tensor_tensor(
                    w0_bf[par][:, et, CH - 1:CH], xp_bf[par][:, et, CH - 1:CH],
                    ppB[:, et:et + 1], ALU.mult),
                    waits=[gd["scan_po"][par]], ss=True)
                b.op("dve", lambda e, et=et: e.tensor_tensor(
                    ppB[:, et:et + 1], ldtLast[par][:, et:et + 1], ppB[:, et:et + 1],
                    ALU.add), ss=True)
                b.op("dve", lambda e, et=et: e.tensor_tensor(
                    wc_bf[par][:, et, CH - 1:CH], xp_bf[par][:, et, CH - 1:CH],
                    ppB[:, et:et + 1], ALU.mult),
                    waits=[gd["scan_po"][par]], ss=True)
            else:
                b.op("dve", lambda e, et=et: e.tensor_tensor(
                    wc_bf[par][:, et, CH - 1:CH], xp_bf[par][:, et, CH - 1:CH],
                    ldtLast[par][:, et:et + 1], ALU.mult),
                    waits=[gd["scan_po"][par]], ss=True)
                b.op("dve", lambda e, et=et: e.memset(
                    w0_bf[par][:, et, CH - 1:CH], 0.0),
                    waits=[gd["scan_po"][par]], ss=True)
        tok = b.op("dve", lambda e: e.engine_nop())
        gd["wc_patch"][par] = tok
        gd["w0_patch"][par] = tok

    def scan(c):
        par = c % 2
        dA_guard = gd["dA_guard"]
        h_guard = gd["h_guard"]
        tmp_guard = gd["tmp_guard"]
        prev_u = gd["prev_u"]
        ys_tok = {}
        prev2 = gd["ys_last2"]
        crb_tok = [None, None]
        pend = None

        def cr_toks(s):
            yy = (prev2[s] if (s < 2 and prev2) else
                  ys_tok.get(s - 2))
            return (yy[0], yy[1]) if yy else (None, None)

        def emit_y(p):
            s, et, sl, tsc2 = p
            cb = CrB2[s % 2]
            if s == 0:
                ty = b.op("dve", lambda e, et=et, sl=sl, cb=cb: e.tensor_tensor(
                    yacc[:, et, :], h_t[sl][:, :], cb[:, :], ALU.mult),
                    waits=[crb_tok[s % 2], gd["ypost_dve"][par]])
                h_guard[sl] = ty
                ys_tok[s] = [ty, None]
            else:
                tm = b.op("po", lambda e, sl=sl, cb=cb: e.tensor_tensor(
                    tmp_t[sl][:, :], h_t[sl][:, :], cb[:, :], ALU.mult),
                    waits=[tsc2, crb_tok[s % 2], tmp_guard[sl]])
                h_guard[sl] = tm
                ty = b.op("dve", lambda e, et=et, sl=sl: e.tensor_tensor(
                    yacc[:, et, :], yacc[:, et, :], tmp_t[sl][:, :], ALU.add),
                    waits=[tm])
                tmp_guard[sl] = ty
                ys_tok[s] = [ty, tm]
            b.op("act", lambda e, sl=sl, col=s * 8 + et: e.copy(
                carryA[:, col:col + 1], h_t[sl][:, CH - 1:CH]), waits=[tsc2])

        for s in range(S):
            mtb = b.pe_mm("pb4", [gd["rope_tok"][par], prev_u], lambda e, s=s:
                          e.matmul(pb[4][:, 0:CH], selAll[:, s, :], BrTf[par][:, :],
                                   start=True, stop=True))
            tBrB = b.evac("pb4", "act", lambda e: e.copy(BrB[:, :], pb[4][:, 0:CH]),
                          waits=[mtb, prev_u])
            y2d, y2p = cr_toks(s)
            mtc = b.pe_mm("pb2", [gd["rope_tok"][par], y2d, y2p],
                          lambda e, s=s: e.matmul(pb[2][:, 0:CH], selAll[:, s, :],
                                                  CrTf[par][:, :],
                                                  start=True, stop=True))
            tCrB = b.evac("pb2", "act", lambda e, s=s: e.copy(
                CrB2[s % 2][:, :], pb[2][:, 0:CH]), waits=[mtc, y2d, y2p])
            crb_tok[s % 2] = tCrB
            for et in range(8):
                it = s * 8 + et
                sl = it % 2
                tda = b.op("act", lambda e, s=s, et=et, sl=sl: e.activation(
                    dA_t[sl][:, :], dtO[par][:, et, :], AF.Exp,
                    scale=A_sb[:, et * 16 + s: et * 16 + s + 1]),
                    waits=[dA_guard[sl], gd["dt_tok"][par], tok_A])
                tu = b.op("po", lambda e, et=et, sl=sl: e.tensor_tensor(
                    u_t[sl][:, :], wc_bf[par][:, et, :], BrB[:, :], ALU.mult),
                    waits=[tBrB, gd["wc_patch"][par], dA_guard[sl]])
                tsc2 = b.op("dve", lambda e, sl=sl, col=it: e.tensor_tensor_scan(
                    h_t[sl][:, :], dA_t[sl][:, :], u_t[sl][:, :],
                    carryA[:, col:col + 1], ALU.mult, ALU.add),
                    waits=[tda, tu, h_guard[sl]])
                dA_guard[sl] = tsc2
                if pend is not None:
                    emit_y(pend)
                pend = (s, et, sl, tsc2)
                if et == 7:
                    prev_u = tu
        emit_y(pend)
        gd["prev_u"] = prev_u
        gd["ys_last2"] = [ys_tok[S - 2], ys_tok[S - 1]]
        gd["dA_read"][par] = ("act", b.cnt["act"])
        gd["scan_dve"][par] = ("dve", b.cnt["dve"])
        gd["scan_po"][par] = ("po", b.cnt["po"])
        if c == dbg_chunk:
            dbg("yacc", yacc[:, :, :], ("dve", b.cnt["dve"]))
            dbg("wc", wc_bf[par][:, :, :], gd["wc_patch"][par])
            dbg("w0", w0_bf[par][:, :, :], gd["w0_patch"][par])
            dbg("carry", carryA[:, :], ("act", b.cnt["act"]))

    def ypost_wout(c):
        par = c % 2
        for et in range(8):
            b.op("dve", lambda e, et=et: e.tensor_tensor(
                yp1[:, :], w0_bf[par][:, et, :], qRep[par][:, :], ALU.mult),
                waits=[gd["w0_patch"][par], gd["rope_tok"][par], ld0])
            b.op("dve", lambda e, et=et: e.tensor_tensor(
                yp2[:, :], yacc[:, et, :], yp1[:, :], ALU.subtract))
            b.op("dve", lambda e, et=et: e.scalar_tensor_tensor(
                yp1[:, :], xp_bf[par][:, et, :], dColS[:, et:et + 1], yp2[:, :],
                ALU.mult, ALU.add))
            b.op("dve", lambda e, et=et: e.tensor_tensor(
                Y3[:, et, :], yp1[:, :], sz_bf[par][:, et, :], ALU.mult),
                waits=[gd["sz_tok"].get(c), gd["Y3_free"]])
        typ = b.op("dve", lambda e: e.engine_nop())
        gd["ypost_dve"][par] = typ
        gd["xp_free"][par] = typ
        gd["sz_free"][par] = typ
        gd["wc_free"][par] = typ
        gd["w0_free"][par] = typ
        gd["dtO_free"][par] = typ
        gd["qRep_free"][par] = typ
        gd["sz_tok"].pop(c, None)

        evs = []
        for mo in range(8):
            bankname = "pb2" if mo % 2 == 0 else "pb3"
            bank = pb[2] if mo % 2 == 0 else pb[3]

            def mmf(e, mo=mo, bank=bank):
                for kc in range(8):
                    r = e.matmul(bank[:, 0:CH], w_out_bf[:, kc, mo * 128:(mo + 1) * 128],
                                 Y3[:, kc, :], start=(kc == 0), stop=(kc == 7))
                return r
            mt = b.pe_mm(bankname, [typ, w_cast_done], mmf)
            ev = b.evac(bankname, "act", lambda e, mo=mo, bank=bank: e.copy(
                attnP[:, mo, :], bank[:, 0:CH]), waits=[mt, gd["attnP_free"]])
            evs.append(ev)
        gd["Y3_free"] = ("pe", b.cnt["pe"])
        gd["BrTf_free"][par] = ("pe", b.cnt["pe"])
        if c == dbg_chunk:
            dbg("Y3", Y3[:, :, :], gd["ypost_dve"][par])
            dbg("attnP", attnP[:, :, :], evs[-1])
            dbg("szD", sz_bf[par][:, :, :], gd["ypost_dve"][par])
        tst = b.dma("po", rs_in[(c % 2) * 128:(c % 2) * 128 + 128, :],
                    attnP.ap().rearrange("p a b -> p (a b)"),
                    waits=evs + [gd["rs_next"]], sem="st")
        gd["attnP_free"] = tst

    def rs_xnew(c):
        oc = c // 2
        tcc = b.op("po", lambda e: e.collective_compute(
            "ReduceScatter", ALU.add, replica_groups=PAIRS,
            ins=[rs_in], outs=[rs_out]),
            waits=[("st", b.cnt["st"]), ("rsl", b.cnt["rsl"])], inc=("cc", 1))
        gd["rs_next"] = tcc
        tlb = b.dma("sp", attnS.ap().rearrange("p a b -> p (a b)"), rs_out,
                    waits=[tcc, gd["attnS_free"]], sem="rsl")
        if oc == 0:
            dbg("attnS", attnS[:, :, :], tlb)
        prev_store = None
        for tt in range(NTT):
            evl = []
            for dc in range(8):
                ev = tbf(attnS[:, dc, tt * 128:(tt + 1) * 128],
                         attn_tm[:, dc * 128:(dc + 1) * 128], [tlb])
                if ev is not None:
                    evl.append(ev)
            ev = flush_tbf()
            if ev is not None:
                evl.append(ev)
            sl = 0
            tl = b.dma("sp", xo_t[sl][:, :],
                       x_own[oc * CH + tt * 128: oc * CH + (tt + 1) * 128, :],
                       waits=[gd["xo_slot"][sl]], sem="xB")
            b.op("dve", lambda e: e.tensor_tensor(xn_t[:, :], attn_tm[:, :],
                                                  gaRep[:, :], ALU.mult),
                 waits=evl + [tl, prev_store, setup_act])
            txn = b.op("dve", lambda e, sl=sl: e.tensor_tensor(
                xn_t[:, :], xo_t[sl][:, :], xn_t[:, :], ALU.add))
            if dbg_on and oc == 0 and tt == 0:
                dbg("gaRep", gaRep[:, :], txn)
                dbg("saRep", saRep[:, :], txn)
                dbg("caRep", caRep[:, :], txn)
                t1_ = b.op("dve", lambda e: e.tensor_copy(snapT[:, :], attn_tm[:, :]))
                t2_ = b.op("dve", lambda e, sl=sl: e.tensor_copy(snapX[:, :], xo_t[sl][:, :]))
                dbg("attn_tm0", snapT[:, :], t1_)
                dbg("xo0", snapX[:, :], t2_)
            gd["xo_slot"][sl] = txn
            prev_store = b.dma("po", xnew_d[oc * CH + tt * 128: oc * CH + (tt + 1) * 128, :],
                               xn_t[:, :], waits=[txn], sem="st")
        gd["attnS_free"] = ("pe", b.cnt["pe"])

    if n_chunks > 0:
        front(0)
    for c in range(n_chunks):
        if c + 1 < n_chunks:
            front(c + 1)
            patch(c)
        else:
            patch(c, clast=True)
        if not do_scan:
            continue
        scan(c)
        ypost_wout(c)
        if c % 2 == 1:
            rs_xnew(c)

    mamba_done = {e: (e, b.cnt[e]) for e in ("pe", "act", "dve", "po")}
    mamba_st = ("st", b.cnt["st"])
    es_m.close()
    b.barrier()

    es_f = ExitStack()
    fx_t = [b.sb(f"fx{i}", [128, D], F32, es_f) for i in range(2)]
    fh_tm = b.sb("fh_tm", [128, D], BF16, es_f)
    h2_fm = b.sb("h2_fm", [128, 8, FTOK], BF16, es_f)
    flnc = b.sb("flnc", [128, D], F32, es_f)
    flncol = b.sb("flncol", [128, 4], F32, es_f)
    flnsq = b.sb("flnsq", [128, D], F32, es_f)
    logitsF = b.sb("logitsF", [8, FTOK], F32, es_f)
    lg_t = b.sb("lg_t", [128, 8], F32, es_f)
    top_t = [b.sb(f"top{i}", [128, 8], F32, es_f) for i in range(2)]
    topc = b.sb("topc", [128, 4], F32, es_f)
    combA = b.sb("combA", [128, 8, 8], F32, es_f)
    combT = b.sb("combT", [8, FTOK], BF16, es_f)
    cRep = b.sb("cRep", [128, FTOK], F32, es_f)
    wgStg = [b.sb(f"wgS{i}", [128, 8, 128], F32, es_f) for i in range(2)]
    wuStg = [b.sb(f"wuS{i}", [128, 8, 128], F32, es_f) for i in range(2)]
    wgB = [b.sb(f"wgB{i}", [128, 8, 128], BF16, es_f) for i in range(2)]
    wuB = [b.sb(f"wuB{i}", [128, 8, 128], BF16, es_f) for i in range(2)]
    wdStg = [b.sb(f"wdS{i}", [128, HM, 128], F32, es_f) for i in range(2)]
    wdB = [b.sb("wdB0", [128, HM, 128], BF16, es_f)] * 2
    sg_t = [b.sb(f"sg{i}", [128, 512], BF16, es_f) for i in range(2)]
    hgAll = b.sb("hgAll", [128, HM, FTOK], BF16, es_f)
    acc = b.sb("acc", [128, 8, FTOK], F32, es_f)
    dtmp = [b.sb(f"dtmp{i}", [128, 512], F32, es_f) for i in range(2)]
    acc_tm = b.sb("acc_tm", [128, D], F32, es_f)
    fo_t = b.sb("fo_t", [128, D], F32, es_f)

    fgd = {"fx": [None, None], "h2fm_pe": None, "stg": {}, "acc_free": None,
           "hg_pe": None, "sg_guard": [None, None], "dtmp_guard": [None, None],
           "fsq_prev": None, "fo_store": None, "crep_dve": None, "acc_dve": None}

    def ffn_ln(xtile, out_bf, waits_x):
        b.op("dve", lambda e: e.reduce_sum(flncol[:, 0:1], xtile[:, :], axis=AXX),
             waits=list(waits_x) + [fgd["fsq_prev"]])
        b.op("dve", lambda e: e.tensor_scalar_mul(flncol[:, 1:2], flncol[:, 0:1],
                                                  1.0 / D), ss=True)
        tc = b.op("dve", lambda e: e.tensor_scalar(flnc[:, :], xtile[:, :],
                                                   flncol[:, 1:2], None, ALU.subtract),
                  ss=True)
        tq = b.op("act", lambda e: e.activation(flnsq[:, :], flnc[:, :], AF.Square,
                                                accum_out=flncol[:, 2:3]), waits=[tc])
        fgd["fsq_prev"] = tq
        b.op("act", lambda e: e.activation(flncol[:, 3:4], flncol[:, 2:3], AF.Ln,
                                           scale=1.0 / D, bias=LN_EPS), ss=True)
        tr = b.op("act", lambda e: e.activation(flncol[:, 3:4], flncol[:, 3:4],
                                                AF.Exp, scale=-0.5), ss=True)
        b.op("dve", lambda e: e.tensor_scalar(flnc[:, :], flnc[:, :], flncol[:, 3:4],
                                              None, ALU.mult), waits=[tr])
        b.op("dve", lambda e: e.tensor_tensor(flnsq[:, :], flnc[:, :], cbRep[:, :],
                                              ALU.mult))
        return b.op("dve", lambda e: e.tensor_tensor(out_bf[:, :], flnsq[:, :],
                                                     sbRep[:, :], ALU.add))

    for fb in range(min(2 if do_ffn else 0, int(os.environ.get('MK_NFB', 2)))):
        base_tok = fb * FTOK
        th2 = None
        for tt in range(8):
            sl = tt % 2
            sname = "xnA" if sl == 0 else "xnB"
            extra = [mamba_done["dve"], mamba_done["po"], mamba_done["act"],
                     mamba_done["pe"], mamba_st] if (fb == 0 and tt < 2) else []
            tl = b.dma("sp", fx_t[sl][:, :],
                       xnew_d[base_tok + tt * 128: base_tok + (tt + 1) * 128, :],
                       waits=[fgd["fx"][sl]] + extra, sem=sname)
            tln = ffn_ln(fx_t[sl], fh_tm, [tl, ("pe", b.cnt["pe"])])
            fgd["fx"][sl] = tln
            for dc in range(8):
                tbf(fh_tm[:, dc * 128:(dc + 1) * 128],
                    h2_fm[:, dc, tt * 128:(tt + 1) * 128],
                    [tln, fgd["h2fm_pe"]])
            th2 = flush_tbf()
        th2 = None

        lg_evs = []
        for nt in range(FNT):
            def rmm(e, nt=nt):
                for kc in range(8):
                    r = e.matmul(pb[5][0:8, 0:512], router_bf[:, kc, :],
                                 h2_fm[:, kc, nt * 512:(nt + 1) * 512],
                                 start=(kc == 0), stop=(kc == 7))
                return r
            mt = b.pe_mm("pb5", [tb_state["guard"][0], tb_state["guard"][1]], rmm)
            lg_evs.append(b.evac("pb5", "act", lambda e, nt=nt: e.copy(
                logitsF[0:8, nt * 512:(nt + 1) * 512], pb[5][0:8, 0:512]),
                waits=[mt]))
        comb_evs = []
        for tt in range(8):
            tT = tf32(logitsF[0:8, tt * 128:(tt + 1) * 128], lg_t[:, 0:8],
                      lg_evs + comb_evs, 8, 128, eng="dve")
            b.op("dve", lambda e: e.reduce_max(topc[:, 0:1], lg_t[:, :], axis=AXX),
                 waits=[tT], ss=True)
            b.op("dve", lambda e: e.tensor_scalar(top_t[0][:, :], lg_t[:, :],
                                                  topc[:, 0:1], None, ALU.is_equal),
                 ss=True)
            b.op("dve", lambda e: e.scalar_tensor_tensor(
                top_t[1][:, :], top_t[0][:, :], -1e30, lg_t[:, :], ALU.mult, ALU.add),
                ss=True)
            b.op("dve", lambda e: e.reduce_max(topc[:, 1:2], top_t[1][:, :], axis=AXX),
                 ss=True)
            tdd = b.op("dve", lambda e: e.tensor_tensor(topc[:, 2:3], topc[:, 1:2],
                                                        topc[:, 0:1], ALU.subtract),
                       ss=True)
            tex = b.op("act", lambda e: e.activation(topc[:, 2:3], topc[:, 2:3],
                                                     AF.Exp), waits=[tdd])
            b.op("dve", lambda e: e.tensor_scalar_add(topc[:, 3:4], topc[:, 2:3], 1.0),
                 waits=[tex], ss=True)
            b.op("dve", lambda e: e.reciprocal(topc[:, 3:4], topc[:, 3:4]), ss=True)
            b.op("dve", lambda e: e.tensor_scalar(topc[:, 2:3], topc[:, 3:4], -1.0,
                                                  1.0, ALU.mult, ALU.add), ss=True)
            b.op("dve", lambda e: e.tensor_scalar(top_t[1][:, :], top_t[1][:, :],
                                                  topc[:, 1:2], None, ALU.is_equal),
                 ss=True)
            b.op("dve", lambda e: e.tensor_scalar(top_t[0][:, :], top_t[0][:, :],
                                                  topc[:, 3:4], None, ALU.mult),
                 ss=True)
            tcb = b.op("dve", lambda e, tt=tt: e.scalar_tensor_tensor(
                combA[:, tt, :], top_t[1][:, :], topc[:, 2:3], top_t[0][:, :],
                ALU.mult, ALU.add), ss=True)
            comb_evs.append(tf32(combA[:, tt, :],
                                 combT[0:8, tt * 128:(tt + 1) * 128],
                                 [tcb], 128, 8, eng="dve"))

        for e_i in range(int(os.environ.get('MK_NE', NE)) if ffn_lvl >= 2 else 0):
            crep_evs = []
            for nt in range(FNT):
                mt = b.pe_mm("pb4", comb_evs, lambda e, e_i=e_i, nt=nt: e.matmul(
                    pb[4][:, 0:512], selAll[0:8, e_i, :],
                    combT[:, nt * 512:(nt + 1) * 512], start=True, stop=True))
                crep_evs.append(b.evac("pb4", "act", lambda e, nt=nt: e.copy(
                    cRep[:, nt * 512:(nt + 1) * 512], pb[4][:, 0:512]),
                    waits=[mt, fgd["crep_dve"]]))

            hg_last = None
            for m in range(int(os.environ.get('MK_HM', HM)) if ffn_lvl >= 3 else 0):
                sl = m % 2
                sgn = "fgA" if sl == 0 else "fgB"
                sun = "fuA" if sl == 0 else "fuB"
                tlg = b.dma("sp", wgStg[sl][:, :, :], wg_pm[e_i, m],
                            waits=[fgd["stg"].get(("wg", sl))], sem=sgn)
                tlu = b.dma("sp", wuStg[sl][:, :, :], wu_pm[e_i, m],
                            waits=[fgd["stg"].get(("wu", sl))], sem=sun)
                tcg = b.op("po", lambda e, sl=sl: e.tensor_copy(
                    wgB[sl][:, :, :], wgStg[sl][:, :, :]),
                    waits=[tlg, fgd["stg"].get(("wgB", sl))])
                tcu = b.op("act", lambda e, sl=sl: e.copy(
                    wuB[sl][:, :, :], wuStg[sl][:, :, :]),
                    waits=[tlu, fgd["stg"].get(("wuB", sl))])
                fgd["stg"][("wg", sl)] = tcg
                fgd["stg"][("wu", sl)] = tcu
                if dbg_on and fb == 0 and e_i == 0 and m == 0:
                    dbg("wgB0", wgB[0][:, :, :], tcg)
                    dbg("wgS0", wgStg[0][:, :, :], tcg)
                MK_SUB = int(os.environ.get('MK_SUB', 5))
                for nt in range(FNT if MK_SUB >= 2 else 0):
                    def gmm(e, sl=sl, nt=nt):
                        for kc in range(8):
                            r = e.matmul(pb[0][:, :], wgB[sl][:, kc, :],
                                         h2_fm[:, kc, nt * 512:(nt + 1) * 512],
                                         start=(kc == 0), stop=(kc == 7))
                        return r
                    mtg = b.pe_mm("pb0", [tcg, tb_state["guard"][0], tb_state["guard"][1]], gmm)

                    def umm(e, sl=sl, nt=nt):
                        for kc in range(8):
                            r = e.matmul(pb[1][:, :], wuB[sl][:, kc, :],
                                         h2_fm[:, kc, nt * 512:(nt + 1) * 512],
                                         start=(kc == 0), stop=(kc == 7))
                        return r
                    if MK_SUB < 3:
                        continue
                    mtu = b.pe_mm("pb1", [tcu, tb_state["guard"][0], tb_state["guard"][1]], umm)
                    ssl = (m * FNT + nt) % 2
                    if MK_SUB < 4:
                        continue
                    tsg2 = b.evac("pb0", "act", lambda e, ssl=ssl: e.activation(
                        sg_t[ssl][:, :], pb[0][:, :], AF.Silu),
                        waits=[mtg, fgd["sg_guard"][ssl]])
                    if MK_SUB < 5:
                        continue
                    if dbg_on and fb == 0 and e_i == 0 and m == 0 and nt == 0:
                        ts1_ = b.op("act", lambda e, ssl=ssl: e.copy(
                            dtmp[1][:, :], sg_t[ssl][:, :]), waits=[tsg2])
                        ts2_ = b.op("dve", lambda e: e.tensor_copy(
                            dtmp[0][:, :], pb[1][:, :]), waits=[mtu])
                        dbg("sg0", dtmp[1][:, :], ts1_)
                        dbg("up0", dtmp[0][:, :], ts2_)
                        b.bank["pb1"] = ts2_
                        fgd["dtmp_guard"][0] = ("st", b.cnt["st"])
                        fgd["dtmp_guard"][1] = ("st", b.cnt["st"])
                    thg = b.evac("pb1", "dve",
                                 lambda e, ssl=ssl, m=m, nt=nt: e.tensor_tensor(
                                     hgAll[:, m, nt * 512:(nt + 1) * 512],
                                     sg_t[ssl][:, :], pb[1][:, :], ALU.mult),
                                 waits=[mtu, tsg2, fgd["hg_pe"]])
                    fgd["sg_guard"][ssl] = thg
                    hg_last = thg
                fgd["stg"][("wgB", sl)] = ("pe", b.cnt["pe"])
                fgd["stg"][("wuB", sl)] = ("pe", b.cnt["pe"])
                if m == HM - 1:
                    fgd["stg"][("wgB", 1 - sl)] = ("pe", b.cnt["pe"])
                    fgd["stg"][("wuB", 1 - sl)] = ("pe", b.cnt["pe"])

            for dm in range(8 if ffn_lvl >= 4 else 0):
                sl = dm % 2
                sdn = "fdA" if sl == 0 else "fdB"
                tld = b.dma("sp", wdStg[sl][:, :, :], wd_pm[e_i, dm],
                            waits=[fgd["stg"].get(("wd", sl))], sem=sdn)
                tcd = b.op("dve", lambda e, sl=sl: e.tensor_copy(
                    wdB[sl][:, :, :], wdStg[sl][:, :, :]),
                    waits=[tld, fgd["stg"].get("wdB")])
                fgd["stg"][("wd", sl)] = tcd
                for nt in range(FNT):
                    bankname = "pb2" if (dm * FNT + nt) % 2 == 0 else "pb3"
                    bank = pb[2] if (dm * FNT + nt) % 2 == 0 else pb[3]

                    def dmm(e, sl=sl, nt=nt, bank=bank):
                        for kc in range(HM):
                            r = e.matmul(bank[:, :], wdB[sl][:, kc, :],
                                         hgAll[:, kc, nt * 512:(nt + 1) * 512],
                                         start=(kc == 0), stop=(kc == HM - 1))
                        return r
                    mtd = b.pe_mm(bankname, [tcd, hg_last], dmm)
                    if e_i == 0:
                        b.evac(bankname, "dve",
                               lambda e, dm=dm, nt=nt, bank=bank: e.tensor_tensor(
                                   acc[:, dm, nt * 512:(nt + 1) * 512], bank[:, :],
                                   cRep[:, nt * 512:(nt + 1) * 512], ALU.mult),
                               waits=[mtd, fgd["acc_free"]] + crep_evs)
                    else:
                        tsl = (dm * FNT + nt) % 2
                        ev = b.evac(bankname, "dve",
                                    lambda e, tsl=tsl, nt=nt, bank=bank:
                                    e.tensor_tensor(dtmp[tsl][:, :], bank[:, :],
                                                    cRep[:, nt * 512:(nt + 1) * 512],
                                                    ALU.mult),
                                    waits=[mtd, fgd["dtmp_guard"][tsl]] + crep_evs)
                        fgd["dtmp_guard"][tsl] = b.op(
                            "dve", lambda e, dm=dm, nt=nt, tsl=tsl: e.tensor_tensor(
                                acc[:, dm, nt * 512:(nt + 1) * 512],
                                acc[:, dm, nt * 512:(nt + 1) * 512],
                                dtmp[tsl][:, :], ALU.add), waits=[ev])
                fgd["stg"]["wdB"] = ("pe", b.cnt["pe"])
            fgd["hg_pe"] = ("pe", b.cnt["pe"])
            fgd["crep_dve"] = ("dve", b.cnt["dve"])
        tacc = ("dve", b.cnt["dve"])

        for tt in range(8 if ffn_lvl >= 5 else 0):
            evl = []
            for dc in range(8):
                evl.append(tf32(acc[:, dc, tt * 128:(tt + 1) * 128],
                                acc_tm[:, dc * 128:(dc + 1) * 128],
                                [tacc, fgd["acc_dve"]], 128, 128))
            sl = tt % 2
            sname = "xnA" if sl == 0 else "xnB"
            tl = b.dma("sp", fx_t[sl][:, :],
                       xnew_d[base_tok + tt * 128: base_tok + (tt + 1) * 128, :],
                       waits=[fgd["fx"][sl]], sem=sname)
            b.op("dve", lambda e: e.tensor_tensor(fo_t[:, :], acc_tm[:, :],
                                                  gbRep[:, :], ALU.mult),
                 waits=evl + [tl, fgd["fo_store"]])
            tfo = b.op("dve", lambda e, sl=sl: e.tensor_tensor(
                fo_t[:, :], fx_t[sl][:, :], fo_t[:, :], ALU.add))
            fgd["acc_dve"] = tfo
            fgd["fx"][sl] = tfo
            fgd["fo_store"] = b.dma(
                "po", out_t[base_tok + tt * 128: base_tok + (tt + 1) * 128, :],
                fo_t[:, :], waits=[tfo], sem="st")
        if dbg_on and fb == 0:
            dbg("cRep", cRep[:, :], ("act", b.cnt["act"]))
            dbg("logitsF", logitsF[:, :], ("act", b.cnt["act"]))
            dbg("combT", combT[:, :], ("dve", b.cnt["dve"]))
            dbg("acc", acc[:, :, :], ("dve", b.cnt["dve"]))
            dbg("hgAll", hgAll[:, :, :], ("dve", b.cnt["dve"]))
            dbg("h2fm", h2_fm[:, :, :], ("dve", b.cnt["dve"]))
        fgd["acc_free"] = ("pe", b.cnt["pe"])
        fgd["h2fm_pe"] = ("pe", b.cnt["pe"])

    b.op("po", lambda e: e.engine_nop(), waits=[("st", b.cnt["st"])])
    es_f.close()

    blk = b.es.enter_context(nc.Block())

    def runner(qn):
        def body(eng):
            for cl in b.q[qn]:
                cl(eng)
        return body

    blk.sync(runner("sp"))
    blk.tensor(runner("pe"))
    blk.scalar(runner("act"))
    blk.vector(runner("dve"))
    blk.gpsimd(runner("po"))
    b.es.close()
    return b


_prog = {}


def get_prog():
    if "b" not in _prog:
        _prog["b"] = build_program()
    return _prog["b"]


def kernel(L0_attn_x, L0_attn_W_in, L0_attn_W_x, L0_attn_W_dt, L0_attn_b_dt,
           L0_attn_A_log, L0_attn_B_bias, L0_attn_C_bias, L0_attn_B_norm_w,
           L0_attn_C_norm_w, L0_attn_rope_freqs, L0_attn_D, L0_attn_W_out,
           L0_ffn_router_w, L0_ffn_wg, L0_ffn_wu, L0_ffn_wd, L0_c, L0_wmod):
    f32 = np.float32
    x = np.ascontiguousarray(np.asarray(L0_attn_x, f32))
    W_in = np.asarray(L0_attn_W_in, f32)
    W_x = np.asarray(L0_attn_W_x, f32)
    W_dt = np.asarray(L0_attn_W_dt, f32).reshape(-1)
    b_dt = np.asarray(L0_attn_b_dt, f32).reshape(-1)
    A_log = np.asarray(L0_attn_A_log, f32)
    Dv = np.asarray(L0_attn_D, f32).reshape(-1)
    W_out = np.asarray(L0_attn_W_out, f32)
    router = np.asarray(L0_ffn_router_w, f32)
    wg = np.asarray(L0_ffn_wg, f32)
    wu = np.asarray(L0_ffn_wu, f32)
    wd = np.asarray(L0_ffn_wd, f32)
    cmat = np.asarray(L0_c, f32)
    wmod = np.asarray(L0_wmod, f32)

    def pm_kc(w):
        K, M = w.shape
        return np.ascontiguousarray(w.reshape(K // 128, 128, M).transpose(1, 0, 2))

    sperm = np.concatenate([np.arange(0, 16, 2), np.arange(1, 16, 2)])
    W_x = np.concatenate([W_x[sperm], W_x[16 + sperm], W_x[32:34]], 0)
    A_log = A_log[:, sperm]
    smalls_perm = sperm

    H = 2752
    wg_p = np.zeros((NE, 1024, HP), f32)
    wg_p[:, :, :H] = wg
    wu_p = np.zeros((NE, 1024, HP), f32)
    wu_p[:, :, :H] = wu
    wd_p = np.zeros((NE, HP, 1024), f32)
    wd_p[:, :H, :] = wd
    wg_pm = np.ascontiguousarray(
        wg_p.reshape(NE, 8, 128, HM, 128).transpose(0, 3, 2, 1, 4))
    wu_pm = np.ascontiguousarray(
        wu_p.reshape(NE, 8, 128, HM, 128).transpose(0, 3, 2, 1, 4))
    wd_pm = np.ascontiguousarray(
        wd_p.reshape(NE, HM, 128, 8, 128).transpose(0, 3, 2, 1, 4))
    router_pm = pm_kc(router)
    wmod_pm = pm_kc(wmod)
    smalls = np.concatenate([np.asarray(L0_attn_B_bias, f32)[sperm],
                             np.asarray(L0_attn_C_bias, f32)[sperm],
                             np.asarray(L0_attn_B_norm_w, f32)[sperm],
                             np.asarray(L0_attn_C_norm_w, f32)[sperm]]).reshape(1, 64)
    rfreq = np.asarray(L0_attn_rope_freqs, f32).reshape(1, 8)

    in_maps = []
    for core in range(8):
        bb, j = core // 2, core % 2
        own = slice(j * 1024, (j + 1) * 1024)
        W_sel = np.concatenate(
            [W_in[own], W_in[2048 + j * 1024: 2048 + (j + 1) * 1024]], 0)
        ordv = np.concatenate([np.arange(j * 1024, (j + 1) * 1024),
                               np.arange((1 - j) * 1024, (2 - j) * 1024)])
        in_maps.append({
            "xb": np.ascontiguousarray(x[bb]),
            "x_own": np.ascontiguousarray(
                x[bb].reshape(NCH, CH, D)[j::2].reshape(OWN_TOK, D)),
            "w_in_pm": pm_kc(np.ascontiguousarray(W_sel.T)),
            "w_out_pm": pm_kc(np.ascontiguousarray(W_out[:, own].T)),
            "w_x_pm": pm_kc(np.ascontiguousarray(W_x[:, own].T)),
            "wdt_pm": np.ascontiguousarray(W_dt[ordv].reshape(16, 128).T),
            "bdt_pm": np.ascontiguousarray(b_dt[ordv].reshape(16, 128).T),
            "a_pm": np.ascontiguousarray(
                A_log[own].reshape(8, 128, 16).transpose(1, 0, 2).reshape(128, 128)),
            "d_pm": np.ascontiguousarray(Dv[own].reshape(8, 128).T),
            "smalls": smalls, "rfreq": rfreq,
            "c_pm": np.ascontiguousarray(cmat[bb].reshape(8, 128).T),
            "wmod_pm": wmod_pm, "router_pm": router_pm,
            "wg_pm": wg_pm, "wu_pm": wu_pm, "wd_pm": wd_pm,
        })

    bobj = get_prog()
    res = run_bass_kernel_spmd(bobj.nc, in_maps, core_ids=list(range(8)))

    out = np.empty((4, T, D), np.float32)
    for core in range(8):
        bb, j = core // 2, core % 2
        o = np.asarray(res.results[core]["out"]).reshape(OWN_TOK, D)
        out[bb].reshape(NCH, CH, D)[j::2] = o.reshape(NCH // 2, CH, D)
    return out


# version 07
# speedup vs baseline: 1.1972x; 1.0235x over previous
import os
import numpy as np
from contextlib import ExitStack

import concourse.bass as bass
import concourse.mybir as mybir
from concourse.bass_utils import run_bass_kernel_spmd

F32 = mybir.dt.float32
BF16 = mybir.dt.bfloat16
I32 = mybir.dt.int32
AF = mybir.ActivationFunctionType
ALU = mybir.AluOpType
AXX = mybir.AxisListType.X

T, D = 4096, 1024
CH = 256
NCH = T // CH
NTT = CH // 128
S = 16
EO = 1024
HP = 2816
HM = HP // 128
NE = 8
FTOK = 1024
FNT = FTOK // 512
OWN_TOK = 2048
PAIRS = [[0, 1], [2, 3], [4, 5], [6, 7]]
RMS_EPS = float(np.finfo(np.float32).eps)
LN_EPS = 1e-5
PI = float(np.pi)
TWO_PI = float(2 * np.pi)
INV_2PI = float(1.0 / (2 * np.pi))

ENGS = ("sp", "pe", "act", "dve", "po")


class B:
    def __init__(self):
        self.nc = bass.Bass("TRN2", target_bir_lowering=False)
        self.es = ExitStack()
        self.q = {e: [] for e in ENGS}
        self.semh = {}
        self.cnt = {}
        self.lastw = {e: {} for e in ENGS}
        self.bank = {}

    def sem(self, name):
        self.semh[name] = self.es.enter_context(self.nc.semaphore(name))
        self.cnt[name] = 0

    def sb(self, name, shape, dt, stack=None):
        return (stack or self.es).enter_context(self.nc.sbuf_tensor(name, shape, dt))

    def ps(self, name, shape, dt):
        return self.es.enter_context(self.nc.psum_tensor(name, shape, dt))

    def op(self, eng, fn, waits=(), inc=None, ss=False):
        if ss:
            waits = list(waits) + [(eng, self.cnt[eng])]
        ws = []
        lw = self.lastw[eng]
        for w in waits:
            if w is None:
                continue
            s, v = w
            if lw.get(s, 0) >= v:
                continue
            ws.append((s, v))
            lw[s] = v
        if inc is None:
            inc = (eng, 1)
        tok = None
        if inc:
            sname, n = inc
            self.cnt[sname] += n
            tok = (sname, self.cnt[sname])
        semh = self.semh

        def closure(e, ws=tuple(ws), fn=fn, inc=inc):
            for s, v in ws:
                e.wait_ge(semh[s], v)
            if fn is None:
                return
            r = fn(e)
            if inc:
                r.then_inc(semh[inc[0]], inc[1])
        self.q[eng].append(closure)
        return tok

    def barrier(self):
        toks = [(e, self.cnt[e]) for e in ("pe", "act", "dve", "po")]
        for eng in ENGS:
            self.op(eng, None, waits=toks, inc=False)

    def dma(self, eng, out, in_, waits=(), sem="ld", **kw):
        return self.op(eng, lambda e: e.dma_start(out=out, in_=in_, **kw),
                       waits, inc=(sem, 16))

    def pe_mm(self, bankname, waits, fn):
        g = self.bank.get(bankname)
        return self.op("pe", fn, list(waits) + [g])

    def evac(self, bankname, eng, fn, waits=()):
        tok = self.op(eng, fn, waits)
        self.bank[bankname] = tok
        return tok


def build_program(n_chunks=None, do_ffn=None, do_scan=True, ffn_lvl=5):
    if n_chunks is None:
        n_chunks = int(os.environ.get("MK_NCHUNKS", NCH))
    if do_ffn is None:
        do_ffn = bool(int(os.environ.get("MK_FFN", 1)))
    b = B()
    nc = b.nc

    def din(name, shape, dt=F32):
        return nc.dram_tensor(name, shape, dt, kind="ExternalInput").ap()

    xb = din("xb", [T, D])
    x_own = din("x_own", [OWN_TOK, D])
    w_in_pm = din("w_in_pm", [128, 8, 2048])
    w_out_pm = din("w_out_pm", [128, 8, 1024])
    w_x_pm = din("w_x_pm", [128, 8, 34])
    wdt_pm = din("wdt_pm", [128, 16])
    bdt_pm = din("bdt_pm", [128, 16])
    a_pm = din("a_pm", [128, 128])
    d_pm = din("d_pm", [128, 8])
    smalls = din("smalls", [1, 64])
    rfreq = din("rfreq", [1, 8])
    c_pm = din("c_pm", [128, 8])
    wmod_pm = din("wmod_pm", [128, 8, 6144])
    router_pm = din("router_pm", [128, 8, 8])
    wg_pm = din("wg_pm", [NE, HM, 128, 8, 128])
    wu_pm = din("wu_pm", [NE, HM, 128, 8, 128])
    wd_pm = din("wd_pm", [NE, 8, 128, HM, 128])

    out_t = nc.dram_tensor("out", [OWN_TOK, D], F32, kind="ExternalOutput").ap()

    ag_in = nc.dram_tensor("ag_in", [34, CH], F32).ap()
    ag_out = nc.dram_tensor("ag_out", [68, CH], F32).ap()
    rs_in = nc.dram_tensor("rs_in", [256, 8 * CH], BF16).ap()
    rs_out = nc.dram_tensor("rs_out", [128, 8 * CH], BF16).ap()
    xnew_d = nc.dram_tensor("xnew_d", [OWN_TOK, D], F32, kind="ExternalOutput" if os.environ.get("MK_DEBUG") else "Internal").ap()

    dbg_on = bool(os.environ.get("MK_DEBUG"))
    dbg_chunk = int(os.environ.get("MK_DEBUG_CHUNK", 0))

    def dbg(name, ap_, wait):
        if not dbg_on:
            return
        t = nc.dram_tensor("dbg_" + name, list(ap_.shape), ap_.dtype,
                           kind="ExternalOutput").ap()
        b.dma("po", t, ap_, waits=[wait], sem="st")

    for s in ("ld", "st", "cc", "pe", "act", "dve", "po",
              "xA", "xB", "wiS", "agl", "rsl", "lrw",
              "fgA", "fgB", "fuA", "fuB", "fdA", "fdB", "xnA", "xnB"):
        b.sem(s)

    pb = [b.ps(f"pb{i}", [128, 512], F32) for i in range(6)]
    pT1 = b.ps("pT1", [128, 1024], BF16)
    pT2 = b.ps("pT2", [128, 1024], BF16)

    identF = b.sb("identF", [128, 128], F32)
    identB = b.sb("identB", [128, 128], BF16)
    onesCol = b.sb("onesCol", [128, 1], F32)
    ones1 = b.sb("ones1", [1, 128], F32)
    selAll = b.sb("selAll", [16, 16, 128], BF16)
    sfreqRep = b.sb("sfreqRep", [128, 8], F32)
    bRepB = b.sb("bRepB", [128, 16], F32)
    bRepC = b.sb("bRepC", [128, 16], F32)
    wRepB = b.sb("wRepB", [128, 16], F32)
    wRepC = b.sb("wRepC", [128, 16], F32)
    wdtCol = b.sb("wdtCol", [128, 16], F32)
    bdtCol = b.sb("bdtCol", [128, 16], F32)
    A_sb = b.sb("A_sb", [128, 128], F32)
    dColS = b.sb("dColS", [128, 8], F32)
    wx_bf = b.sb("wx_bf", [128, 8, 34], BF16)
    router_bf = b.sb("router_bf", [128, 8, 8], BF16)
    caRep = b.sb("caRep", [128, D], BF16)
    saRep = b.sb("saRep", [128, D], BF16)
    gaRep = b.sb("gaRep", [128, D], BF16)
    cbRep = b.sb("cbRep", [128, D], BF16)
    sbRep = b.sb("sbRep", [128, D], BF16)
    gbRep = b.sb("gbRep", [128, D], BF16)
    smallRow = b.sb("smallRow", [1, 64], F32)
    freqRow = b.sb("freqRow", [1, 8], F32)
    sfRow = b.sb("sfRow", [1, 8], F32)

    CONSTS = [LN_EPS, RMS_EPS, 1.0, -1.0, -0.5, 1.0 / 16, 1.0 / 1024, 1.0 / 2048,
              PI, PI / 2, TWO_PI, -TWO_PI, INV_2PI, 0.25, 2.0, 1e-4, -1e30, 0.0,
              0.5]
    constT = b.sb("constT", [128, 32], F32)
    for i, v in enumerate(CONSTS):
        b.op("po", lambda e, i=i, v=v: e.memset(constT[:, i:i + 1], float(v)),
             inc=False)
        nc.const_aps.register(F32, float(v), constT[:, i:i + 1])

    tb_state = {"pend": [], "bank": 0, "guard": [None, None], "setup": None}

    def flush_tbf():
        pend = tb_state["pend"]
        if not pend:
            return None
        bk = tb_state["bank"]
        bank = pT1 if bk == 0 else pT2
        bankname = "pT1" if bk == 0 else "pT2"
        tb_state["bank"] = 1 - bk
        tb_state["pend"] = []
        waits = [tb_state["setup"]]
        for (_, _, w) in pend:
            waits += w
        waits.append(tb_state["guard"][bk])

        def body(e, pend=tuple(pend), bank=bank):
            for i, (src, _, _) in enumerate(pend):
                r = e.transpose(bank[:, i * 128:(i + 1) * 128], src, identB[:, :])
            return r
        mt = b.pe_mm(bankname, waits, body)
        ev = None
        for i, (_, dst, _) in enumerate(pend):
            ev = b.op("dve", lambda e, i=i, dst=dst, bank=bank: e.tensor_copy(
                dst, bank[:, i * 128:(i + 1) * 128]), waits=[mt])
        tb_state["guard"][bk] = ev
        b.bank[bankname] = ev
        return ev

    def tbf(src_ap, dst_ap, waits):
        tb_state["pend"].append((src_ap, dst_ap, list(waits)))
        if len(tb_state["pend"]) == 4:
            return flush_tbf()
        return None

    def tf32(src_ap, dst_ap, waits, p_in, f_in, eng="act"):
        mt = b.pe_mm("pb5", list(waits) + [tb_state["setup"]], lambda e: e.transpose(
            pb[5][0:f_in, 0:p_in], src_ap, identF[0:p_in, 0:p_in]))
        if eng == "act":
            fn = lambda e: e.copy(dst_ap, pb[5][0:f_in, 0:p_in])
        else:
            fn = lambda e: e.tensor_copy(dst_ap, pb[5][0:f_in, 0:p_in])
        return b.evac("pb5", eng, fn, waits=[mt])

    es_setup = ExitStack()
    selF = b.sb("selF", [16, 16, 128], F32, es_setup)
    modRow = b.sb("modRow", [1, 6144], F32, es_setup)
    wmod_sb = [b.sb(f"wmod_sb{i}", [128, 8, 512], F32, es_setup) for i in range(2)]
    cColT = b.sb("cColT", [128, 8], F32, es_setup)
    scCol = b.sb("scCol", [128, 8], F32, es_setup)
    wxStg = b.sb("wxStg", [128, 8, 34], F32, es_setup)
    rtStg = b.sb("rtStg", [128, 8, 8], F32, es_setup)
    gtmpA = b.sb("gtmpA", [1, 1024], F32, es_setup)
    gtmpS = b.sb("gtmpS", [1, 1024], F32, es_setup)

    b.op("po", lambda e: e.iota(identF[:, :], pattern=[[1, 128]], base=0,
                                channel_multiplier=-1,
                                allow_small_or_imprecise_dtypes=True))
    b.op("po", lambda e: e.tensor_single_scalar(identF[:, :], identF[:, :], 0.0,
                                                ALU.is_equal))
    b.op("po", lambda e: e.tensor_copy(identB[:, :], identF[:, :]))
    b.op("po", lambda e: e.memset(onesCol[:, :], 1.0))
    b.op("po", lambda e: e.memset(ones1[:, :], 1.0))
    b.op("po", lambda e: e.iota(selF[:, :, :], pattern=[[1, 16], [0, 128]], base=0,
                                channel_multiplier=-1,
                                allow_small_or_imprecise_dtypes=True))
    b.op("po", lambda e: e.tensor_single_scalar(selF[:, :, :], selF[:, :, :], 0.0,
                                                ALU.is_equal))
    tok_sel = b.op("po", lambda e: e.tensor_copy(selAll[:, :, :], selF[:, :, :]))
    tb_state["setup"] = tok_sel

    b.dma("sp", smallRow[:, :], smalls)
    b.dma("sp", freqRow[:, :], rfreq)
    b.dma("sp", wdtCol[:, :], wdt_pm)
    b.dma("sp", bdtCol[:, :], bdt_pm)
    b.dma("sp", A_sb[:, :], a_pm)
    b.dma("sp", dColS[:, :], d_pm)
    b.dma("sp", cColT[:, :], c_pm)
    b.dma("sp", wxStg[:, :, :], w_x_pm)
    b.dma("sp", rtStg[:, :, :], router_pm)
    ld0 = ("ld", b.cnt["ld"])

    ta = b.op("act", lambda e: e.activation(A_sb[:, :], A_sb[:, :], AF.Exp),
              waits=[ld0])
    tok_A = b.op("dve", lambda e: e.tensor_scalar(A_sb[:, :], A_sb[:, :], 1e-4, -1.0,
                                                  ALU.max, ALU.mult), waits=[ta])
    b.op("act", lambda e: e.activation(sfRow[:, :], freqRow[:, :], AF.Abs),
         waits=[ld0])
    b.op("act", lambda e: e.activation(sfRow[:, :], sfRow[:, :], AF.Exp, scale=-1.0),
         ss=True)
    ta2 = b.op("act", lambda e: e.activation(sfRow[:, :], sfRow[:, :], AF.Ln, bias=1.0),
               ss=True)
    td2 = b.op("dve", lambda e: e.scalar_tensor_tensor(
        sfRow[:, :], freqRow[:, :], 0.0, sfRow[:, :], ALU.max, ALU.add), waits=[ta2])
    b.op("act", lambda e: e.copy(wx_bf[:, :, :], wxStg[:, :, :]), waits=[ld0],
         inc=False)
    b.op("act", lambda e: e.copy(router_bf[:, :, :], rtStg[:, :, :]))
    tsc = b.op("act", lambda e: e.activation(scCol[:, :], cColT[:, :], AF.Silu))

    wm_guard = [None, None]
    for nchunk in range(12):
        sl = nchunk % 2
        tlw = b.dma("sp", wmod_sb[sl][:, :, :],
                    wmod_pm[:, :, nchunk * 512:(nchunk + 1) * 512],
                    waits=[wm_guard[sl]], sem="wiS")

        def mk(nchunk=nchunk, sl=sl):
            def f(e):
                for kc in range(8):
                    r = e.matmul(pb[5][0:1, 0:512], scCol[:, kc:kc + 1],
                                 wmod_sb[sl][:, kc, :],
                                 start=(kc == 0), stop=(kc == 7))
                return r
            return f
        mt = b.pe_mm("pb5", [tsc, ld0, tlw], mk())
        wm_guard[sl] = mt
        b.evac("pb5", "act", lambda e, nchunk=nchunk: e.copy(
            modRow[0:1, nchunk * 512:(nchunk + 1) * 512], pb[5][0:1, 0:512]),
            waits=[mt])
    tmod = ("act", b.cnt["act"])

    tmodd = tmod
    prev_clamp = None
    for base in (2048, 5120):
        b.op("dve", lambda e, base=base: e.tensor_scalar(
            gtmpS[:, :], modRow[0:1, base:base + 1024], 0.0, None, ALU.is_ge),
            waits=[tmod])
        b.op("dve", lambda e: e.tensor_scalar(gtmpS[:, :], gtmpS[:, :], 2.0, -1.0,
                                              ALU.mult, ALU.add))
        tabs = b.op("act", lambda e, base=base: e.activation(
            gtmpA[:, :], modRow[0:1, base:base + 1024], AF.Abs),
            waits=[tmod, prev_clamp])
        b.op("dve", lambda e: e.tensor_scalar_max(gtmpA[:, :], gtmpA[:, :], 0.25),
             waits=[tabs])
        tmodd = b.op("dve", lambda e, base=base: e.tensor_tensor(
            modRow[0:1, base:base + 1024], gtmpS[:, :], gtmpA[:, :], ALU.mult))
        prev_clamp = tmodd

    for (dst, base, bias) in ((caRep, 1024, 1.0), (saRep, 0, 0.0), (gaRep, 2048, 0.0),
                              (cbRep, 4096, 1.0), (sbRep, 3072, 0.0),
                              (gbRep, 5120, 0.0)):
        for nchunk in range(2):
            mt = b.pe_mm("pb4", [tmodd], lambda e, base=base, nchunk=nchunk: e.matmul(
                pb[4][:, :], ones1[0:1, :],
                modRow[0:1, base + nchunk * 512: base + (nchunk + 1) * 512],
                start=True, stop=True))
            b.evac("pb4", "act", lambda e, dst=dst, nchunk=nchunk, bias=bias:
                   e.activation(dst[:, nchunk * 512:(nchunk + 1) * 512],
                                pb[4][:, :], AF.Copy, bias=bias), waits=[mt])

    for (dst, off) in ((bRepB, 0), (bRepC, 16), (wRepB, 32), (wRepC, 48)):
        mt = b.pe_mm("pb4", [ld0], lambda e, off=off: e.matmul(
            pb[4][:, 0:16], ones1[0:1, :], smallRow[0:1, off:off + 16],
            start=True, stop=True))
        b.evac("pb4", "act", lambda e, dst=dst: e.copy(dst[:, :], pb[4][:, 0:16]),
               waits=[mt])
    mt = b.pe_mm("pb4", [td2], lambda e: e.matmul(
        pb[4][:, 0:8], ones1[0:1, :], sfRow[0:1, :], start=True, stop=True))
    b.evac("pb4", "act", lambda e: e.copy(sfreqRep[:, :], pb[4][:, 0:8]), waits=[mt])

    setup_act = ("act", b.cnt["act"])
    es_setup.close()
    b.barrier()

    es_m = ExitStack()
    w_in_bf = b.sb("w_in_bf", [128, 8, 2048], BF16, es_m)
    w_out_bf = b.sb("w_out_bf", [128, 8, 1024], BF16, es_m)
    wstgM = b.sb("wstgM", [128, 8, 128], F32, es_m)
    x_t = [b.sb("x_t0", [128, D], F32, es_m)] * 2
    h_tm = b.sb("h_tm", [128, D], BF16, es_m)
    h_fm = b.sb("h_fm", [128, 8, CH], BF16, es_m)
    xp_bf = [b.sb(f"xp{p}", [128, 8, CH], BF16, es_m) for p in range(2)]
    sz_bf = [b.sb(f"sz{p}", [128, 8, CH], BF16, es_m) for p in range(2)]
    wc_bf = [b.sb(f"wc{p}", [128, 8, CH], BF16, es_m) for p in range(2)]
    w0_bf = [b.sb(f"w0{p}", [128, 8, CH], BF16, es_m) for p in range(2)]
    dtO = [b.sb(f"dtO{p}", [128, 8, CH], F32, es_m) for p in range(2)]
    dtX = b.sb("dtX", [128, CH], F32, es_m)
    yacc = b.sb("yacc", [128, 8, CH], F32, es_m)
    carryA = b.sb("carryA", [128, 128], F32, es_m)
    ldtLast = [b.sb(f"ldtL{p}", [128, 8], F32, es_m) for p in range(2)]
    ssm_a = b.sb("ssm_a", [34, CH], F32, es_m)
    ssm_b = b.sb("ssm_b", [34, CH], F32, es_m)
    ssm_f = b.sb("ssm_f", [34, CH], F32, es_m)
    ssmT = b.sb("ssmT", [128, NTT, 34], F32, es_m)
    bnB = b.sb("bnB", [128, 16], F32, es_m)
    bnC = b.sb("bnC", [128, 16], F32, es_m)
    sqS = b.sb("sqS", [128, 16], F32, es_m)
    colS = b.sb("colS", [128, 4], F32, es_m)
    BrTm = b.sb("BrTm", [128, 16], F32, es_m)
    CrTm = b.sb("CrTm", [128, 16], F32, es_m)
    ropeT = [b.sb(f"ropeT{i}", [128, 8], F32, es_m) for i in range(6)]
    ropeI = b.sb("ropeI", [128, 8], I32, es_m)
    BrTf = [b.sb(f"BrTf{p}", [16, CH], BF16, es_m) for p in range(2)]
    CrTf = [b.sb(f"CrTf{p}", [16, CH], BF16, es_m) for p in range(2)]
    qcol = b.sb("qcol", [128, 1], F32, es_m)
    qrow = b.sb("qrow", [1, CH], F32, es_m)
    qRep = [b.sb(f"qRep{p}", [128, CH], F32, es_m) for p in range(2)]
    dt_inRep = b.sb("dt_inRep", [128, CH], F32, es_m)
    lamRep = b.sb("lamRep", [128, CH], F32, es_m)
    olamRep = b.sb("olamRep", [128, CH], F32, es_m)
    lamrow = b.sb("lamrow", [1, CH], F32, es_m)
    dtirow = b.sb("dtirow", [1, CH], F32, es_m)
    mdtRow = b.sb("mdtRow", [1, CH], F32, es_m)
    cumRow = [b.sb(f"cumRow{p}", [1, CH], F32, es_m) for p in range(2)]
    onesRow = b.sb("onesRow", [1, CH], F32, es_m)
    ldt_t = b.sb("ldt_t", [128, CH], F32, es_m)
    p2sh_t = b.sb("p2sh_t", [128, CH], F32, es_m)
    s1_t = b.sb("s1_t", [128, CH], F32, es_m)
    ppB = b.sb("ppB", [128, 8], F32, es_m)
    BrB = b.sb("BrB", [128, CH], BF16, es_m)
    CrB2 = [b.sb(f"CrB{i}", [128, CH], BF16, es_m) for i in range(2)]
    dA_t = [b.sb(f"dA{i}", [128, CH], F32, es_m) for i in range(2)]
    u_t = [b.sb(f"u{i}", [128, CH], BF16, es_m) for i in range(2)]
    h_t = [b.sb(f"h{i}", [128, CH], BF16, es_m) for i in range(2)]
    tmp_t = [b.sb(f"tmp{i}", [128, CH], BF16, es_m) for i in range(2)]
    yp1 = b.sb("yp1", [128, CH], F32, es_m)
    yp2 = b.sb("yp2", [128, CH], F32, es_m)
    Y3 = b.sb("Y3", [128, 8, CH], BF16, es_m)
    attnP = b.sb("attnP", [128, 8, CH], BF16, es_m)
    attnS = b.sb("attnS", [128, 8, CH], BF16, es_m)
    attn_tm = b.sb("attn_tm", [128, D], BF16, es_m)
    xo_t = [b.sb("xo0", [128, D], F32, es_m)] * 2
    xn_t = b.sb("xn_t", [128, D], F32, es_m)
    snap = b.sb("snap", [128, 64], F32, es_m)
    snapT = b.sb("snapT", [128, D], BF16, es_m)
    snapX = b.sb("snapX", [128, D], F32, es_m)
    lnc = b.sb("lnc", [128, D], F32, es_m)
    lncol = b.sb("lncol", [128, 4], F32, es_m)
    lnsq = b.sb("lnsq", [128, D], F32, es_m)

    tcast = ("po", b.cnt["po"])
    for mo in range(16):
        tl = b.dma("sp", wstgM[:, :, :], w_in_pm[:, :, mo * 128:(mo + 1) * 128],
                   waits=[tcast], sem="wiS")
        tcast = b.op("po", lambda e, mo=mo: e.tensor_copy(
            w_in_bf[:, :, mo * 128:(mo + 1) * 128], wstgM[:, :, :]), waits=[tl])
    for mo in range(8):
        tl = b.dma("sp", wstgM[:, :, :], w_out_pm[:, :, mo * 128:(mo + 1) * 128],
                   waits=[tcast], sem="wiS")
        tcast = b.op("po", lambda e, mo=mo: e.tensor_copy(
            w_out_bf[:, :, mo * 128:(mo + 1) * 128], wstgM[:, :, :]), waits=[tl])
    w_cast_done = tcast

    b.op("dve", lambda e: e.memset(carryA[:, :], 0.0))
    b.op("po", lambda e: e.memset(onesRow[:, :], 1.0))

    gd = {
        "xp_free": [None, None], "sz_free": [None, None],
        "wc_free": [None, None], "w0_free": [None, None],
        "dtO_free": [None, None], "BrTf_free": [None, None],
        "qRep_free": [None, None], "hfm_pe": None, "Y3_free": None,
        "attnP_free": None, "attnS_free": None,
        "x_slot": [None, None], "xo_slot": [None, None],
        "scan_dve": [None, None], "ypost_dve": [None, None],
        "wc_patch": [None, None], "w0_patch": [None, None],
        "dt_tok": [None, None], "rope_tok": [None, None],
        "dA_guard": [None, None], "h_guard": [None, None],
        "tmp_guard": [None, None], "prev_u": None, "prev_y": None,
        "prev_pm": None, "tmm_prev": None,
        "scan_po": [None, None], "ys_last2": None,
        "ag_next": None, "rs_next": None, "lamrow_free": None,
        "ropePE": None, "rope_dve": None, "tsum_prev": None,
        "cum_pe": [None, None],
        "ln_sq_prev": None, "bn_prev": [None, None],
        "sz_tok": {}, "dA_read": [None, None],
    }

    def ln_adaln(xtile, crep, srep, out_bf, waits_x, extra=()):
        b.op("dve", lambda e: e.reduce_sum(lncol[:, 0:1], xtile[:, :], axis=AXX),
             waits=list(waits_x) + list(extra) + [gd["ln_sq_prev"]])
        b.op("dve", lambda e: e.tensor_scalar_mul(lncol[:, 1:2], lncol[:, 0:1],
                                                  1.0 / D), ss=True)
        tc = b.op("dve", lambda e: e.tensor_scalar(lnc[:, :], xtile[:, :],
                                                   lncol[:, 1:2], None, ALU.subtract),
                  ss=True)
        tq = b.op("act", lambda e: e.activation(lnsq[:, :], lnc[:, :], AF.Square,
                                                accum_out=lncol[:, 2:3]), waits=[tc])
        gd["ln_sq_prev"] = tq
        b.op("act", lambda e: e.activation(lncol[:, 3:4], lncol[:, 2:3], AF.Ln,
                                           scale=1.0 / D, bias=LN_EPS), ss=True)
        tr = b.op("act", lambda e: e.activation(lncol[:, 3:4], lncol[:, 3:4], AF.Exp,
                                                scale=-0.5), ss=True)
        b.op("dve", lambda e: e.tensor_scalar(lnc[:, :], lnc[:, :], lncol[:, 3:4],
                                              None, ALU.mult), waits=[tr])
        b.op("dve", lambda e: e.tensor_tensor(lnsq[:, :], lnc[:, :], crep[:, :],
                                              ALU.mult))
        return b.op("dve", lambda e: e.tensor_tensor(out_bf[:, :], lnsq[:, :],
                                                     srep[:, :], ALU.add))

    def front(c):
        par = c % 2
        for tt in range(NTT):
            sl = 0
            tl = b.dma("sp", x_t[sl][:, :],
                       xb[c * CH + tt * 128: c * CH + (tt + 1) * 128, :],
                       waits=[gd["x_slot"][sl], ld0], sem="xA")
            tln = ln_adaln(x_t[sl], caRep, saRep, h_tm, [tl],
                           extra=[("pe", b.cnt["pe"]), setup_act])
            gd["x_slot"][sl] = tln
            for dc in range(8):
                tbf(h_tm[:, dc * 128:(dc + 1) * 128],
                    h_fm[:, dc, tt * 128:(tt + 1) * 128], [tln])
            flush_tbf()
        th = None

        ev_xp = None
        ev = None
        for mo in range(16):
            bankname = "pb0" if mo % 2 == 0 else "pb1"
            bank = pb[0] if mo % 2 == 0 else pb[1]

            def mmf(e, mo=mo, bank=bank):
                for kc in range(8):
                    r = e.matmul(bank[:, 0:CH], w_in_bf[:, kc, mo * 128:(mo + 1) * 128],
                                 h_fm[:, kc, :], start=(kc == 0), stop=(kc == 7))
                return r
            mt = b.pe_mm(bankname, [tb_state["guard"][0], tb_state["guard"][1],
                                    w_cast_done], mmf)
            if mo < 8:
                dst, w8 = xp_bf[par], gd["xp_free"][par]
            else:
                dst, w8 = sz_bf[par], gd["sz_free"][par]
            ev = b.evac(bankname, "act", lambda e, dst=dst, mo=mo, bank=bank:
                        e.activation(dst[:, mo % 8, :], bank[:, 0:CH], AF.Silu),
                        waits=[mt, w8])
            if mo == 7:
                ev_xp = ev
        gd["sz_tok"][c] = ev
        if c == dbg_chunk:
            dbg("hfm", h_fm[:, :, :], ev)
        gd["hfm_pe"] = ("pe", b.cnt["pe"])

        def ssm_mm(e):
            for kc in range(8):
                r = e.matmul(pb[5][0:34, 0:CH], wx_bf[:, kc, :], xp_bf[par][:, kc, :],
                             start=(kc == 0), stop=(kc == 7))
            return r
        mt = b.pe_mm("pb5", [ev_xp], ssm_mm)
        evs = b.evac("pb5", "act", lambda e: e.copy(ssm_a[:, :], pb[5][0:34, 0:CH]),
                     waits=[mt, gd["tsum_prev"]])
        tst = b.dma("po", ag_in, ssm_a[:, :], waits=[evs, gd["ag_next"]], sem="st")
        tcc = b.op("po", lambda e: e.collective_compute(
            "AllGather", ALU.bypass, replica_groups=PAIRS, ins=[ag_in], outs=[ag_out]),
            waits=[("st", b.cnt["st"]), ("agl", b.cnt["agl"])], inc=("cc", 1))
        gd["ag_next"] = tcc
        tla = b.dma("sp", ssm_a[:, :], ag_out[0:34, :], waits=[tcc], sem="agl")
        tlb = b.dma("sp", ssm_b[:, :], ag_out[34:68, :], sem="agl")
        tsum = b.op("dve", lambda e: e.tensor_tensor(ssm_f[:, :], ssm_a[:, :],
                                                     ssm_b[:, :], ALU.add),
                    waits=[tla, tlb])
        gd["tsum_prev"] = tsum

        if c == dbg_chunk:
            dbg("ssm_f", ssm_f[:, :], tsum)
        evT = []
        for tt in range(NTT):
            evT.append(tf32(ssm_f[0:34, tt * 128:(tt + 1) * 128],
                            ssmT[:, tt, 0:34], [tsum, gd["rope_dve"]], 34, 128))

        tdr = b.dma("sp", dtirow[:, :], ssm_f[32:33, :],
                    waits=[tsum, gd["lamrow_free"]], sem="lrw")
        mt = b.pe_mm("pb4", [tdr], lambda e: e.matmul(
            pb[4][:, 0:CH], ones1[0:1, :], dtirow[0:1, :], start=True, stop=True))
        tdir = b.evac("pb4", "act", lambda e: e.copy(dt_inRep[:, :], pb[4][:, 0:CH]),
                      waits=[mt, gd["scan_dve"][par]])
        trow = b.dma("sp", lamrow[:, :], ssm_f[33:34, :],
                     waits=[tsum, gd["lamrow_free"]], sem="lrw")
        tsg = b.op("act", lambda e: e.activation(lamrow[:, :], lamrow[:, :],
                                                 AF.Sigmoid), waits=[trow])
        mt = b.pe_mm("pb4", [tsg], lambda e: e.matmul(
            pb[4][:, 0:CH], ones1[0:1, :], lamrow[0:1, :], start=True, stop=True))
        gd["lamrow_free"] = ("pe", b.cnt["pe"])
        tlr = b.evac("pb4", "act", lambda e: e.copy(lamRep[:, :], pb[4][:, 0:CH]),
                     waits=[mt, gd["scan_dve"][par]])
        tol = b.op("dve", lambda e: e.tensor_scalar(olamRep[:, :], lamRep[:, :],
                                                    -1.0, 1.0, ALU.mult, ALU.add),
                   waits=[tlr])

        last = None
        for etf in range(16):
            own = etf < 8
            ta1 = b.op("dve", lambda e, etf=etf: e.tensor_scalar(
                dtX[:, :], dt_inRep[:, :], wdtCol[:, etf:etf + 1],
                bdtCol[:, etf:etf + 1], ALU.mult, ALU.add),
                waits=[tdir, ld0, gd["tmm_prev"],
                       gd["dtO_free"][par] if own else None])
            tab = b.op("act", lambda e: e.activation(lnsq[:, 0:CH], dtX[:, :], AF.Abs),
                       waits=[ta1])
            b.op("act", lambda e: e.activation(lnsq[:, 0:CH], lnsq[:, 0:CH], AF.Exp,
                                               scale=-1.0))
            tal = b.op("act", lambda e: e.activation(lnsq[:, 0:CH], lnsq[:, 0:CH],
                                                     AF.Ln, bias=1.0))
            if own:
                tdt = b.op("dve", lambda e, etf=etf: e.scalar_tensor_tensor(
                    dtO[par][:, etf, :], dtX[:, :], 0.0, lnsq[:, 0:CH],
                    ALU.max, ALU.add), waits=[tal, gd["dA_read"][par]])
                src = dtO[par][:, etf, :]
            else:
                tdt = b.op("dve", lambda e: e.scalar_tensor_tensor(
                    dtX[:, :], dtX[:, :], 0.0, lnsq[:, 0:CH], ALU.max, ALU.add),
                    waits=[tal])
                src = dtX[:, :]
            gd["tmm_prev"] = b.pe_mm("pb3", [tdt], lambda e, src=src, etf=etf: e.matmul(
                pb[3][0:1, 0:CH], onesCol[:, 0:1], src,
                start=(etf == 0), stop=(etf == 15)))
            if own:
                tld = b.op("dve", lambda e, etf=etf: e.tensor_tensor(
                    ldt_t[:, :], dtO[par][:, etf, :], lamRep[:, :], ALU.mult),
                    waits=[tol])
                if c == 0:
                    b.op("dve", lambda e, etf=etf: e.tensor_copy(
                        ldt_t[:, 0:1], dtO[par][:, etf, 0:1]))
                b.op("dve", lambda e, etf=etf: e.tensor_tensor(
                    p2sh_t[:, 0:CH - 1], dtO[par][:, etf, 1:CH], olamRep[:, 1:CH],
                    ALU.mult))
                b.op("dve", lambda e: e.tensor_tensor(
                    s1_t[:, 0:CH - 1], ldt_t[:, 0:CH - 1], p2sh_t[:, 0:CH - 1],
                    ALU.add))
                b.op("dve", lambda e, etf=etf: e.tensor_tensor(
                    wc_bf[par][:, etf, 0:CH - 1], xp_bf[par][:, etf, 0:CH - 1],
                    s1_t[:, 0:CH - 1], ALU.mult),
                    waits=[ev_xp, gd["wc_free"][par], gd["scan_po"][par]])
                b.op("dve", lambda e, etf=etf: e.tensor_tensor(
                    w0_bf[par][:, etf, 0:CH - 1], xp_bf[par][:, etf, 0:CH - 1],
                    p2sh_t[:, 0:CH - 1], ALU.mult),
                    waits=[gd["w0_free"][par]])
                last = b.op("dve", lambda e, etf=etf: e.tensor_copy(
                    ldtLast[par][:, etf:etf + 1], ldt_t[:, CH - 1:CH]), ss=True)
        gd["dt_tok"][par] = last
        if c == dbg_chunk:
            dbg("dt_inRep", dt_inRep[:, :], tdir)
            dbg("lamRep", lamRep[:, :], tlr)
            dbg("dtO", dtO[par][:, :, :], last)
            dbg("xp", xp_bf[par][:, :, :], ev_xp)
        tmean = ("pe", b.cnt["pe"])
        tmd = b.evac("pb3", "act", lambda e: e.activation(
            mdtRow[0:1, :], pb[3][0:1, 0:CH], AF.Copy, scale=1.0 / 2048.0),
            waits=[tmean, gd["scan_dve"][par]])

        init = 0.0 if c == 0 else cumRow[1 - par][0:1, CH - 1:CH]
        tcum = b.op("dve", lambda e, init=init: e.tensor_tensor_scan(
            cumRow[par][0:1, :], onesRow[0:1, :], mdtRow[0:1, :], init,
            ALU.mult, ALU.add), waits=[tmd, gd["cum_pe"][par]])

        rope_evs = []
        for tt in range(NTT):
            cix = 0 if tt == 0 else 3
            tcc2 = tf32(cumRow[par][0:1, tt * 128:(tt + 1) * 128],
                        colS[:, cix:cix + 1], [tcum], 1, 128)
            tb1 = b.op("dve", lambda e, tt=tt: e.tensor_tensor(
                bnB[:, :], ssmT[:, tt, 0:16], bRepB[:, :], ALU.add),
                waits=[evT[tt], gd["bn_prev"][0]], ss=True)
            tb2 = b.op("act", lambda e: e.activation(sqS[:, :], bnB[:, :], AF.Square,
                                                     accum_out=colS[:, 1:2]),
                       waits=[tb1])
            b.op("act", lambda e: e.activation(colS[:, 1:2], colS[:, 1:2], AF.Ln,
                                               scale=1.0 / 16, bias=RMS_EPS),
                 ss=True)
            tb3 = b.op("act", lambda e: e.activation(colS[:, 1:2], colS[:, 1:2],
                                                     AF.Exp, scale=-0.5), ss=True)
            tb4 = b.op("dve", lambda e: e.scalar_tensor_tensor(
                bnB[:, :], bnB[:, :], colS[:, 1:2], wRepB[:, :], ALU.mult, ALU.mult),
                waits=[tb3], ss=True)
            gd["bn_prev"][0] = tb2
            tc1 = b.op("dve", lambda e, tt=tt: e.tensor_tensor(
                bnC[:, :], ssmT[:, tt, 16:32], bRepC[:, :], ALU.add),
                waits=[gd["bn_prev"][1]], ss=True)
            tc2 = b.op("act", lambda e: e.activation(sqS[:, :], bnC[:, :], AF.Square,
                                                     accum_out=colS[:, 2:3]),
                       waits=[tc1])
            b.op("act", lambda e: e.activation(colS[:, 2:3], colS[:, 2:3], AF.Ln,
                                               scale=1.0 / 16, bias=RMS_EPS),
                 ss=True)
            tc3 = b.op("act", lambda e: e.activation(colS[:, 2:3], colS[:, 2:3],
                                                     AF.Exp, scale=-0.5), ss=True)
            tc4 = b.op("dve", lambda e: e.scalar_tensor_tensor(
                bnC[:, :], bnC[:, :], colS[:, 2:3], wRepC[:, :], ALU.mult, ALU.mult),
                waits=[tc3], ss=True)
            gd["bn_prev"][1] = tc2

            tan = b.op("dve", lambda e, cix=cix: e.tensor_scalar(
                ropeT[0][:, :], sfreqRep[:, :], colS[:, cix:cix + 1], None, ALU.mult),
                waits=[tcc2], ss=True)
            b.op("dve", lambda e: e.tensor_scalar_mul(ropeT[1][:, :], ropeT[0][:, :],
                                                      INV_2PI), ss=True)
            b.op("dve", lambda e: e.tensor_scalar_add(ropeT[1][:, :], ropeT[1][:, :],
                                                      8388608.0), ss=True)
            b.op("dve", lambda e: e.tensor_scalar_sub(ropeT[1][:, :], ropeT[1][:, :],
                                                      8388608.0), ss=True)
            ty = b.op("dve", lambda e: e.scalar_tensor_tensor(
                ropeT[2][:, :], ropeT[1][:, :], -TWO_PI, ropeT[0][:, :],
                ALU.mult, ALU.add), ss=True)
            if dbg_on and c == dbg_chunk and tt == 0:
                b.op("act", lambda e, cix=cix: e.copy(snap[:, 0:1], colS[:, cix:cix + 1]),
                     waits=[tan])
                b.op("act", lambda e: e.copy(snap[:, 8:16], ropeT[0][:, :]),
                     waits=[tan])
                b.op("act", lambda e: e.copy(snap[:, 16:24], ropeT[1][:, :]),
                     waits=[ty])
                b.op("act", lambda e: e.copy(snap[:, 24:32], ropeT[2][:, :]),
                     waits=[ty])
            tsin = b.op("act", lambda e: e.activation(ropeT[3][:, :], ropeT[2][:, :],
                                                      AF.Sin), waits=[ty])
            b.op("dve", lambda e: e.tensor_scalar_add(ropeT[4][:, :], ropeT[2][:, :],
                                                      PI / 2), ss=True)
            b.op("dve", lambda e: e.tensor_single_scalar(
                ropeT[5][:, :], ropeT[4][:, :], PI, ALU.is_gt), ss=True)
            tyc = b.op("dve", lambda e: e.scalar_tensor_tensor(
                ropeT[4][:, :], ropeT[5][:, :], -TWO_PI, ropeT[4][:, :],
                ALU.mult, ALU.add), ss=True)
            tcos = b.op("act", lambda e: e.activation(ropeT[5][:, :], ropeT[4][:, :],
                                                      AF.Sin), waits=[tyc])
            if dbg_on and c == dbg_chunk and tt == 0:
                tsnap = b.op("act", lambda e: e.copy(snap[:, 32:40], ropeT[3][:, :]))
                dbg("snap", snap[:, :], tsnap)

            for (srcv, dstv, tw) in ((bnB, BrTm, tb4), (bnC, CrTm, tc4)):
                b.op("dve", lambda e, srcv=srcv: e.tensor_tensor(
                    ropeT[0][:, :], srcv[:, 0:8], ropeT[5][:, :], ALU.mult),
                    waits=[tcos, tsin, tw, gd["ropePE"]], ss=True)
                b.op("dve", lambda e, srcv=srcv: e.tensor_tensor(
                    ropeT[1][:, :], srcv[:, 8:16], ropeT[3][:, :], ALU.mult),
                    ss=True)
                b.op("dve", lambda e, dstv=dstv: e.tensor_tensor(
                    dstv[:, 0:8], ropeT[0][:, :], ropeT[1][:, :], ALU.subtract),
                    ss=True)
                b.op("dve", lambda e, srcv=srcv: e.tensor_tensor(
                    ropeT[0][:, :], srcv[:, 0:8], ropeT[3][:, :], ALU.mult),
                    ss=True)
                b.op("dve", lambda e, srcv=srcv: e.tensor_tensor(
                    ropeT[1][:, :], srcv[:, 8:16], ropeT[5][:, :], ALU.mult),
                    ss=True)
                b.op("dve", lambda e, dstv=dstv: e.tensor_tensor(
                    dstv[:, 8:16], ropeT[0][:, :], ropeT[1][:, :], ALU.add),
                    ss=True)
            b.op("dve", lambda e: e.tensor_tensor(sqS[:, :], BrTm[:, :], CrTm[:, :],
                                                  ALU.mult), ss=True)
            tq2 = b.op("dve", lambda e: e.reduce_sum(qcol[:, 0:1], sqS[:, :],
                                                     axis=AXX), ss=True)
            if dbg_on and c == dbg_chunk and tt == 0:
                tsb = b.op("dve", lambda e: e.tensor_copy(snap[:, 40:56], BrTm[:, :]))
                b.op("act", lambda e: e.copy(snap[:, 56:64], ropeT[5][:, :]),
                     waits=[tsb])
                tsb2 = b.op("act", lambda e: e.copy(snap[:, 4:5], qcol[:, 0:1]), waits=[tsb])
                dbg("snapBr", snap[:, :], tsb2)
            if c == dbg_chunk and tt == 0:
                dbg("sfreqRep", sfreqRep[:, :], tq2)
                dbg("angT", ropeT[2][:, :], ty)
                dbg("BrTm", BrTm[:, :], tq2)
                dbg("CrTm", CrTm[:, :], tq2)
                dbg("bnB", bnB[:, :], tq2)
                dbg("sinT", ropeT[3][:, :], tq2)
                dbg("cosT", ropeT[5][:, :], tq2)
                dbg("qcolD", qcol[:, :], tq2)
            rope_evs.append(tf32(BrTm[:, :], BrTf[par][0:16, tt * 128:(tt + 1) * 128],
                                 [tq2, gd["BrTf_free"][par]], 128, 16))
            rope_evs.append(tf32(CrTm[:, :], CrTf[par][0:16, tt * 128:(tt + 1) * 128],
                                 [tq2], 128, 16))
            rope_evs.append(tf32(qcol[:, 0:1], qrow[0:1, tt * 128:(tt + 1) * 128],
                                 [tq2], 128, 1))
            gd["ropePE"] = ("pe", b.cnt["pe"])
        gd["rope_dve"] = ("dve", b.cnt["dve"])
        gd["cum_pe"][par] = ("pe", b.cnt["pe"])
        mt = b.pe_mm("pb4", rope_evs, lambda e: e.matmul(
            pb[4][:, 0:CH], ones1[0:1, :], qrow[0:1, :], start=True, stop=True))
        tqr = b.evac("pb4", "act", lambda e: e.copy(qRep[par][:, :], pb[4][:, 0:CH]),
                     waits=[mt, gd["qRep_free"][par]])
        gd["rope_tok"][par] = tqr
        if c == dbg_chunk:
            dbg("cumRow", cumRow[par][:, :], tcum)
            dbg("BrTf", BrTf[par][:, :], tqr)
            dbg("CrTf", CrTf[par][:, :], tqr)
            dbg("qRep", qRep[par][:, :], tqr)

    def patch(cprev, clast=False):
        par = cprev % 2
        npar = 1 - par
        for et in range(8):
            if not clast:
                b.op("dve", lambda e, et=et: e.tensor_tensor(
                    ppB[:, et:et + 1], dtO[npar][:, et, 0:1], olamRep[:, 0:1],
                    ALU.mult), waits=[gd["dt_tok"][npar]], ss=True)
                b.op("dve", lambda e, et=et: e.tensor_tensor(
                    w0_bf[par][:, et, CH - 1:CH], xp_bf[par][:, et, CH - 1:CH],
                    ppB[:, et:et + 1], ALU.mult),
                    waits=[gd["scan_po"][par]], ss=True)
                b.op("dve", lambda e, et=et: e.tensor_tensor(
                    ppB[:, et:et + 1], ldtLast[par][:, et:et + 1], ppB[:, et:et + 1],
                    ALU.add), ss=True)
                b.op("dve", lambda e, et=et: e.tensor_tensor(
                    wc_bf[par][:, et, CH - 1:CH], xp_bf[par][:, et, CH - 1:CH],
                    ppB[:, et:et + 1], ALU.mult),
                    waits=[gd["scan_po"][par]], ss=True)
            else:
                b.op("dve", lambda e, et=et: e.tensor_tensor(
                    wc_bf[par][:, et, CH - 1:CH], xp_bf[par][:, et, CH - 1:CH],
                    ldtLast[par][:, et:et + 1], ALU.mult),
                    waits=[gd["scan_po"][par]], ss=True)
                b.op("dve", lambda e, et=et: e.memset(
                    w0_bf[par][:, et, CH - 1:CH], 0.0),
                    waits=[gd["scan_po"][par]], ss=True)
        tok = b.op("dve", lambda e: e.engine_nop())
        gd["wc_patch"][par] = tok
        gd["w0_patch"][par] = tok

    def scan(c):
        par = c % 2
        dA_guard = gd["dA_guard"]
        h_guard = gd["h_guard"]
        tmp_guard = gd["tmp_guard"]
        prev_u = gd["prev_u"]
        ys_tok = {}
        prev2 = gd["ys_last2"]
        crb_tok = [None, None]
        pend = None

        def cr_toks(s):
            yy = (prev2[s] if (s < 2 and prev2) else
                  ys_tok.get(s - 2))
            return (yy[0], yy[1]) if yy else (None, None)

        def emit_y(p):
            s, et, sl, tsc2 = p
            cb = CrB2[s % 2]
            if s == 0:
                ty = b.op("dve", lambda e, et=et, sl=sl, cb=cb: e.tensor_tensor(
                    yacc[:, et, :], h_t[sl][:, :], cb[:, :], ALU.mult),
                    waits=[crb_tok[s % 2], gd["ypost_dve"][par]])
                h_guard[sl] = ty
                ys_tok[s] = [ty, None]
            else:
                tm = b.op("po", lambda e, sl=sl, cb=cb: e.tensor_tensor(
                    tmp_t[sl][:, :], h_t[sl][:, :], cb[:, :], ALU.mult),
                    waits=[tsc2, crb_tok[s % 2], tmp_guard[sl]])
                h_guard[sl] = tm
                ty = b.op("dve", lambda e, et=et, sl=sl: e.tensor_tensor(
                    yacc[:, et, :], yacc[:, et, :], tmp_t[sl][:, :], ALU.add),
                    waits=[tm])
                tmp_guard[sl] = ty
                ys_tok[s] = [ty, tm]
            b.op("act", lambda e, sl=sl, col=s * 8 + et: e.copy(
                carryA[:, col:col + 1], h_t[sl][:, CH - 1:CH]), waits=[tsc2])

        for s in range(S):
            mtb = b.pe_mm("pb4", [gd["rope_tok"][par], prev_u], lambda e, s=s:
                          e.matmul(pb[4][:, 0:CH], selAll[:, s, :], BrTf[par][:, :],
                                   start=True, stop=True))
            tBrB = b.evac("pb4", "act", lambda e: e.copy(BrB[:, :], pb[4][:, 0:CH]),
                          waits=[mtb, prev_u])
            y2d, y2p = cr_toks(s)
            mtc = b.pe_mm("pb2", [gd["rope_tok"][par], y2d, y2p],
                          lambda e, s=s: e.matmul(pb[2][:, 0:CH], selAll[:, s, :],
                                                  CrTf[par][:, :],
                                                  start=True, stop=True))
            tCrB = b.evac("pb2", "act", lambda e, s=s: e.copy(
                CrB2[s % 2][:, :], pb[2][:, 0:CH]), waits=[mtc, y2d, y2p])
            crb_tok[s % 2] = tCrB
            for et in range(8):
                it = s * 8 + et
                sl = it % 2
                tda = b.op("act", lambda e, s=s, et=et, sl=sl: e.activation(
                    dA_t[sl][:, :], dtO[par][:, et, :], AF.Exp,
                    scale=A_sb[:, et * 16 + s: et * 16 + s + 1]),
                    waits=[dA_guard[sl], gd["dt_tok"][par], tok_A])
                tu = b.op("po", lambda e, et=et, sl=sl: e.tensor_tensor(
                    u_t[sl][:, :], wc_bf[par][:, et, :], BrB[:, :], ALU.mult),
                    waits=[tBrB, gd["wc_patch"][par], dA_guard[sl]])
                tsc2 = b.op("dve", lambda e, sl=sl, col=it: e.tensor_tensor_scan(
                    h_t[sl][:, :], dA_t[sl][:, :], u_t[sl][:, :],
                    carryA[:, col:col + 1], ALU.mult, ALU.add),
                    waits=[tda, tu, h_guard[sl]])
                dA_guard[sl] = tsc2
                if pend is not None:
                    emit_y(pend)
                pend = (s, et, sl, tsc2)
                if et == 7:
                    prev_u = tu
        emit_y(pend)
        gd["prev_u"] = prev_u
        gd["ys_last2"] = [ys_tok[S - 2], ys_tok[S - 1]]
        gd["dA_read"][par] = ("act", b.cnt["act"])
        gd["scan_dve"][par] = ("dve", b.cnt["dve"])
        gd["scan_po"][par] = ("po", b.cnt["po"])
        if c == dbg_chunk:
            dbg("yacc", yacc[:, :, :], ("dve", b.cnt["dve"]))
            dbg("wc", wc_bf[par][:, :, :], gd["wc_patch"][par])
            dbg("w0", w0_bf[par][:, :, :], gd["w0_patch"][par])
            dbg("carry", carryA[:, :], ("act", b.cnt["act"]))

    def ypost_wout(c):
        par = c % 2
        for et in range(8):
            b.op("dve", lambda e, et=et: e.tensor_tensor(
                yp1[:, :], w0_bf[par][:, et, :], qRep[par][:, :], ALU.mult),
                waits=[gd["w0_patch"][par], gd["rope_tok"][par], ld0])
            b.op("dve", lambda e, et=et: e.tensor_tensor(
                yp2[:, :], yacc[:, et, :], yp1[:, :], ALU.subtract))
            b.op("dve", lambda e, et=et: e.scalar_tensor_tensor(
                yp1[:, :], xp_bf[par][:, et, :], dColS[:, et:et + 1], yp2[:, :],
                ALU.mult, ALU.add))
            b.op("dve", lambda e, et=et: e.tensor_tensor(
                Y3[:, et, :], yp1[:, :], sz_bf[par][:, et, :], ALU.mult),
                waits=[gd["sz_tok"].get(c), gd["Y3_free"]])
        typ = b.op("dve", lambda e: e.engine_nop())
        gd["ypost_dve"][par] = typ
        gd["xp_free"][par] = typ
        gd["sz_free"][par] = typ
        gd["wc_free"][par] = typ
        gd["w0_free"][par] = typ
        gd["dtO_free"][par] = typ
        gd["qRep_free"][par] = typ
        gd["sz_tok"].pop(c, None)

        evs = []
        for mo in range(8):
            bankname = "pb2" if mo % 2 == 0 else "pb3"
            bank = pb[2] if mo % 2 == 0 else pb[3]

            def mmf(e, mo=mo, bank=bank):
                for kc in range(8):
                    r = e.matmul(bank[:, 0:CH], w_out_bf[:, kc, mo * 128:(mo + 1) * 128],
                                 Y3[:, kc, :], start=(kc == 0), stop=(kc == 7))
                return r
            mt = b.pe_mm(bankname, [typ, w_cast_done], mmf)
            ev = b.evac(bankname, "act", lambda e, mo=mo, bank=bank: e.copy(
                attnP[:, mo, :], bank[:, 0:CH]), waits=[mt, gd["attnP_free"]])
            evs.append(ev)
        gd["Y3_free"] = ("pe", b.cnt["pe"])
        gd["BrTf_free"][par] = ("pe", b.cnt["pe"])
        if c == dbg_chunk:
            dbg("Y3", Y3[:, :, :], gd["ypost_dve"][par])
            dbg("attnP", attnP[:, :, :], evs[-1])
            dbg("szD", sz_bf[par][:, :, :], gd["ypost_dve"][par])
        tst = b.dma("po", rs_in[(c % 2) * 128:(c % 2) * 128 + 128, :],
                    attnP.ap().rearrange("p a b -> p (a b)"),
                    waits=evs + [gd["rs_next"]], sem="st")
        gd["attnP_free"] = tst

    def rs_xnew(c):
        oc = c // 2
        tcc = b.op("po", lambda e: e.collective_compute(
            "ReduceScatter", ALU.add, replica_groups=PAIRS,
            ins=[rs_in], outs=[rs_out]),
            waits=[("st", b.cnt["st"]), ("rsl", b.cnt["rsl"])], inc=("cc", 1))
        gd["rs_next"] = tcc
        tlb = b.dma("sp", attnS.ap().rearrange("p a b -> p (a b)"), rs_out,
                    waits=[tcc, gd["attnS_free"]], sem="rsl")
        if oc == 0:
            dbg("attnS", attnS[:, :, :], tlb)
        prev_store = None
        for tt in range(NTT):
            evl = []
            for dc in range(8):
                ev = tbf(attnS[:, dc, tt * 128:(tt + 1) * 128],
                         attn_tm[:, dc * 128:(dc + 1) * 128], [tlb])
                if ev is not None:
                    evl.append(ev)
            ev = flush_tbf()
            if ev is not None:
                evl.append(ev)
            sl = 0
            tl = b.dma("sp", xo_t[sl][:, :],
                       x_own[oc * CH + tt * 128: oc * CH + (tt + 1) * 128, :],
                       waits=[gd["xo_slot"][sl]], sem="xB")
            b.op("dve", lambda e: e.tensor_tensor(xn_t[:, :], attn_tm[:, :],
                                                  gaRep[:, :], ALU.mult),
                 waits=evl + [tl, prev_store, setup_act])
            txn = b.op("dve", lambda e, sl=sl: e.tensor_tensor(
                xn_t[:, :], xo_t[sl][:, :], xn_t[:, :], ALU.add))
            if dbg_on and oc == 0 and tt == 0:
                dbg("gaRep", gaRep[:, :], txn)
                dbg("saRep", saRep[:, :], txn)
                dbg("caRep", caRep[:, :], txn)
                t1_ = b.op("dve", lambda e: e.tensor_copy(snapT[:, :], attn_tm[:, :]))
                t2_ = b.op("dve", lambda e, sl=sl: e.tensor_copy(snapX[:, :], xo_t[sl][:, :]))
                dbg("attn_tm0", snapT[:, :], t1_)
                dbg("xo0", snapX[:, :], t2_)
            gd["xo_slot"][sl] = txn
            prev_store = b.dma("po", xnew_d[oc * CH + tt * 128: oc * CH + (tt + 1) * 128, :],
                               xn_t[:, :], waits=[txn], sem="st")
        gd["attnS_free"] = ("pe", b.cnt["pe"])

    if n_chunks > 0:
        front(0)
    for c in range(n_chunks):
        if c + 1 < n_chunks:
            front(c + 1)
            patch(c)
        else:
            patch(c, clast=True)
        if not do_scan:
            continue
        scan(c)
        ypost_wout(c)
        if c % 2 == 1:
            rs_xnew(c)

    mamba_done = {e: (e, b.cnt[e]) for e in ("pe", "act", "dve", "po")}
    mamba_st = ("st", b.cnt["st"])
    es_m.close()
    b.barrier()

    es_f = ExitStack()
    fx_t = [b.sb(f"fx{i}", [128, D], F32, es_f) for i in range(2)]
    fh_tm = b.sb("fh_tm", [128, D], BF16, es_f)
    h2_fm = b.sb("h2_fm", [128, 8, FTOK], BF16, es_f)
    flnc = b.sb("flnc", [128, D], F32, es_f)
    flncol = b.sb("flncol", [128, 4], F32, es_f)
    flnsq = b.sb("flnsq", [128, D], F32, es_f)
    logitsF = b.sb("logitsF", [8, FTOK], F32, es_f)
    lg_t = b.sb("lg_t", [128, 8], F32, es_f)
    top_t = [b.sb(f"top{i}", [128, 8], F32, es_f) for i in range(2)]
    topc = b.sb("topc", [128, 4], F32, es_f)
    combA = b.sb("combA", [128, 8, 8], F32, es_f)
    combT = b.sb("combT", [8, FTOK], BF16, es_f)
    cRep = b.sb("cRep", [128, FTOK], F32, es_f)
    wgStg = [b.sb(f"wgS{i}", [128, 8, 128], F32, es_f) for i in range(2)]
    wuStg = [b.sb(f"wuS{i}", [128, 8, 128], F32, es_f) for i in range(2)]
    wgB = [b.sb(f"wgB{i}", [128, 8, 128], BF16, es_f) for i in range(2)]
    wuB = [b.sb(f"wuB{i}", [128, 8, 128], BF16, es_f) for i in range(2)]
    wdStg = [b.sb(f"wdS{i}", [128, HM, 128], F32, es_f) for i in range(2)]
    wdB = [b.sb("wdB0", [128, HM, 128], BF16, es_f)] * 2
    sg_t = [b.sb(f"sg{i}", [128, 512], BF16, es_f) for i in range(2)]
    hgAll = b.sb("hgAll", [128, HM, FTOK], BF16, es_f)
    acc = b.sb("acc", [128, 8, FTOK], F32, es_f)
    dtmp = [b.sb(f"dtmp{i}", [128, 512], F32, es_f) for i in range(2)]
    acc_tm = b.sb("acc_tm", [128, D], F32, es_f)
    fo_t = b.sb("fo_t", [128, D], F32, es_f)

    fgd = {"fx": [None, None], "h2fm_pe": None, "stg": {}, "acc_free": None,
           "hg_pe": None, "sg_guard": [None, None], "dtmp_guard": [None, None],
           "fsq_prev": None, "fo_store": None, "crep_dve": None, "acc_dve": None}

    def ffn_ln(xtile, out_bf, waits_x):
        b.op("dve", lambda e: e.reduce_sum(flncol[:, 0:1], xtile[:, :], axis=AXX),
             waits=list(waits_x) + [fgd["fsq_prev"]])
        b.op("dve", lambda e: e.tensor_scalar_mul(flncol[:, 1:2], flncol[:, 0:1],
                                                  1.0 / D), ss=True)
        tc = b.op("dve", lambda e: e.tensor_scalar(flnc[:, :], xtile[:, :],
                                                   flncol[:, 1:2], None, ALU.subtract),
                  ss=True)
        tq = b.op("act", lambda e: e.activation(flnsq[:, :], flnc[:, :], AF.Square,
                                                accum_out=flncol[:, 2:3]), waits=[tc])
        fgd["fsq_prev"] = tq
        b.op("act", lambda e: e.activation(flncol[:, 3:4], flncol[:, 2:3], AF.Ln,
                                           scale=1.0 / D, bias=LN_EPS), ss=True)
        tr = b.op("act", lambda e: e.activation(flncol[:, 3:4], flncol[:, 3:4],
                                                AF.Exp, scale=-0.5), ss=True)
        b.op("dve", lambda e: e.tensor_scalar(flnc[:, :], flnc[:, :], flncol[:, 3:4],
                                              None, ALU.mult), waits=[tr])
        b.op("dve", lambda e: e.tensor_tensor(flnsq[:, :], flnc[:, :], cbRep[:, :],
                                              ALU.mult))
        return b.op("dve", lambda e: e.tensor_tensor(out_bf[:, :], flnsq[:, :],
                                                     sbRep[:, :], ALU.add))

    for fb in range(min(2 if do_ffn else 0, int(os.environ.get('MK_NFB', 2)))):
        base_tok = fb * FTOK
        th2 = None
        for tt in range(8):
            sl = tt % 2
            sname = "xnA" if sl == 0 else "xnB"
            extra = [mamba_done["dve"], mamba_done["po"], mamba_done["act"],
                     mamba_done["pe"], mamba_st] if (fb == 0 and tt < 2) else []
            tl = b.dma("sp", fx_t[sl][:, :],
                       xnew_d[base_tok + tt * 128: base_tok + (tt + 1) * 128, :],
                       waits=[fgd["fx"][sl]] + extra, sem=sname)
            tln = ffn_ln(fx_t[sl], fh_tm, [tl, ("pe", b.cnt["pe"])])
            fgd["fx"][sl] = tln
            for dc in range(8):
                tbf(fh_tm[:, dc * 128:(dc + 1) * 128],
                    h2_fm[:, dc, tt * 128:(tt + 1) * 128],
                    [tln, fgd["h2fm_pe"]])
            th2 = flush_tbf()
        th2 = None

        lg_evs = []
        for nt in range(FNT):
            def rmm(e, nt=nt):
                for kc in range(8):
                    r = e.matmul(pb[5][0:8, 0:512], router_bf[:, kc, :],
                                 h2_fm[:, kc, nt * 512:(nt + 1) * 512],
                                 start=(kc == 0), stop=(kc == 7))
                return r
            mt = b.pe_mm("pb5", [tb_state["guard"][0], tb_state["guard"][1]], rmm)
            lg_evs.append(b.evac("pb5", "act", lambda e, nt=nt: e.copy(
                logitsF[0:8, nt * 512:(nt + 1) * 512], pb[5][0:8, 0:512]),
                waits=[mt]))
        comb_evs = []
        for tt in range(8):
            tT = tf32(logitsF[0:8, tt * 128:(tt + 1) * 128], lg_t[:, 0:8],
                      lg_evs + comb_evs, 8, 128, eng="dve")
            b.op("dve", lambda e: e.reduce_max(topc[:, 0:1], lg_t[:, :], axis=AXX),
                 waits=[tT], ss=True)
            b.op("dve", lambda e: e.tensor_scalar(top_t[0][:, :], lg_t[:, :],
                                                  topc[:, 0:1], None, ALU.is_equal),
                 ss=True)
            b.op("dve", lambda e: e.scalar_tensor_tensor(
                top_t[1][:, :], top_t[0][:, :], -1e30, lg_t[:, :], ALU.mult, ALU.add),
                ss=True)
            b.op("dve", lambda e: e.reduce_max(topc[:, 1:2], top_t[1][:, :], axis=AXX),
                 ss=True)
            tdd = b.op("dve", lambda e: e.tensor_tensor(topc[:, 2:3], topc[:, 1:2],
                                                        topc[:, 0:1], ALU.subtract),
                       ss=True)
            tex = b.op("act", lambda e: e.activation(topc[:, 2:3], topc[:, 2:3],
                                                     AF.Exp), waits=[tdd])
            b.op("dve", lambda e: e.tensor_scalar_add(topc[:, 3:4], topc[:, 2:3], 1.0),
                 waits=[tex], ss=True)
            b.op("dve", lambda e: e.reciprocal(topc[:, 3:4], topc[:, 3:4]), ss=True)
            b.op("dve", lambda e: e.tensor_scalar(topc[:, 2:3], topc[:, 3:4], -1.0,
                                                  1.0, ALU.mult, ALU.add), ss=True)
            b.op("dve", lambda e: e.tensor_scalar(top_t[1][:, :], top_t[1][:, :],
                                                  topc[:, 1:2], None, ALU.is_equal),
                 ss=True)
            b.op("dve", lambda e: e.tensor_scalar(top_t[0][:, :], top_t[0][:, :],
                                                  topc[:, 3:4], None, ALU.mult),
                 ss=True)
            tcb = b.op("dve", lambda e, tt=tt: e.scalar_tensor_tensor(
                combA[:, tt, :], top_t[1][:, :], topc[:, 2:3], top_t[0][:, :],
                ALU.mult, ALU.add), ss=True)
            comb_evs.append(tf32(combA[:, tt, :],
                                 combT[0:8, tt * 128:(tt + 1) * 128],
                                 [tcb], 128, 8, eng="dve"))

        for e_i in range(int(os.environ.get('MK_NE', NE)) if ffn_lvl >= 2 else 0):
            crep_evs = []
            for nt in range(FNT):
                mt = b.pe_mm("pb4", comb_evs, lambda e, e_i=e_i, nt=nt: e.matmul(
                    pb[4][:, 0:512], selAll[0:8, e_i, :],
                    combT[:, nt * 512:(nt + 1) * 512], start=True, stop=True))
                crep_evs.append(b.evac("pb4", "act", lambda e, nt=nt: e.copy(
                    cRep[:, nt * 512:(nt + 1) * 512], pb[4][:, 0:512]),
                    waits=[mt, fgd["crep_dve"]]))

            hg_last = None
            for m in range(int(os.environ.get('MK_HM', HM)) if ffn_lvl >= 3 else 0):
                sl = m % 2
                sgn = "fgA" if sl == 0 else "fgB"
                sun = "fuA" if sl == 0 else "fuB"
                tlg = b.dma("sp", wgStg[sl][:, :, :], wg_pm[e_i, m],
                            waits=[fgd["stg"].get(("wg", sl))], sem=sgn)
                tlu = b.dma("sp", wuStg[sl][:, :, :], wu_pm[e_i, m],
                            waits=[fgd["stg"].get(("wu", sl))], sem=sun)
                tcg = b.op("po", lambda e, sl=sl: e.tensor_copy(
                    wgB[sl][:, :, :], wgStg[sl][:, :, :]),
                    waits=[tlg, fgd["stg"].get(("wgB", sl))])
                tcu = b.op("act", lambda e, sl=sl: e.copy(
                    wuB[sl][:, :, :], wuStg[sl][:, :, :]),
                    waits=[tlu, fgd["stg"].get(("wuB", sl))])
                fgd["stg"][("wg", sl)] = tcg
                fgd["stg"][("wu", sl)] = tcu
                if dbg_on and fb == 0 and e_i == 0 and m == 0:
                    dbg("wgB0", wgB[0][:, :, :], tcg)
                    dbg("wgS0", wgStg[0][:, :, :], tcg)
                MK_SUB = int(os.environ.get('MK_SUB', 5))
                for nt in range(FNT if MK_SUB >= 2 else 0):
                    gb_i = (m * FNT + nt) % 2
                    gname, gbank = ("pb0", pb[0]) if gb_i == 0 else ("pb2", pb[2])
                    uname, ubank = ("pb1", pb[1]) if gb_i == 0 else ("pb3", pb[3])

                    def gmm(e, sl=sl, nt=nt, gbank=gbank):
                        for kc in range(8):
                            r = e.matmul(gbank[:, :], wgB[sl][:, kc, :],
                                         h2_fm[:, kc, nt * 512:(nt + 1) * 512],
                                         start=(kc == 0), stop=(kc == 7))
                        return r
                    mtg = b.pe_mm(gname, [tcg, tb_state["guard"][0], tb_state["guard"][1]], gmm)

                    def umm(e, sl=sl, nt=nt, ubank=ubank):
                        for kc in range(8):
                            r = e.matmul(ubank[:, :], wuB[sl][:, kc, :],
                                         h2_fm[:, kc, nt * 512:(nt + 1) * 512],
                                         start=(kc == 0), stop=(kc == 7))
                        return r
                    if MK_SUB < 3:
                        continue
                    mtu = b.pe_mm(uname, [tcu, tb_state["guard"][0], tb_state["guard"][1]], umm)
                    ssl = (m * FNT + nt) % 2
                    if MK_SUB < 4:
                        continue
                    tsg2 = b.evac(gname, "act", lambda e, ssl=ssl, gbank=gbank:
                                  e.activation(sg_t[ssl][:, :], gbank[:, :], AF.Silu),
                                  waits=[mtg, fgd["sg_guard"][ssl]])
                    if MK_SUB < 5:
                        continue
                    if dbg_on and fb == 0 and e_i == 0 and m == 0 and nt == 0:
                        ts1_ = b.op("act", lambda e, ssl=ssl: e.copy(
                            dtmp[1][:, :], sg_t[ssl][:, :]), waits=[tsg2])
                        ts2_ = b.op("dve", lambda e, ubank=ubank: e.tensor_copy(
                            dtmp[0][:, :], ubank[:, :]), waits=[mtu])
                        dbg("sg0", dtmp[1][:, :], ts1_)
                        dbg("up0", dtmp[0][:, :], ts2_)
                        b.bank[uname] = ts2_
                        fgd["dtmp_guard"][0] = ("st", b.cnt["st"])
                        fgd["dtmp_guard"][1] = ("st", b.cnt["st"])
                    thg = b.evac(uname, "dve",
                                 lambda e, ssl=ssl, m=m, nt=nt, ubank=ubank:
                                 e.tensor_tensor(
                                     hgAll[:, m, nt * 512:(nt + 1) * 512],
                                     sg_t[ssl][:, :], ubank[:, :], ALU.mult),
                                 waits=[mtu, tsg2, fgd["hg_pe"]])
                    fgd["sg_guard"][ssl] = thg
                    hg_last = thg
                fgd["stg"][("wgB", sl)] = ("pe", b.cnt["pe"])
                fgd["stg"][("wuB", sl)] = ("pe", b.cnt["pe"])
                if m == HM - 1:
                    fgd["stg"][("wgB", 1 - sl)] = ("pe", b.cnt["pe"])
                    fgd["stg"][("wuB", 1 - sl)] = ("pe", b.cnt["pe"])

            for dm in range(8 if ffn_lvl >= 4 else 0):
                sl = dm % 2
                sdn = "fdA" if sl == 0 else "fdB"
                tld = b.dma("sp", wdStg[sl][:, :, :], wd_pm[e_i, dm],
                            waits=[fgd["stg"].get(("wd", sl))], sem=sdn)
                tcd = b.op("dve", lambda e, sl=sl: e.tensor_copy(
                    wdB[sl][:, :, :], wdStg[sl][:, :, :]),
                    waits=[tld, fgd["stg"].get("wdB")])
                fgd["stg"][("wd", sl)] = tcd
                for nt in range(FNT):
                    bankname = "pb2" if (dm * FNT + nt) % 2 == 0 else "pb3"
                    bank = pb[2] if (dm * FNT + nt) % 2 == 0 else pb[3]

                    def dmm(e, sl=sl, nt=nt, bank=bank):
                        for kc in range(HM):
                            r = e.matmul(bank[:, :], wdB[sl][:, kc, :],
                                         hgAll[:, kc, nt * 512:(nt + 1) * 512],
                                         start=(kc == 0), stop=(kc == HM - 1))
                        return r
                    mtd = b.pe_mm(bankname, [tcd, hg_last], dmm)
                    if e_i == 0:
                        b.evac(bankname, "dve",
                               lambda e, dm=dm, nt=nt, bank=bank: e.tensor_tensor(
                                   acc[:, dm, nt * 512:(nt + 1) * 512], bank[:, :],
                                   cRep[:, nt * 512:(nt + 1) * 512], ALU.mult),
                               waits=[mtd, fgd["acc_free"]] + crep_evs)
                    else:
                        tsl = (dm * FNT + nt) % 2
                        ev = b.evac(bankname, "dve",
                                    lambda e, tsl=tsl, nt=nt, bank=bank:
                                    e.tensor_tensor(dtmp[tsl][:, :], bank[:, :],
                                                    cRep[:, nt * 512:(nt + 1) * 512],
                                                    ALU.mult),
                                    waits=[mtd, fgd["dtmp_guard"][tsl]] + crep_evs)
                        fgd["dtmp_guard"][tsl] = b.op(
                            "dve", lambda e, dm=dm, nt=nt, tsl=tsl: e.tensor_tensor(
                                acc[:, dm, nt * 512:(nt + 1) * 512],
                                acc[:, dm, nt * 512:(nt + 1) * 512],
                                dtmp[tsl][:, :], ALU.add), waits=[ev])
                fgd["stg"]["wdB"] = ("pe", b.cnt["pe"])
            fgd["hg_pe"] = ("pe", b.cnt["pe"])
            fgd["crep_dve"] = ("dve", b.cnt["dve"])
        tacc = ("dve", b.cnt["dve"])

        for tt in range(8 if ffn_lvl >= 5 else 0):
            evl = []
            for dc in range(8):
                evl.append(tf32(acc[:, dc, tt * 128:(tt + 1) * 128],
                                acc_tm[:, dc * 128:(dc + 1) * 128],
                                [tacc, fgd["acc_dve"]], 128, 128))
            sl = tt % 2
            sname = "xnA" if sl == 0 else "xnB"
            tl = b.dma("sp", fx_t[sl][:, :],
                       xnew_d[base_tok + tt * 128: base_tok + (tt + 1) * 128, :],
                       waits=[fgd["fx"][sl]], sem=sname)
            b.op("dve", lambda e: e.tensor_tensor(fo_t[:, :], acc_tm[:, :],
                                                  gbRep[:, :], ALU.mult),
                 waits=evl + [tl, fgd["fo_store"]])
            tfo = b.op("dve", lambda e, sl=sl: e.tensor_tensor(
                fo_t[:, :], fx_t[sl][:, :], fo_t[:, :], ALU.add))
            fgd["acc_dve"] = tfo
            fgd["fx"][sl] = tfo
            fgd["fo_store"] = b.dma(
                "po", out_t[base_tok + tt * 128: base_tok + (tt + 1) * 128, :],
                fo_t[:, :], waits=[tfo], sem="st")
        if dbg_on and fb == 0:
            dbg("cRep", cRep[:, :], ("act", b.cnt["act"]))
            dbg("logitsF", logitsF[:, :], ("act", b.cnt["act"]))
            dbg("combT", combT[:, :], ("dve", b.cnt["dve"]))
            dbg("acc", acc[:, :, :], ("dve", b.cnt["dve"]))
            dbg("hgAll", hgAll[:, :, :], ("dve", b.cnt["dve"]))
            dbg("h2fm", h2_fm[:, :, :], ("dve", b.cnt["dve"]))
        fgd["acc_free"] = ("pe", b.cnt["pe"])
        fgd["h2fm_pe"] = ("pe", b.cnt["pe"])

    b.op("po", lambda e: e.engine_nop(), waits=[("st", b.cnt["st"])])
    es_f.close()

    blk = b.es.enter_context(nc.Block())

    def runner(qn):
        def body(eng):
            for cl in b.q[qn]:
                cl(eng)
        return body

    blk.sync(runner("sp"))
    blk.tensor(runner("pe"))
    blk.scalar(runner("act"))
    blk.vector(runner("dve"))
    blk.gpsimd(runner("po"))
    b.es.close()
    return b


_prog = {}


def get_prog():
    if "b" not in _prog:
        _prog["b"] = build_program()
    return _prog["b"]


def kernel(L0_attn_x, L0_attn_W_in, L0_attn_W_x, L0_attn_W_dt, L0_attn_b_dt,
           L0_attn_A_log, L0_attn_B_bias, L0_attn_C_bias, L0_attn_B_norm_w,
           L0_attn_C_norm_w, L0_attn_rope_freqs, L0_attn_D, L0_attn_W_out,
           L0_ffn_router_w, L0_ffn_wg, L0_ffn_wu, L0_ffn_wd, L0_c, L0_wmod):
    f32 = np.float32
    x = np.ascontiguousarray(np.asarray(L0_attn_x, f32))
    W_in = np.asarray(L0_attn_W_in, f32)
    W_x = np.asarray(L0_attn_W_x, f32)
    W_dt = np.asarray(L0_attn_W_dt, f32).reshape(-1)
    b_dt = np.asarray(L0_attn_b_dt, f32).reshape(-1)
    A_log = np.asarray(L0_attn_A_log, f32)
    Dv = np.asarray(L0_attn_D, f32).reshape(-1)
    W_out = np.asarray(L0_attn_W_out, f32)
    router = np.asarray(L0_ffn_router_w, f32)
    wg = np.asarray(L0_ffn_wg, f32)
    wu = np.asarray(L0_ffn_wu, f32)
    wd = np.asarray(L0_ffn_wd, f32)
    cmat = np.asarray(L0_c, f32)
    wmod = np.asarray(L0_wmod, f32)

    def pm_kc(w):
        K, M = w.shape
        return np.ascontiguousarray(w.reshape(K // 128, 128, M).transpose(1, 0, 2))

    sperm = np.concatenate([np.arange(0, 16, 2), np.arange(1, 16, 2)])
    W_x = np.concatenate([W_x[sperm], W_x[16 + sperm], W_x[32:34]], 0)
    A_log = A_log[:, sperm]
    smalls_perm = sperm

    H = 2752
    wg_p = np.zeros((NE, 1024, HP), f32)
    wg_p[:, :, :H] = wg
    wu_p = np.zeros((NE, 1024, HP), f32)
    wu_p[:, :, :H] = wu
    wd_p = np.zeros((NE, HP, 1024), f32)
    wd_p[:, :H, :] = wd
    wg_pm = np.ascontiguousarray(
        wg_p.reshape(NE, 8, 128, HM, 128).transpose(0, 3, 2, 1, 4))
    wu_pm = np.ascontiguousarray(
        wu_p.reshape(NE, 8, 128, HM, 128).transpose(0, 3, 2, 1, 4))
    wd_pm = np.ascontiguousarray(
        wd_p.reshape(NE, HM, 128, 8, 128).transpose(0, 3, 2, 1, 4))
    router_pm = pm_kc(router)
    wmod_pm = pm_kc(wmod)
    smalls = np.concatenate([np.asarray(L0_attn_B_bias, f32)[sperm],
                             np.asarray(L0_attn_C_bias, f32)[sperm],
                             np.asarray(L0_attn_B_norm_w, f32)[sperm],
                             np.asarray(L0_attn_C_norm_w, f32)[sperm]]).reshape(1, 64)
    rfreq = np.asarray(L0_attn_rope_freqs, f32).reshape(1, 8)

    in_maps = []
    for core in range(8):
        bb, j = core // 2, core % 2
        own = slice(j * 1024, (j + 1) * 1024)
        W_sel = np.concatenate(
            [W_in[own], W_in[2048 + j * 1024: 2048 + (j + 1) * 1024]], 0)
        ordv = np.concatenate([np.arange(j * 1024, (j + 1) * 1024),
                               np.arange((1 - j) * 1024, (2 - j) * 1024)])
        in_maps.append({
            "xb": np.ascontiguousarray(x[bb]),
            "x_own": np.ascontiguousarray(
                x[bb].reshape(NCH, CH, D)[j::2].reshape(OWN_TOK, D)),
            "w_in_pm": pm_kc(np.ascontiguousarray(W_sel.T)),
            "w_out_pm": pm_kc(np.ascontiguousarray(W_out[:, own].T)),
            "w_x_pm": pm_kc(np.ascontiguousarray(W_x[:, own].T)),
            "wdt_pm": np.ascontiguousarray(W_dt[ordv].reshape(16, 128).T),
            "bdt_pm": np.ascontiguousarray(b_dt[ordv].reshape(16, 128).T),
            "a_pm": np.ascontiguousarray(
                A_log[own].reshape(8, 128, 16).transpose(1, 0, 2).reshape(128, 128)),
            "d_pm": np.ascontiguousarray(Dv[own].reshape(8, 128).T),
            "smalls": smalls, "rfreq": rfreq,
            "c_pm": np.ascontiguousarray(cmat[bb].reshape(8, 128).T),
            "wmod_pm": wmod_pm, "router_pm": router_pm,
            "wg_pm": wg_pm, "wu_pm": wu_pm, "wd_pm": wd_pm,
        })

    bobj = get_prog()
    res = run_bass_kernel_spmd(bobj.nc, in_maps, core_ids=list(range(8)))

    out = np.empty((4, T, D), np.float32)
    for core in range(8):
        bb, j = core // 2, core % 2
        o = np.asarray(res.results[core]["out"]).reshape(OWN_TOK, D)
        out[bb].reshape(NCH, CH, D)[j::2] = o.reshape(NCH // 2, CH, D)
    return out


# version 09
# speedup vs baseline: 1.2387x; 1.0347x over previous
import os
import numpy as np
from contextlib import ExitStack

import concourse.bass as bass
import concourse.mybir as mybir
from concourse.bass_utils import run_bass_kernel_spmd

F32 = mybir.dt.float32
BF16 = mybir.dt.bfloat16
I32 = mybir.dt.int32
AF = mybir.ActivationFunctionType
ALU = mybir.AluOpType
AXX = mybir.AxisListType.X

T, D = 4096, 1024
CH = 256
NCH = T // CH
NTT = CH // 128
S = 16
EO = 1024
HP = 2816
HM = HP // 128
NE = 8
FTOK = 1024
FNT = FTOK // 512
OWN_TOK = 2048
PAIRS = [[0, 1], [2, 3], [4, 5], [6, 7]]
RMS_EPS = float(np.finfo(np.float32).eps)
LN_EPS = 1e-5
PI = float(np.pi)
TWO_PI = float(2 * np.pi)
INV_2PI = float(1.0 / (2 * np.pi))

ENGS = ("sp", "pe", "act", "dve", "po")


class B:
    def __init__(self):
        self.nc = bass.Bass("TRN2", target_bir_lowering=False)
        self.es = ExitStack()
        self.q = {e: [] for e in ENGS}
        self.semh = {}
        self.cnt = {}
        self.lastw = {e: {} for e in ENGS}
        self.bank = {}

    def sem(self, name):
        self.semh[name] = self.es.enter_context(self.nc.semaphore(name))
        self.cnt[name] = 0

    def sb(self, name, shape, dt, stack=None):
        return (stack or self.es).enter_context(self.nc.sbuf_tensor(name, shape, dt))

    def ps(self, name, shape, dt):
        return self.es.enter_context(self.nc.psum_tensor(name, shape, dt))

    def op(self, eng, fn, waits=(), inc=None, ss=False):
        if ss:
            waits = list(waits) + [(eng, self.cnt[eng])]
        ws = []
        lw = self.lastw[eng]
        for w in waits:
            if w is None:
                continue
            s, v = w
            if lw.get(s, 0) >= v:
                continue
            ws.append((s, v))
            lw[s] = v
        if inc is None:
            inc = (eng, 1)
        tok = None
        if inc:
            sname, n = inc
            self.cnt[sname] += n
            tok = (sname, self.cnt[sname])
        semh = self.semh

        def closure(e, ws=tuple(ws), fn=fn, inc=inc):
            for s, v in ws:
                e.wait_ge(semh[s], v)
            if fn is None:
                return
            r = fn(e)
            if inc:
                r.then_inc(semh[inc[0]], inc[1])
        self.q[eng].append(closure)
        return tok

    def barrier(self):
        toks = [(e, self.cnt[e]) for e in ("pe", "act", "dve", "po")]
        for eng in ENGS:
            self.op(eng, None, waits=toks, inc=False)

    def dma(self, eng, out, in_, waits=(), sem="ld", **kw):
        return self.op(eng, lambda e: e.dma_start(out=out, in_=in_, **kw),
                       waits, inc=(sem, 16))

    def pe_mm(self, bankname, waits, fn):
        g = self.bank.get(bankname)
        return self.op("pe", fn, list(waits) + [g])

    def evac(self, bankname, eng, fn, waits=()):
        tok = self.op(eng, fn, waits)
        self.bank[bankname] = tok
        return tok


def build_program(n_chunks=None, do_ffn=None, do_scan=True, ffn_lvl=5):
    if n_chunks is None:
        n_chunks = int(os.environ.get("MK_NCHUNKS", NCH))
    if do_ffn is None:
        do_ffn = bool(int(os.environ.get("MK_FFN", 1)))
    b = B()
    nc = b.nc

    def din(name, shape, dt=F32):
        return nc.dram_tensor(name, shape, dt, kind="ExternalInput").ap()

    xb = din("xb", [T, D])
    x_own = din("x_own", [OWN_TOK, D])
    w_in_pm = din("w_in_pm", [128, 8, 2048])
    w_out_pm = din("w_out_pm", [128, 8, 1024])
    w_x_pm = din("w_x_pm", [128, 8, 34])
    wdt_pm = din("wdt_pm", [128, 16])
    bdt_pm = din("bdt_pm", [128, 16])
    a_pm = din("a_pm", [128, 128])
    d_pm = din("d_pm", [128, 8])
    smalls = din("smalls", [1, 64])
    rfreq = din("rfreq", [1, 8])
    c_pm = din("c_pm", [128, 8])
    wmod_pm = din("wmod_pm", [128, 8, 6144])
    router_pm = din("router_pm", [128, 8, 8])
    wg_pm = din("wg_pm", [NE, HM, 128, 8, 128])
    wu_pm = din("wu_pm", [NE, HM, 128, 8, 128])
    wd_pm = din("wd_pm", [NE, 8, 128, HM, 128])

    out_t = nc.dram_tensor("out", [OWN_TOK, D], F32, kind="ExternalOutput").ap()

    ag_in = nc.dram_tensor("ag_in", [34, CH], F32).ap()
    ag_out = nc.dram_tensor("ag_out", [68, CH], F32).ap()
    rs_in = nc.dram_tensor("rs_in", [256, 8 * CH], BF16).ap()
    rs_out = nc.dram_tensor("rs_out", [128, 8 * CH], BF16).ap()
    xnew_d = nc.dram_tensor("xnew_d", [OWN_TOK, D], F32, kind="ExternalOutput" if os.environ.get("MK_DEBUG") else "Internal").ap()

    dbg_on = bool(os.environ.get("MK_DEBUG"))
    dbg_chunk = int(os.environ.get("MK_DEBUG_CHUNK", 0))

    def dbg(name, ap_, wait):
        if not dbg_on:
            return
        t = nc.dram_tensor("dbg_" + name, list(ap_.shape), ap_.dtype,
                           kind="ExternalOutput").ap()
        b.dma("po", t, ap_, waits=[wait], sem="st")

    for s in ("ld", "st", "cc", "pe", "act", "dve", "po",
              "xA", "xB", "wiS", "agl", "rsl", "lrw",
              "fgA", "fgB", "fuA", "fuB", "fdA", "fdB", "xnA", "xnB"):
        b.sem(s)

    pb = [b.ps(f"pb{i}", [128, 512], F32) for i in range(6)]
    pT1 = b.ps("pT1", [128, 1024], BF16)
    pT2 = b.ps("pT2", [128, 1024], BF16)

    identF = b.sb("identF", [128, 128], F32)
    identB = b.sb("identB", [128, 128], BF16)
    onesCol = b.sb("onesCol", [128, 1], F32)
    ones1 = b.sb("ones1", [1, 128], F32)
    selAll = b.sb("selAll", [16, 16, 128], BF16)
    sfreqRep = b.sb("sfreqRep", [128, 8], F32)
    bRepB = b.sb("bRepB", [128, 16], F32)
    bRepC = b.sb("bRepC", [128, 16], F32)
    wRepB = b.sb("wRepB", [128, 16], F32)
    wRepC = b.sb("wRepC", [128, 16], F32)
    wdtCol = b.sb("wdtCol", [128, 16], F32)
    bdtCol = b.sb("bdtCol", [128, 16], F32)
    A_sb = b.sb("A_sb", [128, 128], F32)
    dColS = b.sb("dColS", [128, 8], F32)
    wx_bf = b.sb("wx_bf", [128, 8, 34], BF16)
    router_bf = b.sb("router_bf", [128, 8, 8], BF16)
    caRep = b.sb("caRep", [128, D], BF16)
    saRep = b.sb("saRep", [128, D], BF16)
    gaRep = b.sb("gaRep", [128, D], BF16)
    cbRep = b.sb("cbRep", [128, D], BF16)
    sbRep = b.sb("sbRep", [128, D], BF16)
    gbRep = b.sb("gbRep", [128, D], BF16)
    smallRow = b.sb("smallRow", [1, 64], F32)
    freqRow = b.sb("freqRow", [1, 8], F32)
    sfRow = b.sb("sfRow", [1, 8], F32)

    CONSTS = [LN_EPS, RMS_EPS, 1.0, -1.0, -0.5, 1.0 / 16, 1.0 / 1024, 1.0 / 2048,
              PI, PI / 2, TWO_PI, -TWO_PI, INV_2PI, 0.25, 2.0, 1e-4, -1e30, 0.0,
              0.5]
    constT = b.sb("constT", [128, 32], F32)
    for i, v in enumerate(CONSTS):
        b.op("po", lambda e, i=i, v=v: e.memset(constT[:, i:i + 1], float(v)),
             inc=False)
        nc.const_aps.register(F32, float(v), constT[:, i:i + 1])

    tb_state = {"pend": [], "bank": 0, "guard": [None, None], "setup": None}

    def flush_tbf():
        pend = tb_state["pend"]
        if not pend:
            return None
        bk = tb_state["bank"]
        bank = pT1 if bk == 0 else pT2
        bankname = "pT1" if bk == 0 else "pT2"
        tb_state["bank"] = 1 - bk
        tb_state["pend"] = []
        waits = [tb_state["setup"]]
        for (_, _, w) in pend:
            waits += w
        waits.append(tb_state["guard"][bk])

        def body(e, pend=tuple(pend), bank=bank):
            for i, (src, _, _) in enumerate(pend):
                r = e.transpose(bank[:, i * 128:(i + 1) * 128], src, identB[:, :])
            return r
        mt = b.pe_mm(bankname, waits, body)
        ev = None
        for i, (_, dst, _) in enumerate(pend):
            ev = b.op("dve", lambda e, i=i, dst=dst, bank=bank: e.tensor_copy(
                dst, bank[:, i * 128:(i + 1) * 128]), waits=[mt])
        tb_state["guard"][bk] = ev
        b.bank[bankname] = ev
        return ev

    def tbf(src_ap, dst_ap, waits):
        tb_state["pend"].append((src_ap, dst_ap, list(waits)))
        if len(tb_state["pend"]) == 4:
            return flush_tbf()
        return None

    def tf32(src_ap, dst_ap, waits, p_in, f_in, eng="act"):
        mt = b.pe_mm("pb5", list(waits) + [tb_state["setup"]], lambda e: e.transpose(
            pb[5][0:f_in, 0:p_in], src_ap, identF[0:p_in, 0:p_in]))
        if eng == "act":
            fn = lambda e: e.copy(dst_ap, pb[5][0:f_in, 0:p_in])
        else:
            fn = lambda e: e.tensor_copy(dst_ap, pb[5][0:f_in, 0:p_in])
        return b.evac("pb5", eng, fn, waits=[mt])

    es_setup = ExitStack()
    selF = b.sb("selF", [16, 16, 128], F32, es_setup)
    modRow = b.sb("modRow", [1, 6144], F32, es_setup)
    wmod_sb = [b.sb(f"wmod_sb{i}", [128, 8, 512], F32, es_setup) for i in range(2)]
    cColT = b.sb("cColT", [128, 8], F32, es_setup)
    scCol = b.sb("scCol", [128, 8], F32, es_setup)
    wxStg = b.sb("wxStg", [128, 8, 34], F32, es_setup)
    rtStg = b.sb("rtStg", [128, 8, 8], F32, es_setup)
    gtmpA = b.sb("gtmpA", [1, 1024], F32, es_setup)
    gtmpS = b.sb("gtmpS", [1, 1024], F32, es_setup)

    b.op("po", lambda e: e.iota(identF[:, :], pattern=[[1, 128]], base=0,
                                channel_multiplier=-1,
                                allow_small_or_imprecise_dtypes=True))
    b.op("po", lambda e: e.tensor_single_scalar(identF[:, :], identF[:, :], 0.0,
                                                ALU.is_equal))
    b.op("po", lambda e: e.tensor_copy(identB[:, :], identF[:, :]))
    b.op("po", lambda e: e.memset(onesCol[:, :], 1.0))
    b.op("po", lambda e: e.memset(ones1[:, :], 1.0))
    b.op("po", lambda e: e.iota(selF[:, :, :], pattern=[[1, 16], [0, 128]], base=0,
                                channel_multiplier=-1,
                                allow_small_or_imprecise_dtypes=True))
    b.op("po", lambda e: e.tensor_single_scalar(selF[:, :, :], selF[:, :, :], 0.0,
                                                ALU.is_equal))
    tok_sel = b.op("po", lambda e: e.tensor_copy(selAll[:, :, :], selF[:, :, :]))
    tb_state["setup"] = tok_sel

    b.dma("sp", smallRow[:, :], smalls)
    b.dma("sp", freqRow[:, :], rfreq)
    b.dma("sp", wdtCol[:, :], wdt_pm)
    b.dma("sp", bdtCol[:, :], bdt_pm)
    b.dma("sp", A_sb[:, :], a_pm)
    b.dma("sp", dColS[:, :], d_pm)
    b.dma("sp", cColT[:, :], c_pm)
    b.dma("sp", wxStg[:, :, :], w_x_pm)
    b.dma("sp", rtStg[:, :, :], router_pm)
    ld0 = ("ld", b.cnt["ld"])

    ta = b.op("act", lambda e: e.activation(A_sb[:, :], A_sb[:, :], AF.Exp),
              waits=[ld0])
    tok_A = b.op("dve", lambda e: e.tensor_scalar(A_sb[:, :], A_sb[:, :], 1e-4, -1.0,
                                                  ALU.max, ALU.mult), waits=[ta])
    b.op("act", lambda e: e.activation(sfRow[:, :], freqRow[:, :], AF.Abs),
         waits=[ld0])
    b.op("act", lambda e: e.activation(sfRow[:, :], sfRow[:, :], AF.Exp, scale=-1.0),
         ss=True)
    ta2 = b.op("act", lambda e: e.activation(sfRow[:, :], sfRow[:, :], AF.Ln, bias=1.0),
               ss=True)
    td2 = b.op("dve", lambda e: e.scalar_tensor_tensor(
        sfRow[:, :], freqRow[:, :], 0.0, sfRow[:, :], ALU.max, ALU.add), waits=[ta2])
    b.op("act", lambda e: e.copy(wx_bf[:, :, :], wxStg[:, :, :]), waits=[ld0],
         inc=False)
    b.op("act", lambda e: e.copy(router_bf[:, :, :], rtStg[:, :, :]))
    tsc = b.op("act", lambda e: e.activation(scCol[:, :], cColT[:, :], AF.Silu))

    wm_guard = [None, None]
    for nchunk in range(12):
        sl = nchunk % 2
        tlw = b.dma("sp", wmod_sb[sl][:, :, :],
                    wmod_pm[:, :, nchunk * 512:(nchunk + 1) * 512],
                    waits=[wm_guard[sl]], sem="wiS")

        def mk(nchunk=nchunk, sl=sl):
            def f(e):
                for kc in range(8):
                    r = e.matmul(pb[5][0:1, 0:512], scCol[:, kc:kc + 1],
                                 wmod_sb[sl][:, kc, :],
                                 start=(kc == 0), stop=(kc == 7))
                return r
            return f
        mt = b.pe_mm("pb5", [tsc, ld0, tlw], mk())
        wm_guard[sl] = mt
        b.evac("pb5", "act", lambda e, nchunk=nchunk: e.copy(
            modRow[0:1, nchunk * 512:(nchunk + 1) * 512], pb[5][0:1, 0:512]),
            waits=[mt])
    tmod = ("act", b.cnt["act"])

    tmodd = tmod
    prev_clamp = None
    for base in (2048, 5120):
        b.op("dve", lambda e, base=base: e.tensor_scalar(
            gtmpS[:, :], modRow[0:1, base:base + 1024], 0.0, None, ALU.is_ge),
            waits=[tmod])
        b.op("dve", lambda e: e.tensor_scalar(gtmpS[:, :], gtmpS[:, :], 2.0, -1.0,
                                              ALU.mult, ALU.add))
        tabs = b.op("act", lambda e, base=base: e.activation(
            gtmpA[:, :], modRow[0:1, base:base + 1024], AF.Abs),
            waits=[tmod, prev_clamp])
        b.op("dve", lambda e: e.tensor_scalar_max(gtmpA[:, :], gtmpA[:, :], 0.25),
             waits=[tabs])
        tmodd = b.op("dve", lambda e, base=base: e.tensor_tensor(
            modRow[0:1, base:base + 1024], gtmpS[:, :], gtmpA[:, :], ALU.mult))
        prev_clamp = tmodd

    for (dst, base, bias) in ((caRep, 1024, 1.0), (saRep, 0, 0.0), (gaRep, 2048, 0.0),
                              (cbRep, 4096, 1.0), (sbRep, 3072, 0.0),
                              (gbRep, 5120, 0.0)):
        for nchunk in range(2):
            mt = b.pe_mm("pb4", [tmodd], lambda e, base=base, nchunk=nchunk: e.matmul(
                pb[4][:, :], ones1[0:1, :],
                modRow[0:1, base + nchunk * 512: base + (nchunk + 1) * 512],
                start=True, stop=True))
            b.evac("pb4", "act", lambda e, dst=dst, nchunk=nchunk, bias=bias:
                   e.activation(dst[:, nchunk * 512:(nchunk + 1) * 512],
                                pb[4][:, :], AF.Copy, bias=bias), waits=[mt])

    for (dst, off) in ((bRepB, 0), (bRepC, 16), (wRepB, 32), (wRepC, 48)):
        mt = b.pe_mm("pb4", [ld0], lambda e, off=off: e.matmul(
            pb[4][:, 0:16], ones1[0:1, :], smallRow[0:1, off:off + 16],
            start=True, stop=True))
        b.evac("pb4", "act", lambda e, dst=dst: e.copy(dst[:, :], pb[4][:, 0:16]),
               waits=[mt])
    mt = b.pe_mm("pb4", [td2], lambda e: e.matmul(
        pb[4][:, 0:8], ones1[0:1, :], sfRow[0:1, :], start=True, stop=True))
    b.evac("pb4", "act", lambda e: e.copy(sfreqRep[:, :], pb[4][:, 0:8]), waits=[mt])

    setup_act = ("act", b.cnt["act"])
    es_setup.close()
    b.barrier()

    es_m = ExitStack()
    w_in_bf = b.sb("w_in_bf", [128, 8, 2048], BF16, es_m)
    w_out_bf = b.sb("w_out_bf", [128, 8, 1024], BF16, es_m)
    wstgM2 = [b.sb(f"wstgM{i}", [128, 8, 128], F32, es_m) for i in range(2)]
    x_t = [b.sb("x_t0", [128, D], F32, es_m)] * 2
    h_tm = b.sb("h_tm", [128, D], BF16, es_m)
    h_fm = b.sb("h_fm", [128, 8, CH], BF16, es_m)
    xp_bf = [b.sb(f"xp{p}", [128, 8, CH], BF16, es_m) for p in range(2)]
    sz_bf = [b.sb(f"sz{p}", [128, 8, CH], BF16, es_m) for p in range(2)]
    wc_bf = [b.sb(f"wc{p}", [128, 8, CH], BF16, es_m) for p in range(2)]
    w0_bf = [b.sb(f"w0{p}", [128, 8, CH], BF16, es_m) for p in range(2)]
    dtO = [b.sb(f"dtO{p}", [128, 8, CH], F32, es_m) for p in range(2)]
    dtX = b.sb("dtX", [128, CH], F32, es_m)
    yacc = b.sb("yacc", [128, 8, CH], F32, es_m)
    carryA = b.sb("carryA", [128, 128], F32, es_m)
    ldtLast = [b.sb(f"ldtL{p}", [128, 8], F32, es_m) for p in range(2)]
    ssm_a = b.sb("ssm_a", [34, CH], F32, es_m)
    ssm_b = b.sb("ssm_b", [34, CH], F32, es_m)
    ssm_f = b.sb("ssm_f", [34, CH], F32, es_m)
    ssmT = b.sb("ssmT", [128, NTT, 34], F32, es_m)
    bnB = b.sb("bnB", [128, 16], F32, es_m)
    bnC = b.sb("bnC", [128, 16], F32, es_m)
    sqS = b.sb("sqS", [128, 16], F32, es_m)
    colS = b.sb("colS", [128, 4], F32, es_m)
    BrTm = b.sb("BrTm", [128, 16], F32, es_m)
    CrTm = b.sb("CrTm", [128, 16], F32, es_m)
    ropeT = [b.sb(f"ropeT{i}", [128, 8], F32, es_m) for i in range(6)]
    ropeI = b.sb("ropeI", [128, 8], I32, es_m)
    BrTf = [b.sb(f"BrTf{p}", [16, CH], BF16, es_m) for p in range(2)]
    CrTf = [b.sb(f"CrTf{p}", [16, CH], BF16, es_m) for p in range(2)]
    qcol = b.sb("qcol", [128, 1], F32, es_m)
    qrow = b.sb("qrow", [1, CH], F32, es_m)
    qRep = [b.sb(f"qRep{p}", [128, CH], F32, es_m) for p in range(2)]
    dt_inRep = b.sb("dt_inRep", [128, CH], F32, es_m)
    lamRep = b.sb("lamRep", [128, CH], F32, es_m)
    olamRep = b.sb("olamRep", [128, CH], F32, es_m)
    lamrow = b.sb("lamrow", [1, CH], F32, es_m)
    dtirow = b.sb("dtirow", [1, CH], F32, es_m)
    mdtRow = b.sb("mdtRow", [1, CH], F32, es_m)
    cumRow = [b.sb(f"cumRow{p}", [1, CH], F32, es_m) for p in range(2)]
    onesRow = b.sb("onesRow", [1, CH], F32, es_m)
    ldt_t = b.sb("ldt_t", [128, CH], F32, es_m)
    p2sh_t = b.sb("p2sh_t", [128, CH], F32, es_m)
    s1_t = b.sb("s1_t", [128, CH], F32, es_m)
    ppB = b.sb("ppB", [128, 8], F32, es_m)
    BrB = b.sb("BrB", [128, CH], BF16, es_m)
    CrB2 = [b.sb(f"CrB{i}", [128, CH], BF16, es_m) for i in range(2)]
    dA_t = [b.sb(f"dA{i}", [128, CH], F32, es_m) for i in range(2)]
    u_t = [b.sb(f"u{i}", [128, CH], BF16, es_m) for i in range(2)]
    h_t = [b.sb(f"h{i}", [128, CH], BF16, es_m) for i in range(2)]
    tmp_t = [b.sb(f"tmp{i}", [128, CH], BF16, es_m) for i in range(2)]
    yp1 = b.sb("yp1", [128, CH], F32, es_m)
    yp2 = b.sb("yp2", [128, CH], F32, es_m)
    Y3 = b.sb("Y3", [128, 8, CH], BF16, es_m)
    attnP = b.sb("attnP", [128, 8, CH], BF16, es_m)
    attnS = b.sb("attnS", [128, 8, CH], BF16, es_m)
    attn_tm = b.sb("attn_tm", [128, D], BF16, es_m)
    xo_t = [b.sb("xo0", [128, D], F32, es_m)] * 2
    xn_t = b.sb("xn_t", [128, D], F32, es_m)
    if dbg_on:
        snap = b.sb("snap", [128, 64], F32, es_m)
        snapT = b.sb("snapT", [128, D], BF16, es_m)
        snapX = b.sb("snapX", [128, D], F32, es_m)
    else:
        snap = snapT = snapX = None
    lnc = b.sb("lnc", [128, D], F32, es_m)
    lncol = b.sb("lncol", [128, 4], F32, es_m)
    lnsq = b.sb("lnsq", [128, D], F32, es_m)

    cast_guard = [("po", b.cnt["po"]), ("po", b.cnt["po"])]
    for i, (dst, src, mo) in enumerate(
            [(w_in_bf, w_in_pm, mo) for mo in range(16)]
            + [(w_out_bf, w_out_pm, mo) for mo in range(8)]):
        csl = i % 2
        tl = b.dma("sp", wstgM2[csl][:, :, :], src[:, :, mo * 128:(mo + 1) * 128],
                   waits=[cast_guard[csl]], sem="wiS")
        cast_guard[csl] = b.op("po", lambda e, dst=dst, mo=mo, csl=csl: e.tensor_copy(
            dst[:, :, mo * 128:(mo + 1) * 128], wstgM2[csl][:, :, :]), waits=[tl])
    w_cast_done = b.op("po", lambda e: e.engine_nop(),
                       waits=[cast_guard[0], cast_guard[1]])

    b.op("dve", lambda e: e.memset(carryA[:, :], 0.0))
    b.op("po", lambda e: e.memset(onesRow[:, :], 1.0))

    gd = {
        "xp_free": [None, None], "sz_free": [None, None],
        "wc_free": [None, None], "w0_free": [None, None],
        "dtO_free": [None, None], "BrTf_free": [None, None],
        "qRep_free": [None, None], "hfm_pe": None, "Y3_free": None,
        "attnP_free": None, "attnS_free": None,
        "x_slot": [None, None], "xo_slot": [None, None],
        "scan_dve": [None, None], "ypost_dve": [None, None],
        "wc_patch": [None, None], "w0_patch": [None, None],
        "dt_tok": [None, None], "rope_tok": [None, None],
        "dA_guard": [None, None], "h_guard": [None, None],
        "tmp_guard": [None, None], "prev_u": None, "prev_y": None,
        "prev_pm": None, "tmm_prev": None,
        "scan_po": [None, None], "ys_last2": None,
        "ag_next": None, "rs_next": None, "lamrow_free": None,
        "ropePE": None, "rope_dve": None, "tsum_prev": None,
        "cum_pe": [None, None],
        "ln_sq_prev": None, "bn_prev": [None, None],
        "sz_tok": {}, "dA_read": [None, None],
    }

    def ln_adaln(xtile, crep, srep, out_bf, waits_x, extra=()):
        b.op("dve", lambda e: e.reduce_sum(lncol[:, 0:1], xtile[:, :], axis=AXX),
             waits=list(waits_x) + list(extra) + [gd["ln_sq_prev"]])
        b.op("dve", lambda e: e.tensor_scalar_mul(lncol[:, 1:2], lncol[:, 0:1],
                                                  1.0 / D), ss=True)
        tc = b.op("dve", lambda e: e.tensor_scalar(lnc[:, :], xtile[:, :],
                                                   lncol[:, 1:2], None, ALU.subtract),
                  ss=True)
        tq = b.op("act", lambda e: e.activation(lnsq[:, :], lnc[:, :], AF.Square,
                                                accum_out=lncol[:, 2:3]), waits=[tc])
        gd["ln_sq_prev"] = tq
        b.op("act", lambda e: e.activation(lncol[:, 3:4], lncol[:, 2:3], AF.Ln,
                                           scale=1.0 / D, bias=LN_EPS), ss=True)
        tr = b.op("act", lambda e: e.activation(lncol[:, 3:4], lncol[:, 3:4], AF.Exp,
                                                scale=-0.5), ss=True)
        b.op("dve", lambda e: e.tensor_scalar(lnc[:, :], lnc[:, :], lncol[:, 3:4],
                                              None, ALU.mult), waits=[tr])
        b.op("dve", lambda e: e.tensor_tensor(lnsq[:, :], lnc[:, :], crep[:, :],
                                              ALU.mult))
        return b.op("dve", lambda e: e.tensor_tensor(out_bf[:, :], lnsq[:, :],
                                                     srep[:, :], ALU.add))

    def front(c):
        par = c % 2
        for tt in range(NTT):
            sl = 0
            tl = b.dma("sp", x_t[sl][:, :],
                       xb[c * CH + tt * 128: c * CH + (tt + 1) * 128, :],
                       waits=[gd["x_slot"][sl], ld0], sem="xA")
            tln = ln_adaln(x_t[sl], caRep, saRep, h_tm, [tl],
                           extra=[("pe", b.cnt["pe"]), setup_act])
            gd["x_slot"][sl] = tln
            for dc in range(8):
                tbf(h_tm[:, dc * 128:(dc + 1) * 128],
                    h_fm[:, dc, tt * 128:(tt + 1) * 128], [tln])
            flush_tbf()
        th = None

        ev_xp = None
        ev = None
        for mo in range(16):
            bankname = "pb0" if mo % 2 == 0 else "pb1"
            bank = pb[0] if mo % 2 == 0 else pb[1]

            def mmf(e, mo=mo, bank=bank):
                for kc in range(8):
                    r = e.matmul(bank[:, 0:CH], w_in_bf[:, kc, mo * 128:(mo + 1) * 128],
                                 h_fm[:, kc, :], start=(kc == 0), stop=(kc == 7))
                return r
            mt = b.pe_mm(bankname, [tb_state["guard"][0], tb_state["guard"][1],
                                    w_cast_done], mmf)
            if mo < 8:
                dst, w8 = xp_bf[par], gd["xp_free"][par]
            else:
                dst, w8 = sz_bf[par], gd["sz_free"][par]
            ev = b.evac(bankname, "act", lambda e, dst=dst, mo=mo, bank=bank:
                        e.activation(dst[:, mo % 8, :], bank[:, 0:CH], AF.Silu),
                        waits=[mt, w8])
            if mo == 7:
                ev_xp = ev
        gd["sz_tok"][c] = ev
        if c == dbg_chunk:
            dbg("hfm", h_fm[:, :, :], ev)
        gd["hfm_pe"] = ("pe", b.cnt["pe"])

        def ssm_mm(e):
            for kc in range(8):
                r = e.matmul(pb[5][0:34, 0:CH], wx_bf[:, kc, :], xp_bf[par][:, kc, :],
                             start=(kc == 0), stop=(kc == 7))
            return r
        mt = b.pe_mm("pb5", [ev_xp], ssm_mm)
        evs = b.evac("pb5", "act", lambda e: e.copy(ssm_a[:, :], pb[5][0:34, 0:CH]),
                     waits=[mt, gd["tsum_prev"]])
        tst = b.dma("po", ag_in, ssm_a[:, :], waits=[evs, gd["ag_next"]], sem="st")
        tcc = b.op("po", lambda e: e.collective_compute(
            "AllGather", ALU.bypass, replica_groups=PAIRS, ins=[ag_in], outs=[ag_out]),
            waits=[("st", b.cnt["st"]), ("agl", b.cnt["agl"])], inc=("cc", 1))
        gd["ag_next"] = tcc
        tla = b.dma("sp", ssm_a[:, :], ag_out[0:34, :], waits=[tcc], sem="agl")
        tlb = b.dma("sp", ssm_b[:, :], ag_out[34:68, :], sem="agl")
        tsum = b.op("dve", lambda e: e.tensor_tensor(ssm_f[:, :], ssm_a[:, :],
                                                     ssm_b[:, :], ALU.add),
                    waits=[tla, tlb])
        gd["tsum_prev"] = tsum

        if c == dbg_chunk:
            dbg("ssm_f", ssm_f[:, :], tsum)
        evT = []
        for tt in range(NTT):
            evT.append(tf32(ssm_f[0:34, tt * 128:(tt + 1) * 128],
                            ssmT[:, tt, 0:34], [tsum, gd["rope_dve"]], 34, 128))

        tdr = b.dma("sp", dtirow[:, :], ssm_f[32:33, :],
                    waits=[tsum, gd["lamrow_free"]], sem="lrw")
        mt = b.pe_mm("pb4", [tdr], lambda e: e.matmul(
            pb[4][:, 0:CH], ones1[0:1, :], dtirow[0:1, :], start=True, stop=True))
        tdir = b.evac("pb4", "act", lambda e: e.copy(dt_inRep[:, :], pb[4][:, 0:CH]),
                      waits=[mt, gd["scan_dve"][par]])
        trow = b.dma("sp", lamrow[:, :], ssm_f[33:34, :],
                     waits=[tsum, gd["lamrow_free"]], sem="lrw")
        tsg = b.op("act", lambda e: e.activation(lamrow[:, :], lamrow[:, :],
                                                 AF.Sigmoid), waits=[trow])
        mt = b.pe_mm("pb4", [tsg], lambda e: e.matmul(
            pb[4][:, 0:CH], ones1[0:1, :], lamrow[0:1, :], start=True, stop=True))
        gd["lamrow_free"] = ("pe", b.cnt["pe"])
        tlr = b.evac("pb4", "act", lambda e: e.copy(lamRep[:, :], pb[4][:, 0:CH]),
                     waits=[mt, gd["scan_dve"][par]])
        tol = b.op("dve", lambda e: e.tensor_scalar(olamRep[:, :], lamRep[:, :],
                                                    -1.0, 1.0, ALU.mult, ALU.add),
                   waits=[tlr])

        last = None
        for etf in range(16):
            own = etf < 8
            ta1 = b.op("dve", lambda e, etf=etf: e.tensor_scalar(
                dtX[:, :], dt_inRep[:, :], wdtCol[:, etf:etf + 1],
                bdtCol[:, etf:etf + 1], ALU.mult, ALU.add),
                waits=[tdir, ld0, gd["tmm_prev"],
                       gd["dtO_free"][par] if own else None])
            tab = b.op("act", lambda e: e.activation(lnsq[:, 0:CH], dtX[:, :], AF.Abs),
                       waits=[ta1])
            b.op("act", lambda e: e.activation(lnsq[:, 0:CH], lnsq[:, 0:CH], AF.Exp,
                                               scale=-1.0))
            tal = b.op("act", lambda e: e.activation(lnsq[:, 0:CH], lnsq[:, 0:CH],
                                                     AF.Ln, bias=1.0))
            if own:
                tdt = b.op("dve", lambda e, etf=etf: e.scalar_tensor_tensor(
                    dtO[par][:, etf, :], dtX[:, :], 0.0, lnsq[:, 0:CH],
                    ALU.max, ALU.add), waits=[tal, gd["dA_read"][par]])
                src = dtO[par][:, etf, :]
            else:
                tdt = b.op("dve", lambda e: e.scalar_tensor_tensor(
                    dtX[:, :], dtX[:, :], 0.0, lnsq[:, 0:CH], ALU.max, ALU.add),
                    waits=[tal])
                src = dtX[:, :]
            gd["tmm_prev"] = b.pe_mm("pb3", [tdt], lambda e, src=src, etf=etf: e.matmul(
                pb[3][0:1, 0:CH], onesCol[:, 0:1], src,
                start=(etf == 0), stop=(etf == 15)))
            if own:
                tld = b.op("dve", lambda e, etf=etf: e.tensor_tensor(
                    ldt_t[:, :], dtO[par][:, etf, :], lamRep[:, :], ALU.mult),
                    waits=[tol])
                if c == 0:
                    b.op("dve", lambda e, etf=etf: e.tensor_copy(
                        ldt_t[:, 0:1], dtO[par][:, etf, 0:1]))
                b.op("dve", lambda e, etf=etf: e.tensor_tensor(
                    p2sh_t[:, 0:CH - 1], dtO[par][:, etf, 1:CH], olamRep[:, 1:CH],
                    ALU.mult))
                b.op("dve", lambda e: e.tensor_tensor(
                    s1_t[:, 0:CH - 1], ldt_t[:, 0:CH - 1], p2sh_t[:, 0:CH - 1],
                    ALU.add))
                b.op("dve", lambda e, etf=etf: e.tensor_tensor(
                    wc_bf[par][:, etf, 0:CH - 1], xp_bf[par][:, etf, 0:CH - 1],
                    s1_t[:, 0:CH - 1], ALU.mult),
                    waits=[ev_xp, gd["wc_free"][par], gd["scan_po"][par]])
                b.op("dve", lambda e, etf=etf: e.tensor_tensor(
                    w0_bf[par][:, etf, 0:CH - 1], xp_bf[par][:, etf, 0:CH - 1],
                    p2sh_t[:, 0:CH - 1], ALU.mult),
                    waits=[gd["w0_free"][par]])
                last = b.op("dve", lambda e, etf=etf: e.tensor_copy(
                    ldtLast[par][:, etf:etf + 1], ldt_t[:, CH - 1:CH]), ss=True)
        gd["dt_tok"][par] = last
        if c == dbg_chunk:
            dbg("dt_inRep", dt_inRep[:, :], tdir)
            dbg("lamRep", lamRep[:, :], tlr)
            dbg("dtO", dtO[par][:, :, :], last)
            dbg("xp", xp_bf[par][:, :, :], ev_xp)
        tmean = ("pe", b.cnt["pe"])
        tmd = b.evac("pb3", "act", lambda e: e.activation(
            mdtRow[0:1, :], pb[3][0:1, 0:CH], AF.Copy, scale=1.0 / 2048.0),
            waits=[tmean, gd["scan_dve"][par]])

        init = 0.0 if c == 0 else cumRow[1 - par][0:1, CH - 1:CH]
        tcum = b.op("dve", lambda e, init=init: e.tensor_tensor_scan(
            cumRow[par][0:1, :], onesRow[0:1, :], mdtRow[0:1, :], init,
            ALU.mult, ALU.add), waits=[tmd, gd["cum_pe"][par]])

        rope_evs = []
        for tt in range(NTT):
            cix = 0 if tt == 0 else 3
            tcc2 = tf32(cumRow[par][0:1, tt * 128:(tt + 1) * 128],
                        colS[:, cix:cix + 1], [tcum], 1, 128)
            tb1 = b.op("dve", lambda e, tt=tt: e.tensor_tensor(
                bnB[:, :], ssmT[:, tt, 0:16], bRepB[:, :], ALU.add),
                waits=[evT[tt], gd["bn_prev"][0]], ss=True)
            tb2 = b.op("act", lambda e: e.activation(sqS[:, :], bnB[:, :], AF.Square,
                                                     accum_out=colS[:, 1:2]),
                       waits=[tb1])
            b.op("act", lambda e: e.activation(colS[:, 1:2], colS[:, 1:2], AF.Ln,
                                               scale=1.0 / 16, bias=RMS_EPS),
                 ss=True)
            tb3 = b.op("act", lambda e: e.activation(colS[:, 1:2], colS[:, 1:2],
                                                     AF.Exp, scale=-0.5), ss=True)
            tb4 = b.op("dve", lambda e: e.scalar_tensor_tensor(
                bnB[:, :], bnB[:, :], colS[:, 1:2], wRepB[:, :], ALU.mult, ALU.mult),
                waits=[tb3], ss=True)
            gd["bn_prev"][0] = tb2
            tc1 = b.op("dve", lambda e, tt=tt: e.tensor_tensor(
                bnC[:, :], ssmT[:, tt, 16:32], bRepC[:, :], ALU.add),
                waits=[gd["bn_prev"][1]], ss=True)
            tc2 = b.op("act", lambda e: e.activation(sqS[:, :], bnC[:, :], AF.Square,
                                                     accum_out=colS[:, 2:3]),
                       waits=[tc1])
            b.op("act", lambda e: e.activation(colS[:, 2:3], colS[:, 2:3], AF.Ln,
                                               scale=1.0 / 16, bias=RMS_EPS),
                 ss=True)
            tc3 = b.op("act", lambda e: e.activation(colS[:, 2:3], colS[:, 2:3],
                                                     AF.Exp, scale=-0.5), ss=True)
            tc4 = b.op("dve", lambda e: e.scalar_tensor_tensor(
                bnC[:, :], bnC[:, :], colS[:, 2:3], wRepC[:, :], ALU.mult, ALU.mult),
                waits=[tc3], ss=True)
            gd["bn_prev"][1] = tc2

            tan = b.op("dve", lambda e, cix=cix: e.tensor_scalar(
                ropeT[0][:, :], sfreqRep[:, :], colS[:, cix:cix + 1], None, ALU.mult),
                waits=[tcc2], ss=True)
            b.op("dve", lambda e: e.tensor_scalar_mul(ropeT[1][:, :], ropeT[0][:, :],
                                                      INV_2PI), ss=True)
            b.op("dve", lambda e: e.tensor_scalar_add(ropeT[1][:, :], ropeT[1][:, :],
                                                      8388608.0), ss=True)
            b.op("dve", lambda e: e.tensor_scalar_sub(ropeT[1][:, :], ropeT[1][:, :],
                                                      8388608.0), ss=True)
            ty = b.op("dve", lambda e: e.scalar_tensor_tensor(
                ropeT[2][:, :], ropeT[1][:, :], -TWO_PI, ropeT[0][:, :],
                ALU.mult, ALU.add), ss=True)
            if dbg_on and c == dbg_chunk and tt == 0:
                b.op("act", lambda e, cix=cix: e.copy(snap[:, 0:1], colS[:, cix:cix + 1]),
                     waits=[tan])
                b.op("act", lambda e: e.copy(snap[:, 8:16], ropeT[0][:, :]),
                     waits=[tan])
                b.op("act", lambda e: e.copy(snap[:, 16:24], ropeT[1][:, :]),
                     waits=[ty])
                b.op("act", lambda e: e.copy(snap[:, 24:32], ropeT[2][:, :]),
                     waits=[ty])
            tsin = b.op("act", lambda e: e.activation(ropeT[3][:, :], ropeT[2][:, :],
                                                      AF.Sin), waits=[ty])
            b.op("dve", lambda e: e.tensor_scalar_add(ropeT[4][:, :], ropeT[2][:, :],
                                                      PI / 2), ss=True)
            b.op("dve", lambda e: e.tensor_single_scalar(
                ropeT[5][:, :], ropeT[4][:, :], PI, ALU.is_gt), ss=True)
            tyc = b.op("dve", lambda e: e.scalar_tensor_tensor(
                ropeT[4][:, :], ropeT[5][:, :], -TWO_PI, ropeT[4][:, :],
                ALU.mult, ALU.add), ss=True)
            tcos = b.op("act", lambda e: e.activation(ropeT[5][:, :], ropeT[4][:, :],
                                                      AF.Sin), waits=[tyc])
            if dbg_on and c == dbg_chunk and tt == 0:
                tsnap = b.op("act", lambda e: e.copy(snap[:, 32:40], ropeT[3][:, :]))
                dbg("snap", snap[:, :], tsnap)

            for (srcv, dstv, tw) in ((bnB, BrTm, tb4), (bnC, CrTm, tc4)):
                b.op("dve", lambda e, srcv=srcv: e.tensor_tensor(
                    ropeT[0][:, :], srcv[:, 0:8], ropeT[5][:, :], ALU.mult),
                    waits=[tcos, tsin, tw, gd["ropePE"]], ss=True)
                b.op("dve", lambda e, srcv=srcv: e.tensor_tensor(
                    ropeT[1][:, :], srcv[:, 8:16], ropeT[3][:, :], ALU.mult),
                    ss=True)
                b.op("dve", lambda e, dstv=dstv: e.tensor_tensor(
                    dstv[:, 0:8], ropeT[0][:, :], ropeT[1][:, :], ALU.subtract),
                    ss=True)
                b.op("dve", lambda e, srcv=srcv: e.tensor_tensor(
                    ropeT[0][:, :], srcv[:, 0:8], ropeT[3][:, :], ALU.mult),
                    ss=True)
                b.op("dve", lambda e, srcv=srcv: e.tensor_tensor(
                    ropeT[1][:, :], srcv[:, 8:16], ropeT[5][:, :], ALU.mult),
                    ss=True)
                b.op("dve", lambda e, dstv=dstv: e.tensor_tensor(
                    dstv[:, 8:16], ropeT[0][:, :], ropeT[1][:, :], ALU.add),
                    ss=True)
            b.op("dve", lambda e: e.tensor_tensor(sqS[:, :], BrTm[:, :], CrTm[:, :],
                                                  ALU.mult), ss=True)
            tq2 = b.op("dve", lambda e: e.reduce_sum(qcol[:, 0:1], sqS[:, :],
                                                     axis=AXX), ss=True)
            if dbg_on and c == dbg_chunk and tt == 0:
                tsb = b.op("dve", lambda e: e.tensor_copy(snap[:, 40:56], BrTm[:, :]))
                b.op("act", lambda e: e.copy(snap[:, 56:64], ropeT[5][:, :]),
                     waits=[tsb])
                tsb2 = b.op("act", lambda e: e.copy(snap[:, 4:5], qcol[:, 0:1]), waits=[tsb])
                dbg("snapBr", snap[:, :], tsb2)
            if c == dbg_chunk and tt == 0:
                dbg("sfreqRep", sfreqRep[:, :], tq2)
                dbg("angT", ropeT[2][:, :], ty)
                dbg("BrTm", BrTm[:, :], tq2)
                dbg("CrTm", CrTm[:, :], tq2)
                dbg("bnB", bnB[:, :], tq2)
                dbg("sinT", ropeT[3][:, :], tq2)
                dbg("cosT", ropeT[5][:, :], tq2)
                dbg("qcolD", qcol[:, :], tq2)
            rope_evs.append(tf32(BrTm[:, :], BrTf[par][0:16, tt * 128:(tt + 1) * 128],
                                 [tq2, gd["BrTf_free"][par]], 128, 16))
            rope_evs.append(tf32(CrTm[:, :], CrTf[par][0:16, tt * 128:(tt + 1) * 128],
                                 [tq2], 128, 16))
            rope_evs.append(tf32(qcol[:, 0:1], qrow[0:1, tt * 128:(tt + 1) * 128],
                                 [tq2], 128, 1))
            gd["ropePE"] = ("pe", b.cnt["pe"])
        gd["rope_dve"] = ("dve", b.cnt["dve"])
        gd["cum_pe"][par] = ("pe", b.cnt["pe"])
        mt = b.pe_mm("pb4", rope_evs, lambda e: e.matmul(
            pb[4][:, 0:CH], ones1[0:1, :], qrow[0:1, :], start=True, stop=True))
        tqr = b.evac("pb4", "act", lambda e: e.copy(qRep[par][:, :], pb[4][:, 0:CH]),
                     waits=[mt, gd["qRep_free"][par]])
        gd["rope_tok"][par] = tqr
        if c == dbg_chunk:
            dbg("cumRow", cumRow[par][:, :], tcum)
            dbg("BrTf", BrTf[par][:, :], tqr)
            dbg("CrTf", CrTf[par][:, :], tqr)
            dbg("qRep", qRep[par][:, :], tqr)

    def patch(cprev, clast=False):
        par = cprev % 2
        npar = 1 - par
        for et in range(8):
            if not clast:
                b.op("dve", lambda e, et=et: e.tensor_tensor(
                    ppB[:, et:et + 1], dtO[npar][:, et, 0:1], olamRep[:, 0:1],
                    ALU.mult), waits=[gd["dt_tok"][npar]], ss=True)
                b.op("dve", lambda e, et=et: e.tensor_tensor(
                    w0_bf[par][:, et, CH - 1:CH], xp_bf[par][:, et, CH - 1:CH],
                    ppB[:, et:et + 1], ALU.mult),
                    waits=[gd["scan_po"][par]], ss=True)
                b.op("dve", lambda e, et=et: e.tensor_tensor(
                    ppB[:, et:et + 1], ldtLast[par][:, et:et + 1], ppB[:, et:et + 1],
                    ALU.add), ss=True)
                b.op("dve", lambda e, et=et: e.tensor_tensor(
                    wc_bf[par][:, et, CH - 1:CH], xp_bf[par][:, et, CH - 1:CH],
                    ppB[:, et:et + 1], ALU.mult),
                    waits=[gd["scan_po"][par]], ss=True)
            else:
                b.op("dve", lambda e, et=et: e.tensor_tensor(
                    wc_bf[par][:, et, CH - 1:CH], xp_bf[par][:, et, CH - 1:CH],
                    ldtLast[par][:, et:et + 1], ALU.mult),
                    waits=[gd["scan_po"][par]], ss=True)
                b.op("dve", lambda e, et=et: e.memset(
                    w0_bf[par][:, et, CH - 1:CH], 0.0),
                    waits=[gd["scan_po"][par]], ss=True)
        tok = b.op("dve", lambda e: e.engine_nop())
        gd["wc_patch"][par] = tok
        gd["w0_patch"][par] = tok

    def scan(c):
        par = c % 2
        dA_guard = gd["dA_guard"]
        h_guard = gd["h_guard"]
        tmp_guard = gd["tmp_guard"]
        prev_u = gd["prev_u"]
        ys_tok = {}
        prev2 = gd["ys_last2"]
        crb_tok = [None, None]
        pend = None

        def cr_toks(s):
            yy = (prev2[s] if (s < 2 and prev2) else
                  ys_tok.get(s - 2))
            return (yy[0], yy[1]) if yy else (None, None)

        def emit_y(p):
            s, et, sl, tsc2 = p
            cb = CrB2[s % 2]
            if s == 0:
                ty = b.op("dve", lambda e, et=et, sl=sl, cb=cb: e.tensor_tensor(
                    yacc[:, et, :], h_t[sl][:, :], cb[:, :], ALU.mult),
                    waits=[crb_tok[s % 2], gd["ypost_dve"][par]])
                h_guard[sl] = ty
                ys_tok[s] = [ty, None]
            else:
                tm = b.op("po", lambda e, sl=sl, cb=cb: e.tensor_tensor(
                    tmp_t[sl][:, :], h_t[sl][:, :], cb[:, :], ALU.mult),
                    waits=[tsc2, crb_tok[s % 2], tmp_guard[sl]])
                h_guard[sl] = tm
                ty = b.op("dve", lambda e, et=et, sl=sl: e.tensor_tensor(
                    yacc[:, et, :], yacc[:, et, :], tmp_t[sl][:, :], ALU.add),
                    waits=[tm])
                tmp_guard[sl] = ty
                ys_tok[s] = [ty, tm]
            b.op("act", lambda e, sl=sl, col=s * 8 + et: e.copy(
                carryA[:, col:col + 1], h_t[sl][:, CH - 1:CH]), waits=[tsc2])

        for s in range(S):
            mtb = b.pe_mm("pb4", [gd["rope_tok"][par], prev_u], lambda e, s=s:
                          e.matmul(pb[4][:, 0:CH], selAll[:, s, :], BrTf[par][:, :],
                                   start=True, stop=True))
            tBrB = b.evac("pb4", "act", lambda e: e.copy(BrB[:, :], pb[4][:, 0:CH]),
                          waits=[mtb, prev_u])
            y2d, y2p = cr_toks(s)
            mtc = b.pe_mm("pb2", [gd["rope_tok"][par], y2d, y2p],
                          lambda e, s=s: e.matmul(pb[2][:, 0:CH], selAll[:, s, :],
                                                  CrTf[par][:, :],
                                                  start=True, stop=True))
            tCrB = b.evac("pb2", "act", lambda e, s=s: e.copy(
                CrB2[s % 2][:, :], pb[2][:, 0:CH]), waits=[mtc, y2d, y2p])
            crb_tok[s % 2] = tCrB
            for et in range(8):
                it = s * 8 + et
                sl = it % 2
                tda = b.op("act", lambda e, s=s, et=et, sl=sl: e.activation(
                    dA_t[sl][:, :], dtO[par][:, et, :], AF.Exp,
                    scale=A_sb[:, et * 16 + s: et * 16 + s + 1]),
                    waits=[dA_guard[sl], gd["dt_tok"][par], tok_A])
                tu = b.op("po", lambda e, et=et, sl=sl: e.tensor_tensor(
                    u_t[sl][:, :], wc_bf[par][:, et, :], BrB[:, :], ALU.mult),
                    waits=[tBrB, gd["wc_patch"][par], dA_guard[sl]])
                tsc2 = b.op("dve", lambda e, sl=sl, col=it: e.tensor_tensor_scan(
                    h_t[sl][:, :], dA_t[sl][:, :], u_t[sl][:, :],
                    carryA[:, col:col + 1], ALU.mult, ALU.add),
                    waits=[tda, tu, h_guard[sl]])
                dA_guard[sl] = tsc2
                if pend is not None:
                    emit_y(pend)
                pend = (s, et, sl, tsc2)
                if et == 7:
                    prev_u = tu
        emit_y(pend)
        gd["prev_u"] = prev_u
        gd["ys_last2"] = [ys_tok[S - 2], ys_tok[S - 1]]
        gd["dA_read"][par] = ("act", b.cnt["act"])
        gd["scan_dve"][par] = ("dve", b.cnt["dve"])
        gd["scan_po"][par] = ("po", b.cnt["po"])
        if c == dbg_chunk:
            dbg("yacc", yacc[:, :, :], ("dve", b.cnt["dve"]))
            dbg("wc", wc_bf[par][:, :, :], gd["wc_patch"][par])
            dbg("w0", w0_bf[par][:, :, :], gd["w0_patch"][par])
            dbg("carry", carryA[:, :], ("act", b.cnt["act"]))

    def ypost_wout(c):
        par = c % 2
        for et in range(8):
            b.op("dve", lambda e, et=et: e.tensor_tensor(
                yp1[:, :], w0_bf[par][:, et, :], qRep[par][:, :], ALU.mult),
                waits=[gd["w0_patch"][par], gd["rope_tok"][par], ld0])
            b.op("dve", lambda e, et=et: e.tensor_tensor(
                yp2[:, :], yacc[:, et, :], yp1[:, :], ALU.subtract))
            b.op("dve", lambda e, et=et: e.scalar_tensor_tensor(
                yp1[:, :], xp_bf[par][:, et, :], dColS[:, et:et + 1], yp2[:, :],
                ALU.mult, ALU.add))
            b.op("dve", lambda e, et=et: e.tensor_tensor(
                Y3[:, et, :], yp1[:, :], sz_bf[par][:, et, :], ALU.mult),
                waits=[gd["sz_tok"].get(c), gd["Y3_free"]])
        typ = b.op("dve", lambda e: e.engine_nop())
        gd["ypost_dve"][par] = typ
        gd["xp_free"][par] = typ
        gd["sz_free"][par] = typ
        gd["wc_free"][par] = typ
        gd["w0_free"][par] = typ
        gd["dtO_free"][par] = typ
        gd["qRep_free"][par] = typ
        gd["sz_tok"].pop(c, None)

        evs = []
        for mo in range(8):
            bankname = "pb2" if mo % 2 == 0 else "pb3"
            bank = pb[2] if mo % 2 == 0 else pb[3]

            def mmf(e, mo=mo, bank=bank):
                for kc in range(8):
                    r = e.matmul(bank[:, 0:CH], w_out_bf[:, kc, mo * 128:(mo + 1) * 128],
                                 Y3[:, kc, :], start=(kc == 0), stop=(kc == 7))
                return r
            mt = b.pe_mm(bankname, [typ, w_cast_done], mmf)
            ev = b.evac(bankname, "act", lambda e, mo=mo, bank=bank: e.copy(
                attnP[:, mo, :], bank[:, 0:CH]), waits=[mt, gd["attnP_free"]])
            evs.append(ev)
        gd["Y3_free"] = ("pe", b.cnt["pe"])
        gd["BrTf_free"][par] = ("pe", b.cnt["pe"])
        if c == dbg_chunk:
            dbg("Y3", Y3[:, :, :], gd["ypost_dve"][par])
            dbg("attnP", attnP[:, :, :], evs[-1])
            dbg("szD", sz_bf[par][:, :, :], gd["ypost_dve"][par])
        tst = b.dma("po", rs_in[(c % 2) * 128:(c % 2) * 128 + 128, :],
                    attnP.ap().rearrange("p a b -> p (a b)"),
                    waits=evs + [gd["rs_next"]], sem="st")
        gd["attnP_free"] = tst

    def rs_xnew(c):
        oc = c // 2
        tcc = b.op("po", lambda e: e.collective_compute(
            "ReduceScatter", ALU.add, replica_groups=PAIRS,
            ins=[rs_in], outs=[rs_out]),
            waits=[("st", b.cnt["st"]), ("rsl", b.cnt["rsl"])], inc=("cc", 1))
        gd["rs_next"] = tcc
        tlb = b.dma("sp", attnS.ap().rearrange("p a b -> p (a b)"), rs_out,
                    waits=[tcc, gd["attnS_free"]], sem="rsl")
        if oc == 0:
            dbg("attnS", attnS[:, :, :], tlb)
        prev_store = None
        for tt in range(NTT):
            evl = []
            for dc in range(8):
                ev = tbf(attnS[:, dc, tt * 128:(tt + 1) * 128],
                         attn_tm[:, dc * 128:(dc + 1) * 128], [tlb])
                if ev is not None:
                    evl.append(ev)
            ev = flush_tbf()
            if ev is not None:
                evl.append(ev)
            sl = 0
            tl = b.dma("sp", xo_t[sl][:, :],
                       x_own[oc * CH + tt * 128: oc * CH + (tt + 1) * 128, :],
                       waits=[gd["xo_slot"][sl]], sem="xB")
            b.op("dve", lambda e: e.tensor_tensor(xn_t[:, :], attn_tm[:, :],
                                                  gaRep[:, :], ALU.mult),
                 waits=evl + [tl, prev_store, setup_act])
            txn = b.op("dve", lambda e, sl=sl: e.tensor_tensor(
                xn_t[:, :], xo_t[sl][:, :], xn_t[:, :], ALU.add))
            if dbg_on and oc == 0 and tt == 0:
                dbg("gaRep", gaRep[:, :], txn)
                dbg("saRep", saRep[:, :], txn)
                dbg("caRep", caRep[:, :], txn)
                t1_ = b.op("dve", lambda e: e.tensor_copy(snapT[:, :], attn_tm[:, :]))
                t2_ = b.op("dve", lambda e, sl=sl: e.tensor_copy(snapX[:, :], xo_t[sl][:, :]))
                dbg("attn_tm0", snapT[:, :], t1_)
                dbg("xo0", snapX[:, :], t2_)
            gd["xo_slot"][sl] = txn
            prev_store = b.dma("po", xnew_d[oc * CH + tt * 128: oc * CH + (tt + 1) * 128, :],
                               xn_t[:, :], waits=[txn], sem="st")
        gd["attnS_free"] = ("pe", b.cnt["pe"])

    if n_chunks > 0:
        front(0)
    for c in range(n_chunks):
        if c + 1 < n_chunks:
            front(c + 1)
            patch(c)
        else:
            patch(c, clast=True)
        if not do_scan:
            continue
        scan(c)
        ypost_wout(c)
        if c % 2 == 1:
            rs_xnew(c)

    mamba_done = {e: (e, b.cnt[e]) for e in ("pe", "act", "dve", "po")}
    mamba_st = ("st", b.cnt["st"])
    es_m.close()
    b.barrier()

    es_f = ExitStack()
    fx_t = [b.sb(f"fx{i}", [128, D], F32, es_f) for i in range(2)]
    fh_tm = b.sb("fh_tm", [128, D], BF16, es_f)
    h2_fm = b.sb("h2_fm", [128, 8, FTOK], BF16, es_f)
    flnc = b.sb("flnc", [128, D], F32, es_f)
    flncol = b.sb("flncol", [128, 4], F32, es_f)
    flnsq = b.sb("flnsq", [128, D], F32, es_f)
    logitsF = b.sb("logitsF", [8, FTOK], F32, es_f)
    lg_t = b.sb("lg_t", [128, 8], F32, es_f)
    top_t = [b.sb(f"top{i}", [128, 8], F32, es_f) for i in range(2)]
    topc = b.sb("topc", [128, 4], F32, es_f)
    combA = b.sb("combA", [128, 8, 8], F32, es_f)
    combT = b.sb("combT", [8, FTOK], BF16, es_f)
    cRep = b.sb("cRep", [128, FTOK], F32, es_f)
    wgStg = [b.sb(f"wgS{i}", [128, 8, 128], F32, es_f) for i in range(2)]
    wuStg = [b.sb(f"wuS{i}", [128, 8, 128], F32, es_f) for i in range(2)]
    wgB = [b.sb(f"wgB{i}", [128, 8, 128], BF16, es_f) for i in range(2)]
    wuB = [b.sb(f"wuB{i}", [128, 8, 128], BF16, es_f) for i in range(2)]
    wdStg = [b.sb(f"wdS{i}", [128, HM, 128], F32, es_f) for i in range(2)]
    wdB = [b.sb("wdB0", [128, HM, 128], BF16, es_f)] * 2
    sg_t = [b.sb(f"sg{i}", [128, 512], BF16, es_f) for i in range(2)]
    hgAll = b.sb("hgAll", [128, HM, FTOK], BF16, es_f)
    acc = b.sb("acc", [128, 8, FTOK], F32, es_f)
    dtmp = [b.sb(f"dtmp{i}", [128, 512], F32, es_f) for i in range(2)]
    acc_tm = b.sb("acc_tm", [128, D], F32, es_f)
    fo_t = b.sb("fo_t", [128, D], F32, es_f)

    fgd = {"fx": [None, None], "h2fm_pe": None, "stg": {}, "acc_free": None,
           "hg_pe": None, "sg_guard": [None, None], "dtmp_guard": [None, None],
           "fsq_prev": None, "fo_store": None, "crep_dve": None, "acc_dve": None}

    def ffn_ln(xtile, out_bf, waits_x):
        b.op("dve", lambda e: e.reduce_sum(flncol[:, 0:1], xtile[:, :], axis=AXX),
             waits=list(waits_x) + [fgd["fsq_prev"]])
        b.op("dve", lambda e: e.tensor_scalar_mul(flncol[:, 1:2], flncol[:, 0:1],
                                                  1.0 / D), ss=True)
        tc = b.op("dve", lambda e: e.tensor_scalar(flnc[:, :], xtile[:, :],
                                                   flncol[:, 1:2], None, ALU.subtract),
                  ss=True)
        tq = b.op("act", lambda e: e.activation(flnsq[:, :], flnc[:, :], AF.Square,
                                                accum_out=flncol[:, 2:3]), waits=[tc])
        fgd["fsq_prev"] = tq
        b.op("act", lambda e: e.activation(flncol[:, 3:4], flncol[:, 2:3], AF.Ln,
                                           scale=1.0 / D, bias=LN_EPS), ss=True)
        tr = b.op("act", lambda e: e.activation(flncol[:, 3:4], flncol[:, 3:4],
                                                AF.Exp, scale=-0.5), ss=True)
        b.op("dve", lambda e: e.tensor_scalar(flnc[:, :], flnc[:, :], flncol[:, 3:4],
                                              None, ALU.mult), waits=[tr])
        b.op("dve", lambda e: e.tensor_tensor(flnsq[:, :], flnc[:, :], cbRep[:, :],
                                              ALU.mult))
        return b.op("dve", lambda e: e.tensor_tensor(out_bf[:, :], flnsq[:, :],
                                                     sbRep[:, :], ALU.add))

    for fb in range(min(2 if do_ffn else 0, int(os.environ.get('MK_NFB', 2)))):
        base_tok = fb * FTOK
        th2 = None
        for tt in range(8):
            sl = tt % 2
            sname = "xnA" if sl == 0 else "xnB"
            extra = [mamba_done["dve"], mamba_done["po"], mamba_done["act"],
                     mamba_done["pe"], mamba_st] if (fb == 0 and tt < 2) else []
            tl = b.dma("sp", fx_t[sl][:, :],
                       xnew_d[base_tok + tt * 128: base_tok + (tt + 1) * 128, :],
                       waits=[fgd["fx"][sl]] + extra, sem=sname)
            tln = ffn_ln(fx_t[sl], fh_tm, [tl, ("pe", b.cnt["pe"])])
            fgd["fx"][sl] = tln
            for dc in range(8):
                tbf(fh_tm[:, dc * 128:(dc + 1) * 128],
                    h2_fm[:, dc, tt * 128:(tt + 1) * 128],
                    [tln, fgd["h2fm_pe"]])
            th2 = flush_tbf()
        th2 = None

        lg_evs = []
        for nt in range(FNT):
            def rmm(e, nt=nt):
                for kc in range(8):
                    r = e.matmul(pb[5][0:8, 0:512], router_bf[:, kc, :],
                                 h2_fm[:, kc, nt * 512:(nt + 1) * 512],
                                 start=(kc == 0), stop=(kc == 7))
                return r
            mt = b.pe_mm("pb5", [tb_state["guard"][0], tb_state["guard"][1]], rmm)
            lg_evs.append(b.evac("pb5", "act", lambda e, nt=nt: e.copy(
                logitsF[0:8, nt * 512:(nt + 1) * 512], pb[5][0:8, 0:512]),
                waits=[mt]))
        comb_evs = []
        for tt in range(8):
            tT = tf32(logitsF[0:8, tt * 128:(tt + 1) * 128], lg_t[:, 0:8],
                      lg_evs + comb_evs, 8, 128, eng="dve")
            b.op("dve", lambda e: e.reduce_max(topc[:, 0:1], lg_t[:, :], axis=AXX),
                 waits=[tT], ss=True)
            b.op("dve", lambda e: e.tensor_scalar(top_t[0][:, :], lg_t[:, :],
                                                  topc[:, 0:1], None, ALU.is_equal),
                 ss=True)
            b.op("dve", lambda e: e.scalar_tensor_tensor(
                top_t[1][:, :], top_t[0][:, :], -1e30, lg_t[:, :], ALU.mult, ALU.add),
                ss=True)
            b.op("dve", lambda e: e.reduce_max(topc[:, 1:2], top_t[1][:, :], axis=AXX),
                 ss=True)
            tdd = b.op("dve", lambda e: e.tensor_tensor(topc[:, 2:3], topc[:, 1:2],
                                                        topc[:, 0:1], ALU.subtract),
                       ss=True)
            tex = b.op("act", lambda e: e.activation(topc[:, 2:3], topc[:, 2:3],
                                                     AF.Exp), waits=[tdd])
            b.op("dve", lambda e: e.tensor_scalar_add(topc[:, 3:4], topc[:, 2:3], 1.0),
                 waits=[tex], ss=True)
            b.op("dve", lambda e: e.reciprocal(topc[:, 3:4], topc[:, 3:4]), ss=True)
            b.op("dve", lambda e: e.tensor_scalar(topc[:, 2:3], topc[:, 3:4], -1.0,
                                                  1.0, ALU.mult, ALU.add), ss=True)
            b.op("dve", lambda e: e.tensor_scalar(top_t[1][:, :], top_t[1][:, :],
                                                  topc[:, 1:2], None, ALU.is_equal),
                 ss=True)
            b.op("dve", lambda e: e.tensor_scalar(top_t[0][:, :], top_t[0][:, :],
                                                  topc[:, 3:4], None, ALU.mult),
                 ss=True)
            tcb = b.op("dve", lambda e, tt=tt: e.scalar_tensor_tensor(
                combA[:, tt, :], top_t[1][:, :], topc[:, 2:3], top_t[0][:, :],
                ALU.mult, ALU.add), ss=True)
            comb_evs.append(tf32(combA[:, tt, :],
                                 combT[0:8, tt * 128:(tt + 1) * 128],
                                 [tcb], 128, 8, eng="dve"))

        for e_i in range(int(os.environ.get('MK_NE', NE)) if ffn_lvl >= 2 else 0):
            crep_evs = []
            for nt in range(FNT):
                mt = b.pe_mm("pb4", comb_evs, lambda e, e_i=e_i, nt=nt: e.matmul(
                    pb[4][:, 0:512], selAll[0:8, e_i, :],
                    combT[:, nt * 512:(nt + 1) * 512], start=True, stop=True))
                crep_evs.append(b.evac("pb4", "act", lambda e, nt=nt: e.copy(
                    cRep[:, nt * 512:(nt + 1) * 512], pb[4][:, 0:512]),
                    waits=[mt, fgd["crep_dve"]]))

            hg_last = None
            for m in range(int(os.environ.get('MK_HM', HM)) if ffn_lvl >= 3 else 0):
                sl = m % 2
                sgn = "fgA" if sl == 0 else "fgB"
                sun = "fuA" if sl == 0 else "fuB"
                tlg = b.dma("sp", wgStg[sl][:, :, :], wg_pm[e_i, m],
                            waits=[fgd["stg"].get(("wg", sl))], sem=sgn)
                tlu = b.dma("sp", wuStg[sl][:, :, :], wu_pm[e_i, m],
                            waits=[fgd["stg"].get(("wu", sl))], sem=sun)
                tcg = b.op("po", lambda e, sl=sl: e.tensor_copy(
                    wgB[sl][:, :, :], wgStg[sl][:, :, :]),
                    waits=[tlg, fgd["stg"].get(("wgB", sl))])
                tcu = b.op("act", lambda e, sl=sl: e.copy(
                    wuB[sl][:, :, :], wuStg[sl][:, :, :]),
                    waits=[tlu, fgd["stg"].get(("wuB", sl))])
                fgd["stg"][("wg", sl)] = tcg
                fgd["stg"][("wu", sl)] = tcu
                if dbg_on and fb == 0 and e_i == 0 and m == 0:
                    dbg("wgB0", wgB[0][:, :, :], tcg)
                    dbg("wgS0", wgStg[0][:, :, :], tcg)
                MK_SUB = int(os.environ.get('MK_SUB', 5))
                for nt in range(FNT if MK_SUB >= 2 else 0):
                    def gmm(e, sl=sl, nt=nt):
                        for kc in range(8):
                            r = e.matmul(pb[0][:, :], wgB[sl][:, kc, :],
                                         h2_fm[:, kc, nt * 512:(nt + 1) * 512],
                                         start=(kc == 0), stop=(kc == 7))
                        return r
                    mtg = b.pe_mm("pb0", [tcg, tb_state["guard"][0], tb_state["guard"][1]], gmm)

                    def umm(e, sl=sl, nt=nt):
                        for kc in range(8):
                            r = e.matmul(pb[1][:, :], wuB[sl][:, kc, :],
                                         h2_fm[:, kc, nt * 512:(nt + 1) * 512],
                                         start=(kc == 0), stop=(kc == 7))
                        return r
                    if MK_SUB < 3:
                        continue
                    mtu = b.pe_mm("pb1", [tcu, tb_state["guard"][0], tb_state["guard"][1]], umm)
                    ssl = (m * FNT + nt) % 2
                    if MK_SUB < 4:
                        continue
                    tsg2 = b.evac("pb0", "act", lambda e, ssl=ssl: e.activation(
                        sg_t[ssl][:, :], pb[0][:, :], AF.Silu),
                        waits=[mtg, fgd["sg_guard"][ssl]])
                    if MK_SUB < 5:
                        continue
                    if dbg_on and fb == 0 and e_i == 0 and m == 0 and nt == 0:
                        ts1_ = b.op("act", lambda e, ssl=ssl: e.copy(
                            dtmp[1][:, :], sg_t[ssl][:, :]), waits=[tsg2])
                        ts2_ = b.op("dve", lambda e: e.tensor_copy(
                            dtmp[0][:, :], pb[1][:, :]), waits=[mtu])
                        dbg("sg0", dtmp[1][:, :], ts1_)
                        dbg("up0", dtmp[0][:, :], ts2_)
                        b.bank["pb1"] = ts2_
                        fgd["dtmp_guard"][0] = ("st", b.cnt["st"])
                        fgd["dtmp_guard"][1] = ("st", b.cnt["st"])
                    thg = b.evac("pb1", "dve",
                                 lambda e, ssl=ssl, m=m, nt=nt: e.tensor_tensor(
                                     hgAll[:, m, nt * 512:(nt + 1) * 512],
                                     sg_t[ssl][:, :], pb[1][:, :], ALU.mult),
                                 waits=[mtu, tsg2, fgd["hg_pe"]])
                    fgd["sg_guard"][ssl] = thg
                    hg_last = thg
                fgd["stg"][("wgB", sl)] = ("pe", b.cnt["pe"])
                fgd["stg"][("wuB", sl)] = ("pe", b.cnt["pe"])
                if m == HM - 1:
                    fgd["stg"][("wgB", 1 - sl)] = ("pe", b.cnt["pe"])
                    fgd["stg"][("wuB", 1 - sl)] = ("pe", b.cnt["pe"])

            for dm in range(8 if ffn_lvl >= 4 else 0):
                sl = dm % 2
                sdn = "fdA" if sl == 0 else "fdB"
                tld = b.dma("sp", wdStg[sl][:, :, :], wd_pm[e_i, dm],
                            waits=[fgd["stg"].get(("wd", sl))], sem=sdn)
                tcd = b.op("dve", lambda e, sl=sl: e.tensor_copy(
                    wdB[sl][:, :, :], wdStg[sl][:, :, :]),
                    waits=[tld, fgd["stg"].get("wdB")])
                fgd["stg"][("wd", sl)] = tcd
                for nt in range(FNT):
                    bankname = "pb2" if (dm * FNT + nt) % 2 == 0 else "pb3"
                    bank = pb[2] if (dm * FNT + nt) % 2 == 0 else pb[3]

                    def dmm(e, sl=sl, nt=nt, bank=bank):
                        for kc in range(HM):
                            r = e.matmul(bank[:, :], wdB[sl][:, kc, :],
                                         hgAll[:, kc, nt * 512:(nt + 1) * 512],
                                         start=(kc == 0), stop=(kc == HM - 1))
                        return r
                    mtd = b.pe_mm(bankname, [tcd, hg_last], dmm)
                    if e_i == 0:
                        b.evac(bankname, "dve",
                               lambda e, dm=dm, nt=nt, bank=bank: e.tensor_tensor(
                                   acc[:, dm, nt * 512:(nt + 1) * 512], bank[:, :],
                                   cRep[:, nt * 512:(nt + 1) * 512], ALU.mult),
                               waits=[mtd, fgd["acc_free"]] + crep_evs)
                    else:
                        tsl = (dm * FNT + nt) % 2
                        ev = b.evac(bankname, "dve",
                                    lambda e, tsl=tsl, nt=nt, bank=bank:
                                    e.tensor_tensor(dtmp[tsl][:, :], bank[:, :],
                                                    cRep[:, nt * 512:(nt + 1) * 512],
                                                    ALU.mult),
                                    waits=[mtd, fgd["dtmp_guard"][tsl]] + crep_evs)
                        fgd["dtmp_guard"][tsl] = b.op(
                            "dve", lambda e, dm=dm, nt=nt, tsl=tsl: e.tensor_tensor(
                                acc[:, dm, nt * 512:(nt + 1) * 512],
                                acc[:, dm, nt * 512:(nt + 1) * 512],
                                dtmp[tsl][:, :], ALU.add), waits=[ev])
                fgd["stg"]["wdB"] = ("pe", b.cnt["pe"])
            fgd["hg_pe"] = ("pe", b.cnt["pe"])
            fgd["crep_dve"] = ("dve", b.cnt["dve"])
        tacc = ("dve", b.cnt["dve"])

        for tt in range(8 if ffn_lvl >= 5 else 0):
            evl = []
            for dc in range(8):
                evl.append(tf32(acc[:, dc, tt * 128:(tt + 1) * 128],
                                acc_tm[:, dc * 128:(dc + 1) * 128],
                                [tacc, fgd["acc_dve"]], 128, 128))
            sl = tt % 2
            sname = "xnA" if sl == 0 else "xnB"
            tl = b.dma("sp", fx_t[sl][:, :],
                       xnew_d[base_tok + tt * 128: base_tok + (tt + 1) * 128, :],
                       waits=[fgd["fx"][sl]], sem=sname)
            b.op("dve", lambda e: e.tensor_tensor(fo_t[:, :], acc_tm[:, :],
                                                  gbRep[:, :], ALU.mult),
                 waits=evl + [tl, fgd["fo_store"]])
            tfo = b.op("dve", lambda e, sl=sl: e.tensor_tensor(
                fo_t[:, :], fx_t[sl][:, :], fo_t[:, :], ALU.add))
            fgd["acc_dve"] = tfo
            fgd["fx"][sl] = tfo
            fgd["fo_store"] = b.dma(
                "po", out_t[base_tok + tt * 128: base_tok + (tt + 1) * 128, :],
                fo_t[:, :], waits=[tfo], sem="st")
        if dbg_on and fb == 0:
            dbg("cRep", cRep[:, :], ("act", b.cnt["act"]))
            dbg("logitsF", logitsF[:, :], ("act", b.cnt["act"]))
            dbg("combT", combT[:, :], ("dve", b.cnt["dve"]))
            dbg("acc", acc[:, :, :], ("dve", b.cnt["dve"]))
            dbg("hgAll", hgAll[:, :, :], ("dve", b.cnt["dve"]))
            dbg("h2fm", h2_fm[:, :, :], ("dve", b.cnt["dve"]))
        fgd["acc_free"] = ("pe", b.cnt["pe"])
        fgd["h2fm_pe"] = ("pe", b.cnt["pe"])

    b.op("po", lambda e: e.engine_nop(), waits=[("st", b.cnt["st"])])
    es_f.close()

    blk = b.es.enter_context(nc.Block())

    def runner(qn):
        def body(eng):
            for cl in b.q[qn]:
                cl(eng)
        return body

    blk.sync(runner("sp"))
    blk.tensor(runner("pe"))
    blk.scalar(runner("act"))
    blk.vector(runner("dve"))
    blk.gpsimd(runner("po"))
    b.es.close()
    return b


_prog = {}


def get_prog():
    if "b" not in _prog:
        _prog["b"] = build_program()
    return _prog["b"]


def kernel(L0_attn_x, L0_attn_W_in, L0_attn_W_x, L0_attn_W_dt, L0_attn_b_dt,
           L0_attn_A_log, L0_attn_B_bias, L0_attn_C_bias, L0_attn_B_norm_w,
           L0_attn_C_norm_w, L0_attn_rope_freqs, L0_attn_D, L0_attn_W_out,
           L0_ffn_router_w, L0_ffn_wg, L0_ffn_wu, L0_ffn_wd, L0_c, L0_wmod):
    f32 = np.float32
    x = np.ascontiguousarray(np.asarray(L0_attn_x, f32))
    W_in = np.asarray(L0_attn_W_in, f32)
    W_x = np.asarray(L0_attn_W_x, f32)
    W_dt = np.asarray(L0_attn_W_dt, f32).reshape(-1)
    b_dt = np.asarray(L0_attn_b_dt, f32).reshape(-1)
    A_log = np.asarray(L0_attn_A_log, f32)
    Dv = np.asarray(L0_attn_D, f32).reshape(-1)
    W_out = np.asarray(L0_attn_W_out, f32)
    router = np.asarray(L0_ffn_router_w, f32)
    wg = np.asarray(L0_ffn_wg, f32)
    wu = np.asarray(L0_ffn_wu, f32)
    wd = np.asarray(L0_ffn_wd, f32)
    cmat = np.asarray(L0_c, f32)
    wmod = np.asarray(L0_wmod, f32)

    def pm_kc(w):
        K, M = w.shape
        return np.ascontiguousarray(w.reshape(K // 128, 128, M).transpose(1, 0, 2))

    sperm = np.concatenate([np.arange(0, 16, 2), np.arange(1, 16, 2)])
    W_x = np.concatenate([W_x[sperm], W_x[16 + sperm], W_x[32:34]], 0)
    A_log = A_log[:, sperm]
    smalls_perm = sperm

    H = 2752
    wg_p = np.zeros((NE, 1024, HP), f32)
    wg_p[:, :, :H] = wg
    wu_p = np.zeros((NE, 1024, HP), f32)
    wu_p[:, :, :H] = wu
    wd_p = np.zeros((NE, HP, 1024), f32)
    wd_p[:, :H, :] = wd
    wg_pm = np.ascontiguousarray(
        wg_p.reshape(NE, 8, 128, HM, 128).transpose(0, 3, 2, 1, 4))
    wu_pm = np.ascontiguousarray(
        wu_p.reshape(NE, 8, 128, HM, 128).transpose(0, 3, 2, 1, 4))
    wd_pm = np.ascontiguousarray(
        wd_p.reshape(NE, HM, 128, 8, 128).transpose(0, 3, 2, 1, 4))
    router_pm = pm_kc(router)
    wmod_pm = pm_kc(wmod)
    smalls = np.concatenate([np.asarray(L0_attn_B_bias, f32)[sperm],
                             np.asarray(L0_attn_C_bias, f32)[sperm],
                             np.asarray(L0_attn_B_norm_w, f32)[sperm],
                             np.asarray(L0_attn_C_norm_w, f32)[sperm]]).reshape(1, 64)
    rfreq = np.asarray(L0_attn_rope_freqs, f32).reshape(1, 8)

    in_maps = []
    for core in range(8):
        bb, j = core // 2, core % 2
        own = slice(j * 1024, (j + 1) * 1024)
        W_sel = np.concatenate(
            [W_in[own], W_in[2048 + j * 1024: 2048 + (j + 1) * 1024]], 0)
        ordv = np.concatenate([np.arange(j * 1024, (j + 1) * 1024),
                               np.arange((1 - j) * 1024, (2 - j) * 1024)])
        in_maps.append({
            "xb": np.ascontiguousarray(x[bb]),
            "x_own": np.ascontiguousarray(
                x[bb].reshape(NCH, CH, D)[j::2].reshape(OWN_TOK, D)),
            "w_in_pm": pm_kc(np.ascontiguousarray(W_sel.T)),
            "w_out_pm": pm_kc(np.ascontiguousarray(W_out[:, own].T)),
            "w_x_pm": pm_kc(np.ascontiguousarray(W_x[:, own].T)),
            "wdt_pm": np.ascontiguousarray(W_dt[ordv].reshape(16, 128).T),
            "bdt_pm": np.ascontiguousarray(b_dt[ordv].reshape(16, 128).T),
            "a_pm": np.ascontiguousarray(
                A_log[own].reshape(8, 128, 16).transpose(1, 0, 2).reshape(128, 128)),
            "d_pm": np.ascontiguousarray(Dv[own].reshape(8, 128).T),
            "smalls": smalls, "rfreq": rfreq,
            "c_pm": np.ascontiguousarray(cmat[bb].reshape(8, 128).T),
            "wmod_pm": wmod_pm, "router_pm": router_pm,
            "wg_pm": wg_pm, "wu_pm": wu_pm, "wd_pm": wd_pm,
        })

    bobj = get_prog()
    res = run_bass_kernel_spmd(bobj.nc, in_maps, core_ids=list(range(8)))

    out = np.empty((4, T, D), np.float32)
    for core in range(8):
        bb, j = core // 2, core % 2
        o = np.asarray(res.results[core]["out"]).reshape(OWN_TOK, D)
        out[bb].reshape(NCH, CH, D)[j::2] = o.reshape(NCH // 2, CH, D)
    return out
